# version 28
# baseline (speedup 1.0000x reference)
import numpy as np
import concourse.bass as bass
import concourse.mybir as mybir
from concourse.bass_utils import run_bass_kernel_spmd
from concourse.bass_core_rust import IndirectOffsetOnAxis

F32 = mybir.dt.float32
I32 = mybir.dt.int32
BIGP = 100000.0
AF = mybir.ActivationFunctionType
ALU = mybir.AluOpType
AX = mybir.AxisListType

D = 1024
H = 8
DK = 128
NCORE = 8
CT = 128
EPS = 1e-5
CLAMP = 85.0
NEG = -1.0


class _Op:
    __slots__ = ("eng", "fn", "deps", "need", "sem", "val", "inc", "guard",
                 "chain")

    def __init__(self, eng, fn, sem, inc):
        self.eng, self.fn, self.sem, self.inc = eng, fn, sem, inc
        self.deps = []
        self.need = False
        self.val = None
        self.guard = None
        self.chain = None


class Tracker:
    ENGS = ("sync", "tensor", "vector", "scalar", "gpsimd")

    def __init__(self):
        self.ops = []
        self.last_w = {}
        self.readers = {}
        self.forced = {}
        self.sem_last = {}
        self.dma_last = {}
        self.ncc = 0
        self.guard_src = None
        self.guard_dep = None
        self.prologue = {}

    def add(self, eng, fn, reads=(), writes=(), dma=False, cc=False, tag=None,
            guard=None):
        if cc:
            sem, inc = f"cc{self.ncc}", 1
            self.ncc += 1
        elif dma:
            sem, inc = "dq_" + (tag or "misc"), 16
        else:
            sem, inc = "c_" + eng, 1
        op = _Op(eng, fn, sem, inc)
        deps = set()
        for k in reads:
            w = self.last_w.get(k)
            if w is not None:
                deps.add(w)
        for k in writes:
            w = self.last_w.get(k)
            if w is not None:
                deps.add(w)
            for r in self.readers.get(k, ()):
                deps.add(r)
        fb = self.forced.get(eng)
        if fb is not None:
            deps.add(fb)
        if dma:
            prev = self.dma_last.get(sem)
            if prev is not None:
                deps.add(prev)
                op.chain = prev
            self.dma_last[sem] = op
        deps.discard(op)
        if eng == "tensor":
            deps = {d for d in deps if d.eng != "tensor"}
        op.deps = list(deps)
        op.guard = guard
        for d in op.deps:
            d.need = True
        for k in reads:
            self.readers.setdefault(k, []).append(op)
        for k in writes:
            self.last_w[k] = op
            self.readers[k] = [op]
        self.ops.append(op)
        self.sem_last[sem] = op
        return op

    def barrier(self):
        snap = list(self.sem_last.values())
        for eng in self.ENGS:
            op = _Op(eng, lambda e: e.nop(), "c_" + eng, 1)
            op.deps = list(snap)
            for d in op.deps:
                d.need = True
            self.ops.append(op)
            self.forced[eng] = op
            self.sem_last[op.sem] = op

    def emit(self, block, handles):
        for op in self.ops:
            if op.inc == 16 or op.sem == "cc":
                op.need = True
        counts = {}
        for op in self.ops:
            if op.need:
                counts[op.sem] = counts.get(op.sem, 0) + op.inc
                op.val = counts[op.sem]
        per_eng = {e: [] for e in self.ENGS}
        for op in self.ops:
            per_eng[op.eng].append(op)

        gsrc, gdep = self.guard_src, self.guard_dep
        waited = {}
        for o in self.ops:
            bysem = {}
            for d in o.deps:
                if d.val is not None:
                    if bysem.get(d.sem, 0) < d.val:
                        bysem[d.sem] = d.val
            for sm, v in bysem.items():
                waited.setdefault(sm, set()).add(v)
        if gdep is not None and gdep.val is not None:
            waited.setdefault(gdep.sem, set()).add(gdep.val)

        def mk(ops, ename):
            def body(eng):
                pro = self.prologue.get(ename)
                if pro is not None:
                    pro(eng)
                hwm = {}
                state = {"g": None, "cm": None, "pend": None, "reg": None,
                         "ge": None}

                def close():
                    if state["cm"] is None:
                        return
                    finals = {}
                    starts = {}
                    kinds = {}
                    for o2 in state["pend"]:
                        if o2.sem not in starts:
                            starts[o2.sem] = o2.val - o2.inc
                            kinds[o2.sem] = (o2.eng == "gpsimd"
                                             and o2.inc == 16)
                            if o2.chain is not None \
                                    and o2.chain.val is not None \
                                    and o2.chain not in state["pend"]:
                                starts[o2.sem] = (o2.val - o2.inc, o2.chain)
                        finals[o2.sem] = o2.val
                    for sm, v in finals.items():
                        eng.wait_ge(handles[sm], v)
                    state["cm"].__exit__(None, None, None)
                    el = eng.Else()
                    el.__enter__()
                    for sm, st0 in starts.items():
                        if isinstance(st0, tuple):
                            st0, ch = st0
                            eng.wait_ge(handles[ch.sem], ch.val)
                        fin = finals[sm]
                        steps = sorted(v for v in waited.get(sm, ())
                                       if st0 < v < fin)
                        prev = st0
                        swdge = kinds[sm]
                        for v in steps + [fin]:
                            if v > prev:
                                if swdge:
                                    eng.inc_swdge_sem([handles[sm]],
                                                      [v - prev])
                                else:
                                    eng.sem_inc(handles[sm], v - prev)
                                prev = v
                    for sm, v in finals.items():
                        eng.wait_ge(handles[sm], v)
                    el.__exit__(None, None, None)
                    state["cm"] = None
                    state["g"] = None
                    state["pend"] = None

                for o in ops:
                    if o.guard != state["g"]:
                        close()
                    if o.guard is not None and state["cm"] is None:
                        if (gdep is not None and gdep.val is not None
                                and hwm.get(gdep.sem, 0) < gdep.val):
                            eng.wait_ge(handles[gdep.sem], gdep.val)
                            hwm[gdep.sem] = gdep.val
                        if state["reg"] is None:
                            state["reg"] = eng.alloc_register(name="nfguard")
                        e_i, thresh = o.guard
                        if state["ge"] != e_i:
                            eng.reg_load(state["reg"], gsrc(e_i))
                            state["ge"] = e_i
                        state["cm"] = eng.If_cmp(state["reg"], thresh, "IS_GT")
                        state["cm"].__enter__()
                        state["g"] = o.guard
                        state["pend"] = []
                    bysem = {}
                    for d in o.deps:
                        if d.val is None:
                            continue
                        if bysem.get(d.sem, 0) < d.val:
                            bysem[d.sem] = d.val
                    for sm, v in bysem.items():
                        if hwm.get(sm, 0) >= v:
                            continue
                        eng.wait_ge(handles[sm], v)
                        if state["cm"] is None:
                            hwm[sm] = v
                    ins = o.fn(eng)
                    if o.need:
                        if ins is None:
                            ins = eng.nop()
                        ins.then_inc(handles[o.sem], o.inc)
                        if state["cm"] is not None:
                            state["pend"].append(o)
                close()
            return body

        for e in self.ENGS:
            if per_eng[e]:
                getattr(block, e)(mk(per_eng[e], e))
        return counts


class Arena:
    def __init__(self, t, total):
        self.t, self.total = t, total
        self.off = 0
        self.marks = []

    def alloc(self, n):
        a = self.off
        self.off += n
        assert self.off <= self.total, f"arena overflow {self.off} > {self.total}"
        return self.t[:, a:a + n]

    def mark(self):
        self.marks.append(self.off)

    def release(self):
        self.off = self.marks.pop()


def build_program(TPC):
    TILE = 512
    SPT = TILE // CT
    NT = TPC // TILE
    assert TPC % TILE == 0

    nc = bass.Bass("TRN2", target_bir_lowering=False)

    def din(name, shape):
        return nc.dram_tensor(name, list(shape), F32, kind="ExternalInput").ap()

    def dint(name, shape):
        return nc.dram_tensor(name, list(shape), F32).ap()

    xin = din("xin", [D, TPC])
    wq = din("wq", [D, D]); wk = din("wk", [D, D])
    wv = din("wv", [D, D]); wr = din("wr", [D, D])
    wa_d = din("wa", [D, H])
    wo_d = din("wo", [D, D])
    a_log = din("a_log", [H]); dt_b = din("dt_b", [H])
    qnw_d = din("qnw", [DK]); knw_d = din("knw", [DK]); gnw_d = din("gnw", [D])
    gate_wt = din("gate_wt", [D, H])
    lb_b = din("lb_b", [H])
    egwt = din("egwt", [8, D, 512]); euwt = din("euwt", [8, D, 512])
    edwt = din("edwt", [8, 512, D])
    sgwt = din("sgwt", [D, D]); suwt = din("suwt", [D, D]); sdwt = din("sdwt", [D, D])
    c_in = din("c_in", [1, D])
    wmod = din("wmod", [D, 6 * D])
    ident_d = din("ident", [128, 128])
    tri_d = din("tri", [128, 128])
    tri4_d = din("tri4", [128, 512])
    colsel_d = din("colsel", [128, 2])
    ones_d = din("onesr", [1, 512])
    negr_d = din("negr", [1, 512])
    pmask_d = din("pmask", [NCORE])
    t32_d = din("t32", [32, 32])
    iotac_d = din("iotac", [128, 1])

    NSUB = TPC // CT
    NBLK = TPC // 512
    CAPE = TPC + 512
    DW = D + 1
    o_sp = dint("o_sp", [H, DK, TPC])
    qc_sp = dint("qc_sp", [H, DK, TPC])
    x1_sp = dint("x1_sp", [8, 128, TPC])
    h2_sp = dint("h2_sp", [8, 128, TPC])
    h2tok_d = dint("h2tok", [TPC, DW])
    compact_d = [dint(f"compact{i}", [CAPE, DW]) for i in range(8)]
    acc_tok_d = dint("acc_tok", [TPC, D])
    rowx_d = dint("rowx", [8 * NSUB])
    rowx2_d = dint("rowx2", [8 * NSUB])
    modb_d = dint("modb", [6 * D])
    ag_in = dint("ag_in", [H * DK * DK + H * DK])
    ag_out = dint("ag_out", [NCORE * (H * DK * DK + H * DK)])
    ar_in = dint("ar_in", [16])
    ar_out = dint("ar_out", [16])
    wm_d = dint("wm_d", [8])
    yout = nc.dram_tensor("yout", [TPC, D], F32, kind="ExternalOutput").ap()

    tk = Tracker()
    from contextlib import ExitStack

    with ExitStack() as ctx:
        ARENA_F = 52850
        arena_t = ctx.enter_context(nc.sbuf_tensor("arena", [128, ARENA_F], F32))
        ar = Arena(arena_t, ARENA_F)
        A = ar.alloc
        idxi = ctx.enter_context(nc.sbuf_tensor("idxi", [128, 8 * NSUB], I32))
        idxt = ctx.enter_context(nc.sbuf_tensor("idxt", [128, 32], I32))
        nfi8 = ctx.enter_context(nc.sbuf_tensor("nfi8", [1, 8], I32))
        invi = ctx.enter_context(nc.sbuf_tensor("invi", [128, 8], I32))
        bigi = ctx.enter_context(nc.sbuf_tensor("bigi", [128, 4], I32))

        ident = A(128); tri = A(128); tri4 = A(512); colsel = A(2)
        onesr = A(512); negr = A(512)
        modc = A(48)
        qknw = A(2); gnwc = A(8)
        naec = A(8)
        rowsc = A(24)
        pmb = A(8)
        S_loc = A(H * DK); S_in = A(H * DK)
        cumD = A(H); cumc = A(H)
        wm2 = A(2)
        modrow = A(6 * D)
        wa_s = A(8 * H)
        gw_s = A(8 * H)
        prow = A(3 * 512)
        g16 = A(1)
        t32 = A(32)
        iotac = A(1)
        selcols = A(8 * NSUB)
        wmcol = A(8)

        cvals = (EPS, 1.0, -0.5, -1.0)
        cbuf = A(len(cvals))
        pb = [ctx.enter_context(nc.psum_tensor(f"pb{i}", [128, 512], F32))
              for i in range(8)]
        KPB = [f"pb{i}" for i in range(8)]

        def dma(eng, out, in_, rk, wk_, slow=False):
            tk.add(eng, lambda e, o=out, i=in_, s=slow: e.dma_start(
                out=o, in_=i, allow_slow_non_contiguous=s),
                   reads=rk, writes=wk_, dma=True, tag=wk_[0])

        def dma_acc(out, in_, rk, wk_):
            tk.add("gpsimd", lambda e, o=out, i=in_: e.dma_start(
                out=o, in_=i, accum_op=ALU.add), reads=rk, writes=wk_, dma=True,
                tag=wk_[0])

        def op(eng, fn, rk, wk_):
            tk.add(eng, fn, reads=rk, writes=wk_)

        def modrow_sl(w, c):
            return modrow[0:1, w * D + c * 128: w * D + (c + 1) * 128]

        def modcol(w, c):
            return modc[:, w * 8 + c:w * 8 + c + 1]

        for ci, cv in enumerate(cvals):
            op("vector", lambda e, ci=ci, cv=cv: e.memset(cbuf[:, ci:ci + 1], cv),
               [], ["cbuf"])
            nc.const_aps.register(F32, cv, cbuf[:, ci:ci + 1])
        dma("sync", ident, ident_d, ["i_d"], ["ident"])
        dma("sync", tri, tri_d, ["t_d"], ["tri"])
        dma("sync", tri4, tri4_d, ["t4_d"], ["tri4"])
        dma("sync", colsel, colsel_d, ["cs_d"], ["colsel"])
        dma("sync", onesr[0:1, :], ones_d, ["o_d"], ["onesr"])
        dma("sync", negr[0:1, :], negr_d, ["n_d"], ["negr"])
        dma("sync", qknw[:, 0:1], qnw_d[:, None], ["qn_d"], ["qknw"])
        dma("sync", qknw[:, 1:2], knw_d[:, None], ["kn_d"], ["qknw"])
        dma("sync", gnwc[:, :, None],
            gnw_d.rearrange("(c p) -> p c", p=128)[:, :, None], ["gn_d"], ["gnwc"],
            slow=True)
        dma("sync", pmb, pmask_d.partition_broadcast(128), ["pm_d"], ["pmb"])
        dma("sync", t32[0:32, :], t32_d, ["t32_d"], ["t32"])
        dma("sync", iotac, iotac_d, ["io_d"], ["iotac"])
        dma("sync", rowsc[0:1, 0:8], a_log[None, :], ["al_d"], ["rowsc"])
        dma("sync", rowsc[0:1, 8:16], dt_b[None, :], ["dt_d"], ["rowsc"])
        dma("sync", rowsc[0:1, 16:24], lb_b[None, :], ["lb_d"], ["rowsc"])
        op("scalar", lambda e: e.activation(rowsc[0:1, 0:8], rowsc[0:1, 0:8], AF.Exp),
           ["rowsc"], ["rowsc"])
        op("vector", lambda e: e.tensor_scalar_mul(rowsc[0:1, 0:8],
                                                   rowsc[0:1, 0:8], -1.0),
           ["rowsc"], ["rowsc"])
        op("tensor", lambda e: e.matmul(pb[0][:, 0:8], onesr[0:1, 0:128],
                                        rowsc[0:1, 0:8], start=True, stop=True),
           ["onesr", "rowsc"], [KPB[0]])
        op("vector", lambda e: e.tensor_copy(naec, pb[0][:, 0:8]), [KPB[0]], ["naec"])
        op("vector", lambda e: e.memset(S_loc, 0.0), [], ["S_loc"])
        op("vector", lambda e: e.memset(S_in, 0.0), [], ["S_in"])
        op("vector", lambda e: e.memset(cumD, 1.0), [], ["cumD"])
        dma("sync", wa_s.rearrange("p (c hh) -> p c hh", hh=H),
            wa_d.rearrange("(c p) hh -> p c hh", p=128), ["wa_d"], ["wa_s"])
        dma("sync", gw_s.rearrange("p (c hh) -> p c hh", hh=H),
            gate_wt.rearrange("(c p) hh -> p c hh", p=128), ["gwt_d"], ["gw_s"])

        ar.mark()
        csil = A(8)
        wstr = [A(8 * 512) for _ in range(2)]
        dma("sync", csil[:, :, None],
            c_in.rearrange("one (c p) -> p c", p=128)[:, :, None], ["c_d"], ["csil"],
            slow=True)
        op("scalar", lambda e: e.activation(csil, csil, AF.Silu), ["csil"], ["csil"])
        for og in range(12):
            par = og % 2
            wb = wstr[par]
            wkk = f"wstr{par}"
            dma("sync", wb.rearrange("p (c t) -> p c t", t=512),
                wmod[:, og * 512:(og + 1) * 512].rearrange("(c p) t -> p c t", p=128),
                [], [wkk])
            bank = pb[4 + par]
            for c in range(8):
                op("tensor", lambda e, c=c, wb=wb, bank=bank, f=(c == 0), l=(c == 7):
                   e.matmul(bank[0:1, 0:512], csil[:, c:c + 1],
                            wb.rearrange("p (c t) -> p c t", t=512)[:, c],
                            start=f, stop=l), ["csil", wkk], [KPB[4 + par]])
            op("vector", lambda e, og=og, bank=bank: e.tensor_copy(
                modrow[0:1, og * 512:(og + 1) * 512], bank[0:1, 0:512]),
               [KPB[4 + par]], ["modrow"])
        for w in (1, 4):
            sl = modrow[0:1, w * D:(w + 1) * D]
            op("vector", lambda e, sl=sl: e.tensor_scalar_add(sl, sl, 1.0),
               ["modrow"], ["modrow"])
        sgn = A(D)
        for w in (2, 5):
            sl = modrow[0:1, w * D:(w + 1) * D]
            m_ = sgn[0:1, 0:D]
            op("vector", lambda e, sl=sl, m_=m_: e.tensor_scalar(
                m_, sl, 0.0, None, op0=ALU.is_ge), ["modrow"], ["sgn"])
            op("vector", lambda e, m_=m_: e.tensor_scalar(
                m_, m_, 2.0, -1.0, op0=ALU.mult, op1=ALU.add), ["sgn"], ["sgn"])
            op("scalar", lambda e, sl=sl: e.activation(sl, sl, AF.Abs),
               ["modrow"], ["modrow"])
            op("vector", lambda e, sl=sl: e.tensor_scalar_max(sl, sl, 0.25),
               ["modrow"], ["modrow"])
            op("vector", lambda e, sl=sl, m_=m_: e.tensor_mul(sl, sl, m_),
               ["modrow", "sgn"], ["modrow"])
        dma("scalar", modb_d[None, :], modrow[0:1, :], ["modrow"], ["modb"])
        dma("sync", modc.rearrange("p (w c) -> p w c", c=8),
            modb_d.rearrange("(w c p) -> p w c", p=128, c=8), ["modb"], ["modc"],
            slow=True)
        ar.release()

        tk.barrier()

        ar.mark()
        xt = [A(8 * CT) for _ in range(2)]
        ht = A(8 * TILE)
        wst = [A(512) for _ in range(2)]
        rbuf = A(D)
        q_tok = A(SPT * D)
        k_tok = A(SPT * D)
        v_tok = A(SPT * D)
        r_tok = A(SPT * D)
        f_t = [A(D) for _ in range(SPT)]
        lam = A(D)
        sqsc = A(D)
        sqsc2 = A(2 * CT)
        acol = A(SPT * H)
        fhead = A(SPT * H)
        rqk = A(SPT * 2 * H)
        mx8 = A(8)
        ztop = A(128); wsel = A(128)
        wsum = A(H)
        ppt_s = A(D)
        eqs_s = A(D)
        qfm_s = A(D)
        kfm_s = A(D)
        ams_s = A(D)
        stv_s = A(D)
        dex = A(24)
        gkn = A(8); qcf = A(8)
        dstage = A(16)
        o_sub = A(H * CT)
        qc_sub = A(H * CT)

        WLIST = ((wr, "r", None, None, AF.Sigmoid),
                 (wq, "q", q_tok, "q_tok", AF.Silu),
                 (wk, "k", k_tok, "k_tok", AF.Silu),
                 (wv, "v", v_tok, "v_tok", AF.Silu))

        hvw = ht.rearrange("p (c tt) -> p c tt", tt=TILE)

        def ln1(t2, s):
            tok2 = t2 * TILE
            xpar = (t2 * SPT + s) % 2
            xb = xt[xpar]
            xk = f"xt{xpar}"
            dma("sync", xb.rearrange("p (c t2) -> p c t2", t2=CT),
                xin[:, tok2 + s * CT: tok2 + (s + 1) * CT]
                .rearrange("(c p) t2 -> p c t2", p=128), [], [xk])
            xv = xb.rearrange("p (c t2) -> p c t2", t2=CT)
            for c in range(8):
                op("tensor", lambda e, c=c, xv=xv, f=(c == 0), l=(c == 7):
                   e.matmul(pb[2][0:1, 0:CT], colsel[:, 0:1], xv[:, c],
                            start=f, stop=l), [xk, "colsel"], [KPB[2]])
            for c in range(8):
                cp2 = c % 2
                op("scalar", lambda e, c=c, xv=xv, cp2=cp2: e.activation(
                    sqsc2[:, cp2 * CT:(cp2 + 1) * CT], xv[:, c], AF.Square),
                   [xk], [f"sq2{cp2}"])
                op("tensor", lambda e, c=c, cp2=cp2, f=(c == 0), l=(c == 7):
                   e.matmul(pb[3][0:1, 0:CT], colsel[:, 0:1],
                            sqsc2[:, cp2 * CT:(cp2 + 1) * CT], start=f, stop=l),
                   [f"sq2{cp2}", "colsel"], [KPB[3]])
            mrow = prow[0:1, 0:CT]
            rrow = prow[0:1, 512:512 + CT]
            mrr = prow[0:1, 1024:1024 + CT]
            op("vector", lambda e, mrow=mrow: e.tensor_scalar_mul(
                mrow, pb[2][0:1, 0:CT], 1.0 / D), [KPB[2]], ["prow"])
            op("vector", lambda e, rrow=rrow, mrow=mrow: e.tensor_mul(
                rrow, mrow, mrow), ["prow"], ["prow"])
            op("vector", lambda e, rrow=rrow: e.scalar_tensor_tensor(
                rrow, pb[3][0:1, 0:CT], 1.0 / D, rrow,
                op0=ALU.mult, op1=ALU.subtract), [KPB[3], "prow"], ["prow"])
            op("scalar", lambda e, rrow=rrow: e.activation(
                rrow, rrow, AF.Ln, bias=EPS), ["prow"], ["prow"])
            op("scalar", lambda e, rrow=rrow: e.activation(
                rrow, rrow, AF.Exp, scale=-0.5), ["prow"], ["prow"])
            op("vector", lambda e, mrr=mrr, mrow=mrow, rrow=rrow: e.tensor_mul(
                mrr, mrow, rrow), ["prow"], ["prow"])
            for c in range(8):
                op("tensor", lambda e, c=c, rrow=rrow: e.matmul(
                    pb[2][:, 0:CT], modrow_sl(1, c), rrow, start=True, stop=True),
                   ["modrow", "prow"], [KPB[2]])
                op("tensor", lambda e, c=c, mrr=mrr: e.matmul(
                    pb[3][:, 0:CT], modrow_sl(1, c), mrr, start=True, stop=False),
                   ["modrow", "prow"], [KPB[3]])
                op("tensor", lambda e, c=c: e.matmul(
                    pb[3][:, 0:CT], modrow_sl(0, c), negr[0:1, 0:CT],
                    start=False, stop=True), ["modrow", "negr"], [KPB[3]])
                hc = hvw[:, c, s * CT:(s + 1) * CT]
                op("vector", lambda e, hc=hc, xv=xv, c=c: e.tensor_mul(
                    hc, xv[:, c], pb[2][:, 0:CT]), [xk, KPB[2]], ["ht"])
                op("vector", lambda e, hc=hc: e.tensor_sub(
                    hc, hc, pb[3][:, 0:CT]), ["ht", KPB[3]], ["ht"])

        for s in range(SPT):
            ln1(0, s)

        for t in range(NT):
            tok0 = t * TILE
            rbufk = "r_tok"

            def proj_one(wd, dstt, dkey, func):
                for half in range(2):
                    for c in range(8):
                        par = c % 2
                        wb = wst[par]
                        wkk = f"wst{par}"
                        dma("sync", wb,
                            wd[c * 128:(c + 1) * 128, half * 512:(half + 1) * 512],
                            [], [wkk])
                        for s in range(SPT):
                            op("tensor",
                               lambda e, c=c, s=s, wb=wb, f=(c == 0), l=(c == 7):
                               e.matmul(pb[4 + s][:, 0:512],
                                        hvw[:, c, s * CT:(s + 1) * CT], wb,
                                        start=f, stop=l),
                               ["ht", wkk], [KPB[4 + s]])
                    for s in range(SPT):
                        dv = dstt.rearrange("p (s f) -> p s f", f=D)[
                            :, s, half * 512:(half + 1) * 512]
                        op("scalar", lambda e, dv=dv, s=s, func=func:
                           e.activation(dv, pb[4 + s][:, 0:512], func),
                           [KPB[4 + s]], [dkey])

            proj_one(wr, r_tok, rbufk, AF.Sigmoid)

            for s in range(SPT):
                for c in range(8):
                    op("tensor", lambda e, c=c, s=s, f=(c == 0):
                       e.matmul(pb[2][:, 0:H], hvw[:, c, s * CT:(s + 1) * CT],
                                wa_s.rearrange("p (c hh) -> p c hh", hh=H)[:, c],
                                start=f, stop=False), ["ht", "wa_s"], [KPB[2]])
                op("tensor", lambda e: e.matmul(
                    pb[2][:, 0:H], onesr[0:1, 0:128], rowsc[0:1, 8:16],
                    start=False, stop=True), ["onesr", "rowsc"], [KPB[2]])
                op("vector", lambda e, s=s: e.tensor_copy(
                    acol[:, s * H:(s + 1) * H], pb[2][:, 0:H]), [KPB[2]], ["acol"])

            nsh = SPT * H
            op("scalar", lambda e: e.activation(fhead[:, 0:nsh], acol, AF.Abs),
               ["acol"], ["fhead"])
            op("scalar", lambda e: e.activation(fhead[:, 0:nsh], fhead[:, 0:nsh],
                                                AF.Exp, scale=-1.0),
               ["fhead"], ["fhead"])
            op("scalar", lambda e: e.activation(fhead[:, 0:nsh], fhead[:, 0:nsh],
                                                AF.Ln, bias=1.0), ["fhead"], ["fhead"])
            op("vector", lambda e: e.tensor_scalar_max(sqsc[:, 0:nsh], acol, 0.0),
               ["acol"], ["sqsc"])
            op("vector", lambda e: e.tensor_add(fhead[:, 0:nsh], fhead[:, 0:nsh],
                                                sqsc[:, 0:nsh]),
               ["fhead", "sqsc"], ["fhead"])
            op("vector", lambda e: e.tensor_tensor(
                fhead[:, 0:nsh].rearrange("p (s hh) -> p s hh", hh=H),
                fhead[:, 0:nsh].rearrange("p (s hh) -> p s hh", hh=H),
                naec.to_broadcast([128, H, SPT]).rearrange("p hh s -> p s hh"),
                ALU.mult), ["fhead", "naec"], ["fhead"])

            for s in range(SPT):
                fb = f_t[s]
                fbk = f"f_t{s}"
                for hh in range(H):
                    sc_sl = r_tok.rearrange("p (s f) -> p s f", f=D)[
                        :, s, hh * DK:(hh + 1) * DK]
                    op("vector", lambda e, sc_sl=sc_sl: e.tensor_copy(ztop, sc_sl),
                       [rbufk], ["ztop"])
                    for _ in range(4):
                        op("vector", lambda e: e.max(mx8, ztop), ["ztop"], ["mx8"])
                        op("vector", lambda e: e.match_replace(ztop, mx8, ztop, NEG),
                           ["ztop", "mx8"], ["ztop"])
                    op("vector", lambda e, sc_sl=sc_sl: e.tensor_tensor(
                        wsel, sc_sl, ztop, ALU.is_gt), [rbufk, "ztop"], ["wsel"])
                    op("vector", lambda e, sc_sl=sc_sl: e.tensor_mul(
                        wsel, wsel, sc_sl), ["wsel", rbufk], ["wsel"])
                    wcol = wsum[:, hh:hh + 1]
                    op("vector", lambda e, wcol=wcol: e.reduce_sum(
                        wcol, wsel, axis=AX.X), ["wsel"], ["wsum"])
                    op("vector", lambda e, wcol=wcol: e.tensor_scalar_add(
                        wcol, wcol, 1e-9), ["wsum"], ["wsum"])
                    op("vector", lambda e, wcol=wcol: e.reciprocal(wcol, wcol),
                       ["wsum"], ["wsum"])
                    fsl = fb[:, hh * DK:(hh + 1) * DK]
                    op("vector", lambda e, fsl=fsl, wcol=wcol:
                       e.tensor_scalar(fsl, wsel, wcol, None, op0=ALU.mult),
                       ["wsel", "wsum"], [fbk])

            proj_one(wq, q_tok, "q_tok", AF.Silu)
            proj_one(wk, k_tok, "k_tok", AF.Silu)
            proj_one(wv, v_tok, "v_tok", AF.Silu)

            for s in range(SPT):
                qv = q_tok.rearrange("p (s f) -> p s f", f=D)[:, s]
                kv = k_tok.rearrange("p (s f) -> p s f", f=D)[:, s]
                op("scalar", lambda e, qv=qv: e.activation(sqsc[:, 0:D], qv, AF.Square),
                   ["q_tok"], ["sqsc"])
                op("vector", lambda e, s=s: e.reduce_sum(
                    rqk[:, s * 16:s * 16 + 8],
                    sqsc[:, 0:D].rearrange("p (hh kk) -> p hh kk", kk=DK),
                    axis=AX.X), ["sqsc"], ["rqk"])
                op("scalar", lambda e, kv=kv: e.activation(sqsc[:, 0:D], kv, AF.Square),
                   ["k_tok"], ["sqsc"])
                op("vector", lambda e, s=s: e.reduce_sum(
                    rqk[:, s * 16 + 8:s * 16 + 16],
                    sqsc[:, 0:D].rearrange("p (hh kk) -> p hh kk", kk=DK),
                    axis=AX.X), ["sqsc"], ["rqk"])

            for s in range(SPT):
                sl = rqk[:, s * 16:(s + 1) * 16]
                op("vector", lambda e, sl=sl: e.tensor_scalar(
                    sl, sl, 1.0 / DK, EPS, op0=ALU.mult, op1=ALU.add),
                   ["rqk"], ["rqk"])
                op("scalar", lambda e, sl=sl: e.activation(sl, sl, AF.Ln),
                   ["rqk"], ["rqk"])
                op("scalar", lambda e, sl=sl: e.activation(sl, sl, AF.Exp, scale=-0.5),
                   ["rqk"], ["rqk"])
            for s in range(SPT):
                qv = q_tok.rearrange("p (s f) -> p s f", f=D)[:, s].rearrange(
                    "p (hh kk) -> p hh kk", kk=DK)
                kv = k_tok.rearrange("p (s f) -> p s f", f=D)[:, s].rearrange(
                    "p (hh kk) -> p hh kk", kk=DK)
                rq = rqk[:, s * 16:s * 16 + 8]
                rk = rqk[:, s * 16 + 8:s * 16 + 16]
                op("vector", lambda e, qv=qv, rq=rq: e.tensor_tensor(
                    qv, qv, rq.to_broadcast([128, 8, DK]), ALU.mult),
                   ["q_tok", "rqk"], ["q_tok"])
                op("vector", lambda e, kv=kv, rk=rk: e.tensor_tensor(
                    kv, kv, rk.to_broadcast([128, 8, DK]), ALU.mult),
                   ["k_tok", "rqk"], ["k_tok"])

            for s in range(SPT):
                fb = f_t[s]
                fbk = f"f_t{s}"
                op("vector", lambda e, fb=fb, s=s: e.tensor_tensor(
                    fb.rearrange("p (hh kk) -> p hh kk", kk=DK),
                    fb.rearrange("p (hh kk) -> p hh kk", kk=DK),
                    fhead[:, s * H:(s + 1) * H].to_broadcast([128, H, DK]),
                    ALU.mult), [fbk, "fhead"], [fbk])
                op("scalar", lambda e, fb=fb: e.activation(
                    lam[:, 0:D], fb, AF.Exp), [fbk], ["lam"])
                op("vector", lambda e: e.tensor_scalar(
                    lam[:, 0:D], lam[:, 0:D], -1.0, 1.0, op0=ALU.mult, op1=ALU.add),
                   ["lam"], ["lam"])
                kv = k_tok.rearrange("p (s f) -> p s f", f=D)[:, s]
                op("vector", lambda e, kv=kv: e.tensor_mul(kv, kv, lam[:, 0:D]),
                   ["k_tok", "lam"], ["k_tok"])
                sp = s % 2
                bX, bY = pb[4 + 2 * sp], pb[5 + 2 * sp]
                kX, kY = KPB[4 + 2 * sp], KPB[5 + 2 * sp]
                fv = fb.rearrange("p (hh kk) -> p hh kk", kk=DK)
                q_s = q_tok.rearrange("p (s f) -> p s f", f=D)[:, s]
                k_s = k_tok.rearrange("p (s f) -> p s f", f=D)[:, s]
                v_s = v_tok.rearrange("p (s f) -> p s f", f=D)[:, s]
                v_sv = v_s.rearrange("p (hh kk) -> p hh kk", kk=DK)
                for half, (bk, kk) in enumerate(((bX, kX), (bY, kY))):
                    op("tensor", lambda e, bk=bk, half=half, fb=fb: e.matmul(
                        bk[0:1, 0:512], colsel[:, 1:2],
                        fb[:, half * 512:(half + 1) * 512],
                        start=True, stop=True), [fbk, "colsel"], [kk])
                    op("vector", lambda e, bk=bk, half=half: e.tensor_copy(
                        rbuf[0:1, half * 512:(half + 1) * 512], bk[0:1, 0:512]),
                       [kk], ["rbuf"])
                for hh in range(H):
                    op("tensor", lambda e, hh=hh, fv=fv, bX=bX: e.matmul(
                        bX[:, hh * 2:hh * 2 + 2], fv[:, hh], colsel,
                        start=(hh == 0), stop=(hh == 7)), [fbk, "colsel"], [kX])
                op("vector", lambda e, bX=bX: e.tensor_copy(dstage, bX[:, 0:16]),
                   [kX], ["dstage"])
                ds2 = dstage.rearrange("p (hh two) -> p hh two", two=2)
                op("vector", lambda e: e.tensor_copy(dex[:, 0:8], ds2[:, :, 0]),
                   ["dstage"], ["dex"])
                op("vector", lambda e: e.tensor_copy(dex[:, 16:24], ds2[:, :, 1]),
                   ["dstage"], ["dex"])
                op("vector", lambda e: e.tensor_sub(dex[:, 8:16], dex[:, 0:8],
                                                    dex[:, 16:24]), ["dex"], ["dex"])
                op("scalar", lambda e: e.activation(dex[:, 0:24], dex[:, 0:24],
                                                    AF.Exp), ["dex"], ["dex"])
                op("vector", lambda e: e.tensor_scalar(
                    gkn, dex[:, 8:16], qknw[:, 1:2], None, op0=ALU.mult),
                   ["dex", "qknw"], ["gkn"])
                op("vector", lambda e: e.tensor_mul(qcf, dex[:, 16:24], cumD),
                   ["dex", "cumD"], ["qcf"])
                for half, (bk, kk) in enumerate(((bX, kX), (bY, kY))):
                    for j in range(4):
                        hh = half * 4 + j
                        op("tensor", lambda e, bk=bk, j=j, hh=hh, fv=fv,
                           f=(j == 0):
                           e.matmul(bk[:, j * 128:(j + 1) * 128], tri, fv[:, hh],
                                    start=f, stop=False), [fbk, "tri"], [kk])
                        op("tensor", lambda e, bk=bk, j=j, hh=hh, l=(j == 3):
                           e.matmul(bk[:, j * 128:(j + 1) * 128], negr[0:1, 0:128],
                                    rbuf[0:1, hh * 128:(hh + 1) * 128],
                                    start=False, stop=l), ["negr", "rbuf"], [kk])
                    op("vector", lambda e, bk=bk, half=half: e.tensor_scalar(
                        ppt_s[:, half * 512:(half + 1) * 512], bk[:, 0:512],
                        CLAMP, -CLAMP, op0=ALU.min, op1=ALU.max), [kk], ["ppt_s"])
                op("scalar", lambda e: e.activation(eqs_s, ppt_s, AF.Exp),
                   ["ppt_s"], ["eqs_s"])
                op("scalar", lambda e: e.activation(ppt_s, ppt_s, AF.Exp, scale=-1.0),
                   ["ppt_s"], ["ppt_s"])
                op("vector", lambda e, q_s=q_s: e.tensor_mul(eqs_s, q_s, eqs_s),
                   ["q_tok", "eqs_s"], ["eqs_s"])
                op("vector", lambda e, k_s=k_s: e.tensor_mul(ppt_s, k_s, ppt_s),
                   ["k_tok", "ppt_s"], ["ppt_s"])
                for src_t, srck, dst_t, dstk, wcol in (
                        (eqs_s, "eqs_s", qfm_s, "qfm_s", qknw[:, 0:1]),
                        (ppt_s, "ppt_s", kfm_s, "kfm_s", qknw[:, 1:2])):
                    for half, (bk, kk) in enumerate(((bX, kX), (bY, kY))):
                        for j in range(4):
                            hh = half * 4 + j
                            op("tensor", lambda e, bk=bk, j=j, hh=hh, src_t=src_t,
                               f=(j == 0), l=(j == 3):
                               e.matmul(bk[:, j * 128:(j + 1) * 128],
                                        src_t[:, hh * 128:(hh + 1) * 128],
                                        ident, is_transpose=True,
                                        start=f, stop=l), [srck, "ident"], [kk])
                        op("scalar", lambda e, bk=bk, half=half, dst_t=dst_t,
                           wcol=wcol: e.activation(
                               dst_t[:, half * 512:(half + 1) * 512], bk[:, 0:512],
                               AF.Copy, scale=wcol), [kk, "qknw"], [dstk])
                for half, (bk, kk) in enumerate(((bX, kX), (bY, kY))):
                    for j in range(4):
                        hh = half * 4 + j
                        op("tensor", lambda e, bk=bk, j=j, hh=hh, f=(j == 0),
                           l=(j == 3): e.matmul(
                               bk[:, j * 128:(j + 1) * 128],
                               kfm_s[:, hh * 128:(hh + 1) * 128],
                               qfm_s[:, hh * 128:(hh + 1) * 128],
                               start=f, stop=l), ["kfm_s", "qfm_s"], [kk])
                    op("vector", lambda e, bk=bk, half=half: e.tensor_mul(
                        ams_s[:, half * 512:(half + 1) * 512], bk[:, 0:512], tri4),
                       [kk, "tri4"], ["ams_s"])
                op("vector", lambda e: e.tensor_tensor(
                    stv_s.rearrange("p (hh vv) -> p hh vv", vv=DK),
                    S_loc.rearrange("p (hh vv) -> p hh vv", vv=DK),
                    dex[:, 16:24].to_broadcast([128, 8, DK]), ALU.mult),
                   ["S_loc", "dex"], ["stv_s"])
                for half, (bk, kk) in enumerate(((bX, kX), (bY, kY))):
                    for j in range(4):
                        hh = half * 4 + j
                        op("tensor", lambda e, bk=bk, j=j, hh=hh, v_sv=v_sv,
                           f=(j == 0):
                           e.matmul(bk[:, j * 128:(j + 1) * 128], v_sv[:, hh],
                                    ams_s[:, hh * 128:(hh + 1) * 128],
                                    start=f, stop=False),
                           ["v_tok", "ams_s"], [kk])
                        op("tensor", lambda e, bk=bk, j=j, hh=hh, l=(j == 3):
                           e.matmul(bk[:, j * 128:(j + 1) * 128],
                                    stv_s[:, hh * 128:(hh + 1) * 128],
                                    qfm_s[:, hh * 128:(hh + 1) * 128],
                                    start=False, stop=l),
                           ["stv_s", "qfm_s"], [kk])
                    eng_e = "vector" if half == 0 else "scalar"
                    if half == 0:
                        op("vector", lambda e, bk=bk: e.tensor_copy(
                            o_sub[:, 0:512], bk[:, 0:512]), [kk], ["o_sub"])
                    else:
                        op("scalar", lambda e, bk=bk: e.copy(
                            o_sub[:, 512:1024], bk[:, 0:512]), [kk], ["o_sub"])
                for half, (bk, kk) in enumerate(((bX, kX), (bY, kY))):
                    for j in range(4):
                        hh = half * 4 + j
                        op("tensor", lambda e, bk=bk, j=j, hh=hh, v_sv=v_sv,
                           f=(j == 0),
                           l=(j == 3): e.matmul(
                               bk[:, j * 128:(j + 1) * 128],
                               ppt_s[:, hh * 128:(hh + 1) * 128], v_sv[:, hh],
                               start=f, stop=l), ["ppt_s", "v_tok"], [kk])
                    op("vector", lambda e, bk=bk, half=half: e.tensor_tensor(
                        stv_s.rearrange("p (hh vv) -> p hh vv", vv=DK)[
                            :, half * 4:(half + 1) * 4],
                        bk[:, 0:512].rearrange("p (j vv) -> p j vv", vv=DK),
                        gkn[:, half * 4:(half + 1) * 4].to_broadcast([128, 4, DK]),
                        ALU.mult), [kk, "gkn"], ["stv_s"])
                op("vector", lambda e: e.tensor_tensor(
                    qc_sub.rearrange("p (hh i) -> p hh i", i=CT),
                    qfm_s.rearrange("p (hh i) -> p hh i", i=CT),
                    qcf.to_broadcast([128, 8, CT]), ALU.mult),
                   ["qfm_s", "qcf"], ["qc_sub"])
                op("vector", lambda e: e.tensor_tensor(
                    S_loc.rearrange("p (hh vv) -> p hh vv", vv=DK),
                    S_loc.rearrange("p (hh vv) -> p hh vv", vv=DK),
                    dex[:, 0:8].to_broadcast([128, 8, DK]), ALU.mult),
                   ["S_loc", "dex"], ["S_loc"])
                op("vector", lambda e: e.tensor_add(S_loc, S_loc, stv_s),
                   ["S_loc", "stv_s"], ["S_loc"])
                op("vector", lambda e: e.tensor_mul(cumD, cumD, dex[:, 0:8]),
                   ["cumD", "dex"], ["cumD"])
                sub0 = tok0 + s * CT
                dma("scalar", o_sp.rearrange("hh v tt -> v hh tt")[:, :, sub0:sub0 + CT],
                    o_sub.rearrange("p (hh i) -> p hh i", i=CT), ["o_sub"], ["o_sp_d"])
                dma("scalar", qc_sp.rearrange("hh k tt -> k hh tt")[:, :, sub0:sub0 + CT],
                    qc_sub.rearrange("p (hh i) -> p hh i", i=CT), ["qc_sub"], ["qc_sp_d"])
                if t + 1 < NT:
                    ln1(t + 1, s)

        dma("gpsimd", ag_in[0:H * DK * DK].rearrange("(hh k c) -> k hh c",
                                                      k=DK, c=DK),
            S_loc.rearrange("p (hh vv) -> p hh vv", vv=DK), ["S_loc"], ["ag_in_d"])
        dma("gpsimd", ag_in[H * DK * DK:].rearrange("(k hh) -> k hh", hh=H),
            cumD, ["cumD"], ["ag_in_d"])
        tk.add("gpsimd", lambda e: e.collective_compute(
            "AllGather", ALU.bypass, replica_groups=[list(range(NCORE))],
            ins=[ag_in], outs=[ag_out]),
            reads=["ag_in_d"], writes=["ag_out_d"], cc=True)
        ar.mark()
        gS = A(H * DK)
        gD = A(H)
        op("vector", lambda e: e.memset(cumc, 1.0), [], ["cumc"])
        AGW = H * DK * DK + H * DK
        for j in range(NCORE - 1, -1, -1):
            base = j * AGW
            dma("sync", gS.rearrange("p (hh vv) -> p hh vv", vv=DK),
                ag_out[base:base + H * DK * DK]
                .rearrange("(hh k c) -> k hh c", k=DK, c=DK),
                ["ag_out_d"], ["gS"])
            dma("sync", gD,
                ag_out[base + H * DK * DK:base + AGW]
                .rearrange("(k hh) -> k hh", hh=H), ["ag_out_d"], ["gD"])
            pmj = pmb[:, j:j + 1]
            op("vector", lambda e, pmj=pmj: e.tensor_scalar(
                stv_s[:, 0:H], cumc[:, 0:H], pmj, None, op0=ALU.mult),
               ["pmb", "cumc"], ["stv_s"])
            op("vector", lambda e: e.tensor_tensor(
                ppt_s.rearrange("p (hh vv) -> p hh vv", vv=DK),
                gS.rearrange("p (hh vv) -> p hh vv", vv=DK),
                stv_s[:, 0:H].to_broadcast([128, H, DK]), ALU.mult),
               ["gS", "stv_s"], ["ppt_s"])
            op("vector", lambda e: e.tensor_add(S_in, S_in, ppt_s),
               ["S_in", "ppt_s"], ["S_in"])
            op("vector", lambda e: e.tensor_scalar_add(
                stv_s[:, 16:16 + H], gD[:, 0:H], -1.0), ["gD"], ["stv_s"])
            op("vector", lambda e, pmj=pmj: e.tensor_scalar(
                stv_s[:, 16:16 + H], stv_s[:, 16:16 + H], pmj, 1.0,
                op0=ALU.mult, op1=ALU.add), ["stv_s", "pmb"], ["stv_s"])
            op("vector", lambda e: e.tensor_mul(
                cumc[:, 0:H], cumc[:, 0:H], stv_s[:, 16:16 + H]),
               ["cumc", "stv_s"], ["cumc"])
        ar.release()

        tk.barrier()
        ar.release()

        ar.mark()
        o2 = A(H * 512)
        qc2 = A(H * 512)
        xtile = A(8 * 512)
        x1t = A(8 * 512)
        h2t = A(8 * 512)
        httok = [A(DW) for _ in range(2)]
        wo_s = A(8 * D)
        osq = [A(512) for _ in range(2)]
        gsc = A(8)
        mx8g = A(8)
        selw = A(16)
        ilv = A(16)
        dma("sync", wo_s.rearrange("p (c f) -> p c f", f=D),
            wo_d.rearrange("(c p) f -> p c f", p=128), [], ["wo_s"])

        for t in range(NT):
            tok0 = t * TILE
            dma("sync", o2.rearrange("p (hh tt) -> p hh tt", tt=512),
                o_sp.rearrange("hh v tt -> v hh tt")[:, :, tok0:tok0 + TILE],
                ["o_sp_d"], ["o2"])
            dma("sync", qc2.rearrange("p (hh tt) -> p hh tt", tt=512),
                qc_sp.rearrange("hh k tt -> k hh tt")[:, :, tok0:tok0 + TILE],
                ["qc_sp_d"], ["qc2"])
            dma("sync", xtile.rearrange("p (c tt) -> p c tt", tt=512),
                xin[:, tok0:tok0 + TILE].rearrange("(c p) tt -> p c tt", p=128),
                [], ["xtile"])
            for hh in range(H):
                par = hh % 2
                Ssl = S_in.rearrange("p (hh vv) -> p hh vv", vv=DK)[:, hh]
                qcs = qc2.rearrange("p (hh tt) -> p hh tt", tt=512)[:, hh]
                osl = o2.rearrange("p (hh tt) -> p hh tt", tt=512)[:, hh]
                op("tensor", lambda e, Ssl=Ssl, qcs=qcs, par=par: e.matmul(
                    pb[par][:, 0:512], Ssl, qcs, start=True, stop=True),
                   ["S_in", "qc2"], [KPB[par]])
                op("vector", lambda e, osl=osl, par=par: e.tensor_add(
                    osl, osl, pb[par][:, 0:512]), ["o2", KPB[par]], ["o2"])
            for hh in range(H):
                par = hh % 2
                osl = o2.rearrange("p (hh tt) -> p hh tt", tt=512)[:, hh]
                op("scalar", lambda e, osl=osl, par=par: e.activation(
                    osq[par], osl, AF.Square), ["o2"], [f"osq{par}"])
                op("tensor", lambda e, par=par, f=(hh == 0), l=(hh == 7): e.matmul(
                    pb[2][0:1, 0:512], colsel[:, 0:1], osq[par],
                    start=f, stop=l), [f"osq{par}", "colsel"], [KPB[2]])
            rr2 = prow[0:1, 0:512]
            op("vector", lambda e: e.tensor_scalar(
                rr2, pb[2][0:1, 0:512], 1.0 / D, EPS, op0=ALU.mult, op1=ALU.add),
               [KPB[2]], ["prow"])
            op("scalar", lambda e: e.activation(rr2, rr2, AF.Ln), ["prow"], ["prow"])
            op("scalar", lambda e: e.activation(rr2, rr2, AF.Exp, scale=-0.5),
               ["prow"], ["prow"])
            op("tensor", lambda e: e.matmul(
                pb[3][:, 0:512], onesr[0:1, 0:128], rr2, start=True, stop=True),
               ["prow", "onesr"], [KPB[3]])
            for hh in range(H):
                osl = o2.rearrange("p (hh tt) -> p hh tt", tt=512)[:, hh]
                op("vector", lambda e, osl=osl, hh=hh: e.scalar_tensor_tensor(
                    osl, osl, gnwc[:, hh:hh + 1], pb[3][:, 0:512],
                    op0=ALU.mult, op1=ALU.mult), ["o2", "gnwc", KPB[3]], ["o2"])
            for c in range(8):
                par = c % 2
                for hv in range(8):
                    op("tensor", lambda e, c=c, hv=hv, par=par, f=(hv == 0), l=(hv == 7):
                       e.matmul(pb[4 + par][:, 0:512],
                                wo_s.rearrange("p (cc f) -> p cc f", f=D)[
                                    :, hv, c * 128:(c + 1) * 128],
                                o2.rearrange("p (hh tt) -> p hh tt", tt=512)[:, hv],
                                start=f, stop=l), ["wo_s", "o2"], [KPB[4 + par]])
                x1c = x1t.rearrange("p (c tt) -> p c tt", tt=512)[:, c]
                xc = xtile.rearrange("p (c tt) -> p c tt", tt=512)[:, c]
                op("vector", lambda e, x1c=x1c, xc=xc, c=c, par=par:
                   e.scalar_tensor_tensor(x1c, pb[4 + par][:, 0:512], modcol(2, c),
                                          xc, op0=ALU.mult, op1=ALU.add),
                   [KPB[4 + par], "modc", "xtile"], ["x1t"])
            dma("scalar", x1_sp.rearrange("c p tt -> p c tt")[:, :, tok0:tok0 + TILE],
                x1t.rearrange("p (c tt) -> p c tt", tt=512), ["x1t"], ["x1_sp_d"])
            x1v = x1t.rearrange("p (c tt) -> p c tt", tt=512)
            for c in range(8):
                op("tensor", lambda e, c=c, x1v=x1v, f=(c == 0), l=(c == 7):
                   e.matmul(pb[2][0:1, 0:512], colsel[:, 0:1], x1v[:, c],
                            start=f, stop=l), ["x1t", "colsel"], [KPB[2]])
            for c in range(8):
                par = c % 2
                op("scalar", lambda e, c=c, x1v=x1v, par=par: e.activation(
                    osq[par], x1v[:, c], AF.Square), ["x1t"], [f"osq{par}"])
                op("tensor", lambda e, c=c, par=par, f=(c == 0), l=(c == 7):
                   e.matmul(pb[3][0:1, 0:512], colsel[:, 0:1], osq[par],
                            start=f, stop=l), [f"osq{par}", "colsel"], [KPB[3]])
            m2 = prow[0:1, 0:512]; r2 = prow[0:1, 512:1024]; mr2 = prow[0:1, 1024:1536]
            op("vector", lambda e: e.tensor_scalar_mul(m2, pb[2][0:1, 0:512], 1.0 / D),
               [KPB[2]], ["prow"])
            op("vector", lambda e: e.tensor_mul(r2, m2, m2), ["prow"], ["prow"])
            op("vector", lambda e: e.scalar_tensor_tensor(
                r2, pb[3][0:1, 0:512], 1.0 / D, r2, op0=ALU.mult, op1=ALU.subtract),
               [KPB[3], "prow"], ["prow"])
            op("scalar", lambda e: e.activation(r2, r2, AF.Ln, bias=EPS),
               ["prow"], ["prow"])
            op("scalar", lambda e: e.activation(r2, r2, AF.Exp, scale=-0.5),
               ["prow"], ["prow"])
            op("vector", lambda e: e.tensor_mul(mr2, m2, r2), ["prow"], ["prow"])
            for c in range(8):
                op("tensor", lambda e, c=c: e.matmul(
                    pb[2][:, 0:512], modrow_sl(4, c), r2, start=True, stop=True),
                   ["modrow", "prow"], [KPB[2]])
                op("tensor", lambda e, c=c: e.matmul(
                    pb[3][:, 0:512], modrow_sl(4, c), mr2, start=True, stop=False),
                   ["modrow", "prow"], [KPB[3]])
                op("tensor", lambda e, c=c: e.matmul(
                    pb[3][:, 0:512], modrow_sl(3, c), negr[0:1, 0:512],
                    start=False, stop=True), ["modrow", "negr"], [KPB[3]])
                h2c = h2t.rearrange("p (c tt) -> p c tt", tt=512)[:, c]
                op("vector", lambda e, h2c=h2c, c=c, x1v=x1v: e.tensor_mul(
                    h2c, x1v[:, c], pb[2][:, 0:512]), ["x1t", KPB[2]], ["h2t"])
                op("vector", lambda e, h2c=h2c: e.tensor_sub(
                    h2c, h2c, pb[3][:, 0:512]), ["h2t", KPB[3]], ["h2t"])
            dma("scalar", h2_sp.rearrange("c p tt -> p c tt")[:, :, tok0:tok0 + TILE],
                h2t.rearrange("p (c tt) -> p c tt", tt=512), ["h2t"], ["h2_sp_d"])
            h2vv = h2t.rearrange("p (c tt) -> p c tt", tt=512)
            for s in range(SPT):
                hpar = s % 2
                hb = httok[hpar]
                hbk = f"httok{hpar}"
                for c in range(8):
                    tp = c % 2
                    op("tensor", lambda e, c=c, s=s, tp=tp, h2vv=h2vv: e.matmul(
                        pb[tp][:, 0:128], h2vv[:, c, s * CT:(s + 1) * CT], ident,
                        is_transpose=True, start=True, stop=True),
                       ["h2t", "ident"], [KPB[tp]])
                    op("scalar", lambda e, c=c, tp=tp, hb=hb: e.copy(
                        hb[:, c * 128:(c + 1) * 128], pb[tp][:, 0:128]),
                       [KPB[tp]], [hbk])
                sub0 = tok0 + s * CT
                op("vector", lambda e, hb=hb, sub0=sub0: e.tensor_scalar_add(
                    hb[:, D:D + 1], iotac, float(sub0)), ["iotac"], [hbk])
                dma("scalar", h2tok_d[sub0:sub0 + CT, :], hb, [hbk], ["h2tok_dd"])
            h2v = h2t.rearrange("p (c tt) -> p c tt", tt=512)
            for s in range(SPT):
                for c in range(8):
                    op("tensor", lambda e, c=c, s=s, h2v=h2v, f=(c == 0):
                       e.matmul(pb[6][:, 0:H], h2v[:, c, s * CT:(s + 1) * CT],
                                gw_s.rearrange("p (c hh) -> p c hh", hh=H)[:, c],
                                start=f, stop=False), ["h2t", "gw_s"], [KPB[6]])
                op("tensor", lambda e: e.matmul(
                    pb[6][:, 0:H], onesr[0:1, 0:128], rowsc[0:1, 16:24],
                    start=False, stop=True), ["onesr", "rowsc"], [KPB[6]])
                op("scalar", lambda e: e.activation(gsc, pb[6][:, 0:H], AF.Sigmoid),
                   [KPB[6]], ["gsc"])
                op("vector", lambda e: e.max(mx8g, gsc), ["gsc"], ["mx8g"])
                op("vector", lambda e: e.tensor_scalar(
                    selw[:, 0:8], gsc, mx8g[:, 1:2], None, op0=ALU.is_ge),
                   ["gsc", "mx8g"], ["selw"])
                op("vector", lambda e: e.tensor_add(
                    selw[:, 8:9], mx8g[:, 0:1], mx8g[:, 1:2]), ["mx8g"], ["selw"])
                op("vector", lambda e: e.tensor_scalar_add(
                    selw[:, 8:9], selw[:, 8:9], 1e-8), ["selw"], ["selw"])
                op("vector", lambda e: e.reciprocal(selw[:, 9:10], selw[:, 8:9]),
                   ["selw"], ["selw"])
                ilv2 = ilv.rearrange("p (ee two) -> p ee two", two=2)
                op("vector", lambda e, ilv2=ilv2: e.tensor_mul(
                    ilv2[:, :, 0], selw[:, 0:8], gsc), ["selw", "gsc"], ["ilv"])
                op("vector", lambda e, ilv2=ilv2: e.tensor_scalar(
                    ilv2[:, :, 0], ilv2[:, :, 0], selw[:, 9:10], None, op0=ALU.mult),
                   ["ilv", "selw"], ["ilv"])
                op("vector", lambda e, ilv2=ilv2: e.tensor_copy(
                    ilv2[:, :, 1], selw[:, 0:8]), ["selw"], ["ilv"])
                fs = t * SPT + s
                op("vector", lambda e, fs=fs: e.tensor_copy(
                    selcols.rearrange("p (ee f) -> p ee f", f=NSUB)[:, :, fs],
                    selw[:, 0:8]), ["selw"], ["selcols"])
                first = (t == 0 and s == 0)
                last = (t == NT - 1 and s == SPT - 1)
                op("tensor", lambda e, f=first, l=last: e.matmul(
                    pb[7][0:16, 0:1], ilv, colsel[:, 0:1],
                    start=f, stop=l), ["ilv", "colsel"], [KPB[7]])

        op("vector", lambda e: e.tensor_copy(g16[0:16, 0:1], pb[7][0:16, 0:1]),
           [KPB[7]], ["g16"])
        dma("gpsimd", ar_in[:, None], g16[0:16, 0:1], ["g16"], ["ar_in_d"])
        tk.add("gpsimd", lambda e: e.collective_compute(
            "AllReduce", ALU.add, replica_groups=[list(range(NCORE))],
            ins=[ar_in], outs=[ar_out]), reads=["ar_in_d"], writes=["ar_out_d"],
            cc=True)
        tk.barrier()
        ar.release()

        ar.mark()
        sgw_s = A(8 * D); suw_s = A(8 * D); sdw_s = A(8 * D)
        h2l = A(8 * 512)
        zt = A(8 * 512)
        zg = [A(512) for _ in range(2)]
        ysh = [A(D) for _ in range(2)]
        csr2 = prow[:, 0:8 * NSUB]
        cexr = prow[:, 512:512 + 8 * NSUB]
        ccolT = A(NSUB)
        ccolF = A(8)
        cexf = A(8)
        postot = A(8 * NSUB)
        nfbc = A(8)
        nff_f = A(8)
        tailf = A(32)
        ztile = A(DW)
        rstage = [A(DW) for _ in range(2)]

        NW = 8 * NSUB
        BNDREG = {}

        def _gp_pro(eng):
            r = eng.alloc_register(name="bndcape")
            eng.reg_mov(r, CAPE - 1)
            BNDREG["r"] = r
        tk.prologue["gpsimd"] = _gp_pro
        op("tensor", lambda e: e.matmul(pb[1][0:1, 0:NW], colsel[:, 0:1],
                                        selcols, start=True, stop=True),
           ["selcols", "colsel"], [KPB[1]])
        op("vector", lambda e: e.tensor_copy(csr2[0:1, :], pb[1][0:1, 0:NW]),
           [KPB[1]], ["csr2", "prow"])
        op("vector", lambda e: e.reduce_sum(
            nff_f[0:1, 0:8],
            csr2[0:1, :].rearrange("o (ee f) -> o ee f", f=NSUB),
            axis=AX.X), ["csr2"], ["nff"])
        gdep_op = tk.add("vector", lambda e: e.tensor_copy(
            nfi8[0:1, 0:8], nff_f[0:1, 0:8]), reads=["nff"], writes=["nfi8"])
        tk.guard_src = lambda e_i: nfi8[0:1, e_i:e_i + 1]
        tk.guard_dep = gdep_op
        dma("scalar", rowx_d[None, :], csr2[0:1, :], ["csr2"], ["rowx"])
        dma("scalar", ccolT[0:8, :], rowx_d.rearrange("(p f) -> p f", p=8),
            ["rowx"], ["ccolT"])
        op("tensor", lambda e: e.matmul(
            pb[1][0:NSUB, 256:264], ccolT[0:8, :], ident[0:8, 0:8],
            is_transpose=True, start=True, stop=True), ["ccolT", "ident"], [KPB[1]])
        op("vector", lambda e: e.tensor_copy(ccolF[0:NSUB, 0:8],
                                             pb[1][0:NSUB, 256:264]),
           [KPB[1]], ["ccolF"])
        op("tensor", lambda e: e.matmul(
            pb[1][0:NSUB, 264:272], t32[0:NSUB, 0:NSUB], ccolF[0:NSUB, 0:8],
            start=True, stop=True), ["t32", "ccolF"], [KPB[1]])
        op("vector", lambda e: e.tensor_copy(cexf[0:NSUB, 0:8],
                                             pb[1][0:NSUB, 264:272]),
           [KPB[1]], ["cexf"])
        dma("scalar", rowx2_d.rearrange("(ee f) -> f ee", f=NSUB),
            cexf[0:NSUB, 0:8], ["cexf"], ["rowx2"], slow=True)
        dma("scalar", cexr[0:1, :], rowx2_d[None, :], ["rowx2"], ["cexr"])
        op("tensor", lambda e: e.matmul(pb[0][:, 0:NW], tri, selcols,
                                        start=True, stop=False),
           ["tri", "selcols"], [KPB[0]])
        op("tensor", lambda e: e.matmul(pb[0][:, 0:NW], onesr[0:1, 0:128],
                                        cexr[0:1, :], start=False, stop=True),
           ["onesr", "cexr"], [KPB[0]])
        op("vector", lambda e: e.tensor_copy(postot, pb[0][:, 0:NW]),
           [KPB[0]], ["postot"])
        op("vector", lambda e: e.tensor_sub(postot, postot, selcols),
           ["postot", "selcols"], ["postot"])
        op("vector", lambda e: e.scalar_tensor_tensor(
            postot, selcols, -BIGP, postot, op0=ALU.mult, op1=ALU.add),
           ["selcols", "postot"], ["postot"])
        op("vector", lambda e: e.tensor_scalar_add(postot, postot, BIGP),
           ["postot"], ["postot"])
        op("vector", lambda e: e.tensor_copy(idxi[:, :], postot),
           ["postot"], ["idxi"])
        op("tensor", lambda e: e.matmul(pb[1][:, 272:280], onesr[0:1, 0:128],
                                        nff_f[0:1, 0:8], start=True, stop=True),
           ["onesr", "nff"], [KPB[1]])
        op("vector", lambda e: e.tensor_copy(nfbc, pb[1][:, 272:280]),
           [KPB[1]], ["nfbc"])
        for e_i in range(8):
            for j in range(4):
                op("vector", lambda e, e_i=e_i, j=j: e.scalar_tensor_tensor(
                    tailf[:, e_i * 4 + j:e_i * 4 + j + 1], iotac, float(j * 128),
                    nfbc[:, e_i:e_i + 1], op0=ALU.add, op1=ALU.add),
                   ["iotac", "nfbc"], ["tailf"])
        op("vector", lambda e: e.tensor_copy(idxt[:, :], tailf),
           ["tailf"], ["idxt"])
        op("vector", lambda e: e.memset(ztile, 0.0), [], ["ztile"])
        op("vector", lambda e: e.memset(ztile[:, D:D + 1], BIGP), [], ["ztile"])
        op("vector", lambda e: e.memset(postot[:, 0:4], BIGP), ["postot"],
           ["postot"])
        op("vector", lambda e: e.tensor_copy(bigi[:, :], postot[:, 0:4]),
           ["postot"], ["bigi"])
        for fs in range(NSUB):
            rb = rstage[fs % 2]
            rbk = f"rstage{fs % 2}"
            dma("gpsimd", rb, h2tok_d[fs * CT:(fs + 1) * CT, :],
                ["h2tok_dd"], [rbk])
            for e_i in range(8):
                idxcol = idxi[:, e_i * NSUB + fs:e_i * NSUB + fs + 1]
                tk.add("gpsimd", lambda e, e_i=e_i, rb=rb, idxcol=idxcol:
                       e.indirect_dma_start(
                           out=compact_d[e_i],
                           out_offset=IndirectOffsetOnAxis(idxcol, 0),
                           in_=rb, in_offset=None,
                           bounds_check=BNDREG["r"], oob_is_err=False),
                       reads=[rbk, "idxi"], writes=[f"cp{e_i}"], dma=True,
                       tag=f"cp{e_i}")
        for e_i in range(8):
            for j in range(4):
                idc = idxt[:, e_i * 4 + j:e_i * 4 + j + 1]
                tk.add("gpsimd", lambda e, e_i=e_i, idc=idc:
                       e.indirect_dma_start(
                           out=compact_d[e_i],
                           out_offset=IndirectOffsetOnAxis(idc, 0),
                           in_=ztile, in_offset=None,
                           bounds_check=BNDREG["r"], oob_is_err=False),
                       reads=["ztile", "idxt"], writes=[f"cp{e_i}"], dma=True,
                       tag=f"cp{e_i}")
        dma("sync", sgw_s.rearrange("p (c f) -> p c f", f=D),
            sgwt.rearrange("(c p) f -> p c f", p=128), [], ["sgw_s"])
        dma("sync", suw_s.rearrange("p (c f) -> p c f", f=D),
            suwt.rearrange("(c p) f -> p c f", p=128), [], ["suw_s"])
        dma("sync", sdw_s.rearrange("p (c f) -> p c f", f=D),
            sdwt.rearrange("(c p) f -> p c f", p=128), [], ["sdw_s"])
        for t in range(NT):
            tok0 = t * TILE
            dma("sync", h2l.rearrange("p (c tt) -> p c tt", tt=512),
                h2_sp.rearrange("c p tt -> p c tt")[:, :, tok0:tok0 + TILE],
                ["h2_sp_d"], ["h2l"])
            h2v = h2l.rearrange("p (c tt) -> p c tt", tt=512)
            for ic in range(8):
                gp_ = ic % 2
                for c in range(8):
                    op("tensor", lambda e, ic=ic, c=c, h2v=h2v, gp_=gp_,
                       f=(c == 0), l=(c == 7):
                       e.matmul(pb[gp_][:, 0:512],
                                sgw_s.rearrange("p (cc f) -> p cc f", f=D)[
                                    :, c, ic * 128:(ic + 1) * 128],
                                h2v[:, c], start=f, stop=l),
                       ["sgw_s", "h2l"], [KPB[gp_]])
                op("scalar", lambda e, gp_=gp_: e.activation(
                    zg[gp_], pb[gp_][:, 0:512], AF.Silu), [KPB[gp_]], [f"zg{gp_}"])
                for c in range(8):
                    op("tensor", lambda e, ic=ic, c=c, h2v=h2v, gp_=gp_,
                       f=(c == 0), l=(c == 7):
                       e.matmul(pb[2 + gp_][:, 0:512],
                                suw_s.rearrange("p (cc f) -> p cc f", f=D)[
                                    :, c, ic * 128:(ic + 1) * 128],
                                h2v[:, c], start=f, stop=l),
                       ["suw_s", "h2l"], [KPB[2 + gp_]])
                zsl = zt.rearrange("p (ic tt) -> p ic tt", tt=512)[:, ic]
                op("vector", lambda e, zsl=zsl, gp_=gp_: e.tensor_mul(
                    zsl, zg[gp_], pb[2 + gp_][:, 0:512]),
                   [f"zg{gp_}", KPB[2 + gp_]], ["zt"])
            for s4 in range(SPT):
                ypar = s4 % 2
                yb = ysh[ypar]
                ybk = f"ysh{ypar}"
                for half in range(2):
                    par2 = (s4 * 2 + half) % 2
                    for ic in range(8):
                        op("tensor", lambda e, s4=s4, half=half, ic=ic, par2=par2,
                           f=(ic == 0), l=(ic == 7):
                           e.matmul(pb[4 + par2][:, 0:512],
                                    zt.rearrange("p (ic tt) -> p ic tt", tt=512)[
                                        :, ic, s4 * CT:(s4 + 1) * CT],
                                    sdw_s.rearrange("p (cc f) -> p cc f", f=D)[
                                        :, ic, half * 512:(half + 1) * 512],
                                    start=f, stop=l), ["zt", "sdw_s"],
                           [KPB[4 + par2]])
                    op("vector", lambda e, half=half, yb=yb, par2=par2:
                       e.tensor_copy(yb[:, half * 512:(half + 1) * 512],
                                     pb[4 + par2][:, 0:512]),
                       [KPB[4 + par2]], [ybk])
                sub0 = tok0 + s4 * CT
                dma("scalar", acc_tok_d[sub0:sub0 + CT, :], yb, [ybk],
                    ["acc_tok"])
        tk.barrier()
        ar.release()

        dma("gpsimd", wm2[0:8, :], ar_out.rearrange("(ee two) -> ee two", two=2),
            ["ar_out_d"], ["wm2"])
        op("vector", lambda e: e.tensor_scalar_max(wm2[0:8, 1:2], wm2[0:8, 1:2], 1.0),
           ["wm2"], ["wm2"])
        op("vector", lambda e: e.reciprocal(wm2[0:8, 1:2], wm2[0:8, 1:2]),
           ["wm2"], ["wm2"])
        op("vector", lambda e: e.tensor_mul(wm2[0:8, 0:1], wm2[0:8, 0:1],
                                            wm2[0:8, 1:2]), ["wm2"], ["wm2"])
        dma("gpsimd", wm_d[:, None], wm2[0:8, 0:1], ["wm2"], ["wm_dd"])
        for e_i in range(8):
            dma("gpsimd", wmcol[:, e_i:e_i + 1],
                wm_d[e_i:e_i + 1].partition_broadcast(128), ["wm_dd"], ["wmcol"])
        ar.mark()
        wgd = [A(8 * 512) for _ in range(2)]
        wud = [A(8 * 512) for _ in range(2)]
        wdd = [A(4 * D) for _ in range(2)]
        cblk = [A(4 * DW)]
        h2cm = [A(8 * 512)]
        zcm = [A(4 * 512)]
        zge = [A(512) for _ in range(2)]
        ysb = [A(D) for _ in range(4)]
        def ld_wg(e_i):
            p_ = e_i % 2
            dma("scalar", wgd[p_].rearrange("p (c f) -> p c f", f=512),
                egwt[e_i].rearrange("(c p) f -> p c f", p=128), [],
                [f"wg_s{p_}"])

        def ld_wu(e_i):
            p_ = e_i % 2
            dma("scalar", wud[p_].rearrange("p (c f) -> p c f", f=512),
                euwt[e_i].rearrange("(c p) f -> p c f", p=128), [],
                [f"wu_s{p_}"])

        def ld_wd(e_i):
            p_ = e_i % 2
            dma("scalar", wdd[p_].rearrange("p (c f) -> p c f", f=D),
                edwt[e_i].rearrange("(c p) f -> p c f", p=128), [],
                [f"wdn_s{p_}"])

        ld_wg(0); ld_wu(0); ld_wd(0)
        for e_i in range(8):
            epar = e_i % 2
            wg_s = wgd[epar]; wu_s = wud[epar]; wdn_s = wdd[epar]
            wgk = f"wg_s{epar}"; wuk = f"wu_s{epar}"; wdk = f"wdn_s{epar}"
            for b in range(NBLK):
                if e_i + 1 < 8:
                    if b == min(2, NBLK - 1):
                        ld_wg(e_i + 1)
                    if b == min(4, NBLK - 1):
                        ld_wu(e_i + 1)
                    if b == min(6, NBLK - 1):
                        ld_wd(e_i + 1)
                g = (e_i, b * 512)
                bpar = b % 2
                cb = cblk[0]; cbk = "cblk0"
                hc = h2cm[0]; hck = "h2cm0"
                zb = zcm[0]; zbk = "zcm0"
                tk.add("sync", lambda e, cb=cb, e_i=e_i, b=b: e.dma_start(
                    out=cb.rearrange("p (s f) -> p s f", f=DW),
                    in_=compact_d[e_i][b * 512:(b + 1) * 512, :]
                    .rearrange("(s p) f -> p s f", p=128)),
                    reads=[f"cp{e_i}"], writes=[cbk], dma=True, tag=cbk, guard=g)
                cbv = cb.rearrange("p (s f) -> p s f", f=DW)
                op("vector", lambda e, bpar=bpar: e.tensor_copy(
                    invi[:, bpar * 4:(bpar + 1) * 4], bigi[:, :]),
                   ["bigi"], ["invi"])
                for s4 in range(SPT):
                    for c in range(8):
                        tp = 6 + (c % 2)
                        op2 = tk.add("tensor", lambda e, s4=s4, c=c, tp=tp, cbv=cbv:
                                     e.matmul(pb[tp][:, 0:128],
                                              cbv[:, s4, c * 128:(c + 1) * 128],
                                              ident, is_transpose=True,
                                              start=True, stop=True),
                                     reads=[cbk, "ident"], writes=[KPB[tp]],
                                     guard=g)
                        tk.add("scalar", lambda e, s4=s4, c=c, tp=tp, hc=hc:
                               e.copy(hc.rearrange("p (cc tt) -> p cc tt", tt=512)[
                                   :, c, s4 * CT:(s4 + 1) * CT], pb[tp][:, 0:128]),
                               reads=[KPB[tp]], writes=[hck], guard=g)
                    tk.add("vector", lambda e, s4=s4, bpar=bpar, cbv=cbv:
                           e.tensor_copy(invi[:, bpar * 4 + s4:bpar * 4 + s4 + 1],
                                         cbv[:, s4, D:D + 1]),
                           reads=[cbk], writes=["invi"], guard=g)
                hcv = hc.rearrange("p (cc tt) -> p cc tt", tt=512)
                for ic in range(4):
                    gp_ = ic % 2
                    for c in range(8):
                        tk.add("tensor", lambda e, ic=ic, c=c, hcv=hcv, gp_=gp_,
                               wg_s=wg_s, f=(c == 0), l=(c == 7):
                               e.matmul(pb[gp_][:, 0:512],
                                        wg_s.rearrange("p (cc f) -> p cc f",
                                                       f=512)[
                                            :, c, ic * 128:(ic + 1) * 128],
                                        hcv[:, c], start=f, stop=l),
                               reads=[wgk, hck], writes=[KPB[gp_]], guard=g)
                    tk.add("scalar", lambda e, gp_=gp_: e.activation(
                        zge[gp_], pb[gp_][:, 0:512], AF.Silu),
                        reads=[KPB[gp_]], writes=[f"zge{gp_}"], guard=g)
                    for c in range(8):
                        tk.add("tensor", lambda e, ic=ic, c=c, hcv=hcv, gp_=gp_,
                               wu_s=wu_s, f=(c == 0), l=(c == 7):
                               e.matmul(pb[2 + gp_][:, 0:512],
                                        wu_s.rearrange("p (cc f) -> p cc f",
                                                       f=512)[
                                            :, c, ic * 128:(ic + 1) * 128],
                                        hcv[:, c], start=f, stop=l),
                               reads=[wuk, hck], writes=[KPB[2 + gp_]], guard=g)
                    zsl = zb.rearrange("p (ic tt) -> p ic tt", tt=512)[:, ic]
                    tk.add("vector", lambda e, zsl=zsl, gp_=gp_: e.tensor_mul(
                        zsl, zge[gp_], pb[2 + gp_][:, 0:512]),
                        reads=[f"zge{gp_}", KPB[2 + gp_]], writes=[zbk], guard=g)
                for s4 in range(SPT):
                    yb = ysb[s4]; ybk2 = f"ysb{s4}"
                    for half in range(2):
                        par2 = (s4 * 2 + half) % 2
                        for ic in range(4):
                            tk.add("tensor", lambda e, s4=s4, half=half, ic=ic,
                                   par2=par2, zb=zb, wdn_s=wdn_s,
                                   f=(ic == 0), l=(ic == 3):
                                   e.matmul(pb[4 + par2][:, 0:512],
                                            zb.rearrange("p (ic tt) -> p ic tt",
                                                         tt=512)[
                                                :, ic, s4 * CT:(s4 + 1) * CT],
                                            wdn_s.rearrange("p (cc f) -> p cc f",
                                                            f=D)[
                                                :, ic,
                                                half * 512:(half + 1) * 512],
                                            start=f, stop=l),
                                   reads=[zbk, wdk], writes=[KPB[4 + par2]],
                                   guard=g)
                        tk.add("scalar", lambda e, half=half, yb=yb, par2=par2,
                               e_i=e_i: e.activation(
                                   yb[:, half * 512:(half + 1) * 512],
                                   pb[4 + par2][:, 0:512], AF.Copy,
                                   scale=wmcol[:, e_i:e_i + 1]),
                               reads=[KPB[4 + par2], "wmcol"], writes=[ybk2],
                               guard=g)
                    iv = invi[:, bpar * 4 + s4:bpar * 4 + s4 + 1]
                    tk.add("gpsimd", lambda e, yb=yb, iv=iv:
                           e.indirect_dma_start(
                               out=acc_tok_d,
                               out_offset=IndirectOffsetOnAxis(iv, 0),
                               in_=yb, in_offset=None,
                               bounds_check=BNDREG["r"], oob_is_err=False,
                               compute_op=ALU.add),
                           reads=[ybk2, "invi"], writes=["acc_tok"], dma=True,
                           tag="accadd")
        tk.barrier()
        ar.release()

        ar.mark()
        x1l2 = [A(8 * 512) for _ in range(2)]
        accl2 = [A(4 * D) for _ in range(2)]
        gbbc = A(D)
        tmp4 = [A(D) for _ in range(2)]
        outt = [A(D) for _ in range(2)]
        dma("sync", gbbc, modb_d[5 * D:6 * D].partition_broadcast(128),
            ["modb"], ["gbbc"])
        for t in range(NT):
            tok0 = t * TILE
            tp4 = t % 2
            x1l = x1l2[tp4]
            accl = accl2[tp4]
            x1lk = f"x1l{tp4}"
            acclk = f"accl{tp4}"
            dma("sync", x1l.rearrange("p (c tt) -> p c tt", tt=512),
                x1_sp.rearrange("c p tt -> p c tt")[:, :, tok0:tok0 + TILE],
                ["x1_sp_d"], [x1lk])
            dma("sync", accl.rearrange("p (s f) -> p s f", f=D),
                acc_tok_d[tok0:tok0 + TILE, :].rearrange("(s p) f -> p s f",
                                                         p=128),
                ["acc_tok"], [acclk])
            accv = accl.rearrange("p (s f) -> p s f", f=D)
            x1v4 = x1l.rearrange("p (c tt) -> p c tt", tt=512)
            for s in range(SPT):
                par = s % 2
                ob = outt[par]
                obk = f"outt{par}"
                for c in range(8):
                    tp = 4 + (c % 2)
                    op("tensor", lambda e, c=c, s=s, tp=tp, x1v4=x1v4: e.matmul(
                        pb[tp][:, 0:128], x1v4[:, c, s * CT:(s + 1) * CT],
                        ident, is_transpose=True, start=True, stop=True),
                       [x1lk, "ident"], [KPB[tp]])
                    op("vector", lambda e, c=c, tp=tp, ob=ob: e.tensor_copy(
                        ob[:, c * 128:(c + 1) * 128], pb[tp][:, 0:128]),
                       [KPB[tp]], [obk])
                tb = tmp4[par]
                op("vector", lambda e, s=s, tb=tb, accv=accv: e.tensor_mul(
                    tb, accv[:, s], gbbc), [acclk, "gbbc"], [f"tmp4{par}"])
                op("vector", lambda e, ob=ob, tb=tb: e.tensor_add(ob, ob, tb),
                   [obk, f"tmp4{par}"], [obk])
                sub0 = tok0 + s * CT
                dma("gpsimd", yout[sub0:sub0 + CT, :], ob, [obk], ["yout_d"])
        tk.barrier()
        ar.release()

        globals()["_last_tk"] = tk
        semnames = sorted({o.sem for o in tk.ops})
        sems = {nm: ctx.enter_context(nc.semaphore(nm)) for nm in semnames}
        block = ctx.enter_context(nc.Block())
        tk.emit(block, sems)
    return nc


def _host_prep(inputs, TPC):
    x = np.ascontiguousarray(np.asarray(inputs["L0_attn_hidden_states"],
                                        dtype=np.float32))
    B, T, _ = x.shape
    ident = np.eye(128, dtype=np.float32)
    tri = np.triu(np.ones((128, 128), np.float32))
    tri4 = np.ascontiguousarray(np.tile(tri, (1, 4)))
    colsel = np.zeros((128, 2), np.float32)
    colsel[:, 0] = 1.0
    colsel[:CT // 2, 1] = 1.0
    onesr = np.ones((1, 512), np.float32)
    negr = -np.ones((1, 512), np.float32)
    k32 = np.arange(32)
    t32 = (k32[:, None] < k32[None, :]).astype(np.float32)
    iotac = np.arange(128, dtype=np.float32)[:, None]
    f32c = lambda a: np.ascontiguousarray(np.asarray(a, dtype=np.float32))
    common = {
        "wq": f32c(inputs["L0_attn_Wq"]), "wk": f32c(inputs["L0_attn_Wk"]),
        "wv": f32c(inputs["L0_attn_Wv"]), "wr": f32c(inputs["L0_attn_Wr"]),
        "wa": f32c(inputs["L0_attn_Wa"]), "wo": f32c(inputs["L0_attn_Wo"]),
        "a_log": f32c(inputs["L0_attn_A_log"]),
        "dt_b": f32c(inputs["L0_attn_dt_bias"]),
        "qnw": f32c(inputs["L0_attn_q_norm_w"]),
        "knw": f32c(inputs["L0_attn_k_norm_w"]),
        "gnw": f32c(inputs["L0_attn_g_norm_w"]),
        "gate_wt": f32c(np.transpose(inputs["L0_ffn_gate_w"])),
        "lb_b": f32c(inputs["L0_ffn_lb_bias"]),
        "egwt": f32c(np.transpose(inputs["L0_ffn_expert_gate_w"], (0, 2, 1))),
        "euwt": f32c(np.transpose(inputs["L0_ffn_expert_up_w"], (0, 2, 1))),
        "edwt": f32c(np.transpose(inputs["L0_ffn_expert_down_w"], (0, 2, 1))),
        "sgwt": f32c(np.transpose(inputs["L0_ffn_shared_gate_w"])),
        "suwt": f32c(np.transpose(inputs["L0_ffn_shared_up_w"])),
        "sdwt": f32c(np.transpose(inputs["L0_ffn_shared_down_w"])),
        "wmod": f32c(inputs["L0_wmod"]),
        "ident": ident, "tri": tri, "tri4": tri4, "colsel": colsel,
        "onesr": onesr, "negr": negr, "t32": t32, "iotac": iotac,
    }
    cfull = f32c(inputs["L0_c"])
    in_maps = []
    for core in range(NCORE):
        b = core // 4
        p = core % 4
        xs = np.ascontiguousarray(x[b, p * TPC:(p + 1) * TPC, :].T)
        pmask = np.zeros(NCORE, np.float32)
        for j in range(NCORE):
            if j // 4 == b and (j % 4) < p:
                pmask[j] = 1.0
        m = dict(common)
        m["xin"] = xs
        m["c_in"] = np.ascontiguousarray(cfull[b:b + 1, :])
        m["pmask"] = pmask
        in_maps.append(m)
    return in_maps, (B, T)


def kernel(**inputs):
    T = int(np.asarray(inputs["L0_attn_hidden_states"]).shape[1])
    TPC = 2 * T // NCORE
    nc = build_program(TPC)
    in_maps, (B, T) = _host_prep(inputs, TPC)
    res = run_bass_kernel_spmd(nc, in_maps, core_ids=list(range(NCORE)))
    out = np.zeros((B, T, D), np.float32)
    for core in range(NCORE):
        b = core // 4
        p = core % 4
        out[b, p * TPC:(p + 1) * TPC, :] = res.results[core]["yout"]
    return out


# version 29
# speedup vs baseline: 1.0101x; 1.0101x over previous
import numpy as np
import concourse.bass as bass
import concourse.mybir as mybir
from concourse.bass_utils import run_bass_kernel_spmd
from concourse.bass_core_rust import IndirectOffsetOnAxis

F32 = mybir.dt.float32
I32 = mybir.dt.int32
BIGP = 100000.0
AF = mybir.ActivationFunctionType
ALU = mybir.AluOpType
AX = mybir.AxisListType

D = 1024
H = 8
DK = 128
NCORE = 8
CT = 128
EPS = 1e-5
CLAMP = 85.0
NEG = -1.0


class _Op:
    __slots__ = ("eng", "fn", "deps", "need", "sem", "val", "inc", "guard",
                 "chain")

    def __init__(self, eng, fn, sem, inc):
        self.eng, self.fn, self.sem, self.inc = eng, fn, sem, inc
        self.deps = []
        self.need = False
        self.val = None
        self.guard = None
        self.chain = None


class Tracker:
    ENGS = ("sync", "tensor", "vector", "scalar", "gpsimd")

    def __init__(self):
        self.ops = []
        self.last_w = {}
        self.readers = {}
        self.forced = {}
        self.sem_last = {}
        self.dma_last = {}
        self.ncc = 0
        self.guard_src = None
        self.guard_dep = None
        self.prologue = {}

    def add(self, eng, fn, reads=(), writes=(), dma=False, cc=False, tag=None,
            guard=None):
        if cc:
            sem, inc = f"cc{self.ncc}", 1
            self.ncc += 1
        elif dma:
            sem, inc = "dq_" + (tag or "misc"), 16
        else:
            sem, inc = "c_" + eng, 1
        op = _Op(eng, fn, sem, inc)
        deps = set()
        for k in reads:
            w = self.last_w.get(k)
            if w is not None:
                deps.add(w)
        for k in writes:
            w = self.last_w.get(k)
            if w is not None:
                deps.add(w)
            for r in self.readers.get(k, ()):
                deps.add(r)
        fb = self.forced.get(eng)
        if fb is not None:
            deps.add(fb)
        if dma:
            prev = self.dma_last.get(sem)
            if prev is not None:
                deps.add(prev)
                op.chain = prev
            self.dma_last[sem] = op
        deps.discard(op)
        if eng == "tensor":
            deps = {d for d in deps if d.eng != "tensor"}
        op.deps = list(deps)
        op.guard = guard
        for d in op.deps:
            d.need = True
        for k in reads:
            self.readers.setdefault(k, []).append(op)
        for k in writes:
            self.last_w[k] = op
            self.readers[k] = [op]
        self.ops.append(op)
        self.sem_last[sem] = op
        return op

    def barrier(self):
        snap = list(self.sem_last.values())
        for eng in self.ENGS:
            op = _Op(eng, lambda e: e.nop(), "c_" + eng, 1)
            op.deps = list(snap)
            for d in op.deps:
                d.need = True
            self.ops.append(op)
            self.forced[eng] = op
            self.sem_last[op.sem] = op

    def emit(self, block, handles):
        for op in self.ops:
            if op.inc == 16 or op.sem == "cc":
                op.need = True
        counts = {}
        for op in self.ops:
            if op.need:
                counts[op.sem] = counts.get(op.sem, 0) + op.inc
                op.val = counts[op.sem]
        per_eng = {e: [] for e in self.ENGS}
        for op in self.ops:
            per_eng[op.eng].append(op)

        gsrc, gdep = self.guard_src, self.guard_dep
        waited = {}
        for o in self.ops:
            bysem = {}
            for d in o.deps:
                if d.val is not None:
                    if bysem.get(d.sem, 0) < d.val:
                        bysem[d.sem] = d.val
            for sm, v in bysem.items():
                waited.setdefault(sm, set()).add(v)
        if gdep is not None and gdep.val is not None:
            waited.setdefault(gdep.sem, set()).add(gdep.val)

        def mk(ops, ename):
            def body(eng):
                pro = self.prologue.get(ename)
                if pro is not None:
                    pro(eng)
                hwm = {}
                state = {"g": None, "cm": None, "pend": None, "reg": None,
                         "ge": None}

                def close():
                    if state["cm"] is None:
                        return
                    finals = {}
                    starts = {}
                    kinds = {}
                    for o2 in state["pend"]:
                        if o2.sem not in starts:
                            starts[o2.sem] = o2.val - o2.inc
                            kinds[o2.sem] = (o2.eng == "gpsimd"
                                             and o2.inc == 16)
                            if o2.chain is not None \
                                    and o2.chain.val is not None \
                                    and o2.chain not in state["pend"]:
                                starts[o2.sem] = (o2.val - o2.inc, o2.chain)
                        finals[o2.sem] = o2.val
                    for sm, v in finals.items():
                        eng.wait_ge(handles[sm], v)
                    state["cm"].__exit__(None, None, None)
                    el = eng.Else()
                    el.__enter__()
                    for sm, st0 in starts.items():
                        if isinstance(st0, tuple):
                            st0, ch = st0
                            eng.wait_ge(handles[ch.sem], ch.val)
                        fin = finals[sm]
                        steps = sorted(v for v in waited.get(sm, ())
                                       if st0 < v < fin)
                        prev = st0
                        swdge = kinds[sm]
                        for v in steps + [fin]:
                            if v > prev:
                                if swdge:
                                    eng.inc_swdge_sem([handles[sm]],
                                                      [v - prev])
                                else:
                                    eng.sem_inc(handles[sm], v - prev)
                                prev = v
                    for sm, v in finals.items():
                        eng.wait_ge(handles[sm], v)
                    el.__exit__(None, None, None)
                    state["cm"] = None
                    state["g"] = None
                    state["pend"] = None

                for o in ops:
                    if o.guard != state["g"]:
                        close()
                    if o.guard is not None and state["cm"] is None:
                        if (gdep is not None and gdep.val is not None
                                and hwm.get(gdep.sem, 0) < gdep.val):
                            eng.wait_ge(handles[gdep.sem], gdep.val)
                            hwm[gdep.sem] = gdep.val
                        if state["reg"] is None:
                            state["reg"] = eng.alloc_register(name="nfguard")
                        e_i, thresh = o.guard
                        if state["ge"] != e_i:
                            eng.reg_load(state["reg"], gsrc(e_i))
                            state["ge"] = e_i
                        state["cm"] = eng.If_cmp(state["reg"], thresh, "IS_GT")
                        state["cm"].__enter__()
                        state["g"] = o.guard
                        state["pend"] = []
                    bysem = {}
                    for d in o.deps:
                        if d.val is None:
                            continue
                        if bysem.get(d.sem, 0) < d.val:
                            bysem[d.sem] = d.val
                    for sm, v in bysem.items():
                        if hwm.get(sm, 0) >= v:
                            continue
                        eng.wait_ge(handles[sm], v)
                        if state["cm"] is None:
                            hwm[sm] = v
                    ins = o.fn(eng)
                    if o.need:
                        if ins is None:
                            ins = eng.nop()
                        ins.then_inc(handles[o.sem], o.inc)
                        if state["cm"] is not None:
                            state["pend"].append(o)
                close()
            return body

        for e in self.ENGS:
            if per_eng[e]:
                getattr(block, e)(mk(per_eng[e], e))
        return counts


class Arena:
    def __init__(self, t, total):
        self.t, self.total = t, total
        self.off = 0
        self.marks = []

    def alloc(self, n):
        a = self.off
        self.off += n
        assert self.off <= self.total, f"arena overflow {self.off} > {self.total}"
        return self.t[:, a:a + n]

    def mark(self):
        self.marks.append(self.off)

    def release(self):
        self.off = self.marks.pop()


def build_program(TPC):
    TILE = 512
    SPT = TILE // CT
    NT = TPC // TILE
    assert TPC % TILE == 0

    nc = bass.Bass("TRN2", target_bir_lowering=False)

    def din(name, shape):
        return nc.dram_tensor(name, list(shape), F32, kind="ExternalInput").ap()

    def dint(name, shape):
        return nc.dram_tensor(name, list(shape), F32).ap()

    xin = din("xin", [D, TPC])
    wq = din("wq", [D, D]); wk = din("wk", [D, D])
    wv = din("wv", [D, D]); wr = din("wr", [D, D])
    wa_d = din("wa", [D, H])
    wo_d = din("wo", [D, D])
    a_log = din("a_log", [H]); dt_b = din("dt_b", [H])
    qnw_d = din("qnw", [DK]); knw_d = din("knw", [DK]); gnw_d = din("gnw", [D])
    gate_wt = din("gate_wt", [D, H])
    lb_b = din("lb_b", [H])
    egwt = din("egwt", [8, D, 512]); euwt = din("euwt", [8, D, 512])
    edwt = din("edwt", [8, 512, D])
    sgwt = din("sgwt", [D, D]); suwt = din("suwt", [D, D]); sdwt = din("sdwt", [D, D])
    c_in = din("c_in", [1, D])
    wmod = din("wmod", [D, 6 * D])
    ident_d = din("ident", [128, 128])
    tri_d = din("tri", [128, 128])
    tri4_d = din("tri4", [128, 512])
    colsel_d = din("colsel", [128, 2])
    ones_d = din("onesr", [1, 512])
    negr_d = din("negr", [1, 512])
    pmask_d = din("pmask", [NCORE])
    t32_d = din("t32", [32, 32])
    iotac_d = din("iotac", [128, 1])

    NSUB = TPC // CT
    NBLK = TPC // 512
    CAPE = TPC + 512
    DW = D + 1
    o_sp = dint("o_sp", [H, DK, TPC])
    qc_sp = dint("qc_sp", [H, DK, TPC])
    x1_sp = dint("x1_sp", [8, 128, TPC])
    h2_sp = dint("h2_sp", [8, 128, TPC])
    h2tok_d = dint("h2tok", [TPC, DW])
    compact_d = [dint(f"compact{i}", [CAPE, DW]) for i in range(8)]
    acc_tok_d = dint("acc_tok", [TPC, D])
    rowx_d = dint("rowx", [8 * NSUB])
    rowx2_d = dint("rowx2", [8 * NSUB])
    modb_d = dint("modb", [6 * D])
    ag_in = dint("ag_in", [H * DK * DK + H * DK])
    ag_out = dint("ag_out", [NCORE * (H * DK * DK + H * DK)])
    ar_in = dint("ar_in", [16])
    ar_out = dint("ar_out", [16])
    wm_d = dint("wm_d", [8])
    yout = nc.dram_tensor("yout", [TPC, D], F32, kind="ExternalOutput").ap()

    tk = Tracker()
    from contextlib import ExitStack

    with ExitStack() as ctx:
        ARENA_F = 52850
        arena_t = ctx.enter_context(nc.sbuf_tensor("arena", [128, ARENA_F], F32))
        ar = Arena(arena_t, ARENA_F)
        A = ar.alloc
        idxi = ctx.enter_context(nc.sbuf_tensor("idxi", [128, 8 * NSUB], I32))
        idxt = ctx.enter_context(nc.sbuf_tensor("idxt", [128, 32], I32))
        nfi8 = ctx.enter_context(nc.sbuf_tensor("nfi8", [1, 8], I32))
        invi = ctx.enter_context(nc.sbuf_tensor("invi", [128, 8], I32))
        bigi = ctx.enter_context(nc.sbuf_tensor("bigi", [128, 4], I32))

        ident = A(128); tri = A(128); tri4 = A(512); colsel = A(2)
        onesr = A(512); negr = A(512)
        modc = A(48)
        qknw = A(2); gnwc = A(8)
        naec = A(8)
        rowsc = A(24)
        pmb = A(8)
        S_loc = A(H * DK); S_in = A(H * DK)
        cumD = A(H); cumc = A(H)
        wm2 = A(2)
        modrow = A(6 * D)
        wa_s = A(8 * H)
        gw_s = A(8 * H)
        prow = A(3 * 512)
        g16 = A(1)
        t32 = A(32)
        iotac = A(1)
        selcols = A(8 * NSUB)
        wmcol = A(8)

        cvals = (EPS, 1.0, -0.5, -1.0)
        cbuf = A(len(cvals))
        pb = [ctx.enter_context(nc.psum_tensor(f"pb{i}", [128, 512], F32))
              for i in range(8)]
        KPB = [f"pb{i}" for i in range(8)]

        def dma(eng, out, in_, rk, wk_, slow=False):
            tk.add(eng, lambda e, o=out, i=in_, s=slow: e.dma_start(
                out=o, in_=i, allow_slow_non_contiguous=s),
                   reads=rk, writes=wk_, dma=True, tag=wk_[0])

        def dma_acc(out, in_, rk, wk_):
            tk.add("gpsimd", lambda e, o=out, i=in_: e.dma_start(
                out=o, in_=i, accum_op=ALU.add), reads=rk, writes=wk_, dma=True,
                tag=wk_[0])

        def op(eng, fn, rk, wk_):
            tk.add(eng, fn, reads=rk, writes=wk_)

        def modrow_sl(w, c):
            return modrow[0:1, w * D + c * 128: w * D + (c + 1) * 128]

        def modcol(w, c):
            return modc[:, w * 8 + c:w * 8 + c + 1]

        for ci, cv in enumerate(cvals):
            op("vector", lambda e, ci=ci, cv=cv: e.memset(cbuf[:, ci:ci + 1], cv),
               [], ["cbuf"])
            nc.const_aps.register(F32, cv, cbuf[:, ci:ci + 1])
        dma("sync", ident, ident_d, ["i_d"], ["ident"])
        dma("sync", tri, tri_d, ["t_d"], ["tri"])
        dma("sync", tri4, tri4_d, ["t4_d"], ["tri4"])
        dma("sync", colsel, colsel_d, ["cs_d"], ["colsel"])
        dma("sync", onesr[0:1, :], ones_d, ["o_d"], ["onesr"])
        dma("sync", negr[0:1, :], negr_d, ["n_d"], ["negr"])
        dma("sync", qknw[:, 0:1], qnw_d[:, None], ["qn_d"], ["qknw"])
        dma("sync", qknw[:, 1:2], knw_d[:, None], ["kn_d"], ["qknw"])
        dma("sync", gnwc[:, :, None],
            gnw_d.rearrange("(c p) -> p c", p=128)[:, :, None], ["gn_d"], ["gnwc"],
            slow=True)
        dma("sync", pmb, pmask_d.partition_broadcast(128), ["pm_d"], ["pmb"])
        dma("sync", t32[0:32, :], t32_d, ["t32_d"], ["t32"])
        dma("sync", iotac, iotac_d, ["io_d"], ["iotac"])
        dma("sync", rowsc[0:1, 0:8], a_log[None, :], ["al_d"], ["rowsc"])
        dma("sync", rowsc[0:1, 8:16], dt_b[None, :], ["dt_d"], ["rowsc"])
        dma("sync", rowsc[0:1, 16:24], lb_b[None, :], ["lb_d"], ["rowsc"])
        op("scalar", lambda e: e.activation(rowsc[0:1, 0:8], rowsc[0:1, 0:8], AF.Exp),
           ["rowsc"], ["rowsc"])
        op("vector", lambda e: e.tensor_scalar_mul(rowsc[0:1, 0:8],
                                                   rowsc[0:1, 0:8], -1.0),
           ["rowsc"], ["rowsc"])
        op("tensor", lambda e: e.matmul(pb[0][:, 0:8], onesr[0:1, 0:128],
                                        rowsc[0:1, 0:8], start=True, stop=True),
           ["onesr", "rowsc"], [KPB[0]])
        op("vector", lambda e: e.tensor_copy(naec, pb[0][:, 0:8]), [KPB[0]], ["naec"])
        op("vector", lambda e: e.memset(S_loc, 0.0), [], ["S_loc"])
        op("vector", lambda e: e.memset(S_in, 0.0), [], ["S_in"])
        op("vector", lambda e: e.memset(cumD, 1.0), [], ["cumD"])
        dma("sync", wa_s.rearrange("p (c hh) -> p c hh", hh=H),
            wa_d.rearrange("(c p) hh -> p c hh", p=128), ["wa_d"], ["wa_s"])
        dma("sync", gw_s.rearrange("p (c hh) -> p c hh", hh=H),
            gate_wt.rearrange("(c p) hh -> p c hh", p=128), ["gwt_d"], ["gw_s"])

        ar.mark()
        csil = A(8)
        wstr = [A(8 * 512) for _ in range(2)]
        dma("sync", csil[:, :, None],
            c_in.rearrange("one (c p) -> p c", p=128)[:, :, None], ["c_d"], ["csil"],
            slow=True)
        op("scalar", lambda e: e.activation(csil, csil, AF.Silu), ["csil"], ["csil"])
        for og in range(12):
            par = og % 2
            wb = wstr[par]
            wkk = f"wstr{par}"
            dma("sync", wb.rearrange("p (c t) -> p c t", t=512),
                wmod[:, og * 512:(og + 1) * 512].rearrange("(c p) t -> p c t", p=128),
                [], [wkk])
            bank = pb[4 + par]
            for c in range(8):
                op("tensor", lambda e, c=c, wb=wb, bank=bank, f=(c == 0), l=(c == 7):
                   e.matmul(bank[0:1, 0:512], csil[:, c:c + 1],
                            wb.rearrange("p (c t) -> p c t", t=512)[:, c],
                            start=f, stop=l), ["csil", wkk], [KPB[4 + par]])
            op("vector", lambda e, og=og, bank=bank: e.tensor_copy(
                modrow[0:1, og * 512:(og + 1) * 512], bank[0:1, 0:512]),
               [KPB[4 + par]], ["modrow"])
        for w in (1, 4):
            sl = modrow[0:1, w * D:(w + 1) * D]
            op("vector", lambda e, sl=sl: e.tensor_scalar_add(sl, sl, 1.0),
               ["modrow"], ["modrow"])
        sgn = A(D)
        for w in (2, 5):
            sl = modrow[0:1, w * D:(w + 1) * D]
            m_ = sgn[0:1, 0:D]
            op("vector", lambda e, sl=sl, m_=m_: e.tensor_scalar(
                m_, sl, 0.0, None, op0=ALU.is_ge), ["modrow"], ["sgn"])
            op("vector", lambda e, m_=m_: e.tensor_scalar(
                m_, m_, 2.0, -1.0, op0=ALU.mult, op1=ALU.add), ["sgn"], ["sgn"])
            op("scalar", lambda e, sl=sl: e.activation(sl, sl, AF.Abs),
               ["modrow"], ["modrow"])
            op("vector", lambda e, sl=sl: e.tensor_scalar_max(sl, sl, 0.25),
               ["modrow"], ["modrow"])
            op("vector", lambda e, sl=sl, m_=m_: e.tensor_mul(sl, sl, m_),
               ["modrow", "sgn"], ["modrow"])
        dma("scalar", modb_d[None, :], modrow[0:1, :], ["modrow"], ["modb"])
        dma("sync", modc.rearrange("p (w c) -> p w c", c=8),
            modb_d.rearrange("(w c p) -> p w c", p=128, c=8), ["modb"], ["modc"],
            slow=True)
        ar.release()

        tk.barrier()

        ar.mark()
        xt = [A(8 * CT) for _ in range(2)]
        ht = A(8 * TILE)
        wst = [A(512) for _ in range(2)]
        rbuf = A(D)
        q_tok = A(SPT * D)
        k_tok = A(SPT * D)
        v_tok = A(SPT * D)
        r_tok = A(SPT * D)
        f_t = [A(D) for _ in range(SPT)]
        lam = A(D)
        sqsc = A(D)
        sqsc2 = A(2 * CT)
        acol = A(SPT * H)
        fhead = A(SPT * H)
        rqk = A(SPT * 2 * H)
        mx8 = A(8)
        ztop = A(128); wsel = A(128)
        wsum = A(H)
        ppt_s = A(D)
        eqs_s = A(D)
        qfm_s = A(D)
        kfm_s = A(D)
        ams_s = A(D)
        stv_s = A(D)
        dex = A(24)
        gkn = A(8); qcf = A(8)
        dstage = A(16)
        o_sub = A(H * CT)
        qc_sub = A(H * CT)

        WLIST = ((wr, "r", None, None, AF.Sigmoid),
                 (wq, "q", q_tok, "q_tok", AF.Silu),
                 (wk, "k", k_tok, "k_tok", AF.Silu),
                 (wv, "v", v_tok, "v_tok", AF.Silu))

        hvw = ht.rearrange("p (c tt) -> p c tt", tt=TILE)

        def ln1(t2, s):
            tok2 = t2 * TILE
            xpar = (t2 * SPT + s) % 2
            xb = xt[xpar]
            xk = f"xt{xpar}"
            dma("sync", xb.rearrange("p (c t2) -> p c t2", t2=CT),
                xin[:, tok2 + s * CT: tok2 + (s + 1) * CT]
                .rearrange("(c p) t2 -> p c t2", p=128), [], [xk])
            xv = xb.rearrange("p (c t2) -> p c t2", t2=CT)
            for c in range(8):
                op("tensor", lambda e, c=c, xv=xv, f=(c == 0), l=(c == 7):
                   e.matmul(pb[2][0:1, 0:CT], colsel[:, 0:1], xv[:, c],
                            start=f, stop=l), [xk, "colsel"], [KPB[2]])
            for c in range(8):
                cp2 = c % 2
                op("scalar", lambda e, c=c, xv=xv, cp2=cp2: e.activation(
                    sqsc2[:, cp2 * CT:(cp2 + 1) * CT], xv[:, c], AF.Square),
                   [xk], [f"sq2{cp2}"])
                op("tensor", lambda e, c=c, cp2=cp2, f=(c == 0), l=(c == 7):
                   e.matmul(pb[3][0:1, 0:CT], colsel[:, 0:1],
                            sqsc2[:, cp2 * CT:(cp2 + 1) * CT], start=f, stop=l),
                   [f"sq2{cp2}", "colsel"], [KPB[3]])
            mrow = prow[0:1, 0:CT]
            rrow = prow[0:1, 512:512 + CT]
            mrr = prow[0:1, 1024:1024 + CT]
            op("vector", lambda e, mrow=mrow: e.tensor_scalar_mul(
                mrow, pb[2][0:1, 0:CT], 1.0 / D), [KPB[2]], ["prow"])
            op("vector", lambda e, rrow=rrow, mrow=mrow: e.tensor_mul(
                rrow, mrow, mrow), ["prow"], ["prow"])
            op("vector", lambda e, rrow=rrow: e.scalar_tensor_tensor(
                rrow, pb[3][0:1, 0:CT], 1.0 / D, rrow,
                op0=ALU.mult, op1=ALU.subtract), [KPB[3], "prow"], ["prow"])
            op("scalar", lambda e, rrow=rrow: e.activation(
                rrow, rrow, AF.Ln, bias=EPS), ["prow"], ["prow"])
            op("scalar", lambda e, rrow=rrow: e.activation(
                rrow, rrow, AF.Exp, scale=-0.5), ["prow"], ["prow"])
            op("vector", lambda e, mrr=mrr, mrow=mrow, rrow=rrow: e.tensor_mul(
                mrr, mrow, rrow), ["prow"], ["prow"])
            for c in range(8):
                op("tensor", lambda e, c=c, rrow=rrow: e.matmul(
                    pb[2][:, 0:CT], modrow_sl(1, c), rrow, start=True, stop=True),
                   ["modrow", "prow"], [KPB[2]])
                op("tensor", lambda e, c=c, mrr=mrr: e.matmul(
                    pb[3][:, 0:CT], modrow_sl(1, c), mrr, start=True, stop=False),
                   ["modrow", "prow"], [KPB[3]])
                op("tensor", lambda e, c=c: e.matmul(
                    pb[3][:, 0:CT], modrow_sl(0, c), negr[0:1, 0:CT],
                    start=False, stop=True), ["modrow", "negr"], [KPB[3]])
                hc = hvw[:, c, s * CT:(s + 1) * CT]
                op("vector", lambda e, hc=hc, xv=xv, c=c: e.tensor_mul(
                    hc, xv[:, c], pb[2][:, 0:CT]), [xk, KPB[2]], ["ht"])
                op("vector", lambda e, hc=hc: e.tensor_sub(
                    hc, hc, pb[3][:, 0:CT]), ["ht", KPB[3]], ["ht"])

        for s in range(SPT):
            ln1(0, s)

        for t in range(NT):
            tok0 = t * TILE
            rbufk = "r_tok"

            def proj_one(wd, dstt, dkey, func):
                for half in range(2):
                    for c in range(8):
                        par = c % 2
                        wb = wst[par]
                        wkk = f"wst{par}"
                        dma("sync", wb,
                            wd[c * 128:(c + 1) * 128, half * 512:(half + 1) * 512],
                            [], [wkk])
                        for s in range(SPT):
                            op("tensor",
                               lambda e, c=c, s=s, wb=wb, f=(c == 0), l=(c == 7):
                               e.matmul(pb[4 + s][:, 0:512],
                                        hvw[:, c, s * CT:(s + 1) * CT], wb,
                                        start=f, stop=l),
                               ["ht", wkk], [KPB[4 + s]])
                    for s in range(SPT):
                        dv = dstt.rearrange("p (s f) -> p s f", f=D)[
                            :, s, half * 512:(half + 1) * 512]
                        op("scalar", lambda e, dv=dv, s=s, func=func:
                           e.activation(dv, pb[4 + s][:, 0:512], func),
                           [KPB[4 + s]], [dkey])

            proj_one(wr, r_tok, rbufk, AF.Sigmoid)

            for s in range(SPT):
                for c in range(8):
                    op("tensor", lambda e, c=c, s=s, f=(c == 0):
                       e.matmul(pb[2][:, 0:H], hvw[:, c, s * CT:(s + 1) * CT],
                                wa_s.rearrange("p (c hh) -> p c hh", hh=H)[:, c],
                                start=f, stop=False), ["ht", "wa_s"], [KPB[2]])
                op("tensor", lambda e: e.matmul(
                    pb[2][:, 0:H], onesr[0:1, 0:128], rowsc[0:1, 8:16],
                    start=False, stop=True), ["onesr", "rowsc"], [KPB[2]])
                op("vector", lambda e, s=s: e.tensor_copy(
                    acol[:, s * H:(s + 1) * H], pb[2][:, 0:H]), [KPB[2]], ["acol"])

            nsh = SPT * H
            op("scalar", lambda e: e.activation(fhead[:, 0:nsh], acol, AF.Abs),
               ["acol"], ["fhead"])
            op("scalar", lambda e: e.activation(fhead[:, 0:nsh], fhead[:, 0:nsh],
                                                AF.Exp, scale=-1.0),
               ["fhead"], ["fhead"])
            op("scalar", lambda e: e.activation(fhead[:, 0:nsh], fhead[:, 0:nsh],
                                                AF.Ln, bias=1.0), ["fhead"], ["fhead"])
            op("vector", lambda e: e.tensor_scalar_max(sqsc[:, 0:nsh], acol, 0.0),
               ["acol"], ["sqsc"])
            op("vector", lambda e: e.tensor_add(fhead[:, 0:nsh], fhead[:, 0:nsh],
                                                sqsc[:, 0:nsh]),
               ["fhead", "sqsc"], ["fhead"])
            op("vector", lambda e: e.tensor_tensor(
                fhead[:, 0:nsh].rearrange("p (s hh) -> p s hh", hh=H),
                fhead[:, 0:nsh].rearrange("p (s hh) -> p s hh", hh=H),
                naec.to_broadcast([128, H, SPT]).rearrange("p hh s -> p s hh"),
                ALU.mult), ["fhead", "naec"], ["fhead"])

            for s in range(SPT):
                fb = f_t[s]
                fbk = f"f_t{s}"
                for hh in range(H):
                    sc_sl = r_tok.rearrange("p (s f) -> p s f", f=D)[
                        :, s, hh * DK:(hh + 1) * DK]
                    op("vector", lambda e, sc_sl=sc_sl: e.tensor_copy(ztop, sc_sl),
                       [rbufk], ["ztop"])
                    for _ in range(4):
                        op("vector", lambda e: e.max(mx8, ztop), ["ztop"], ["mx8"])
                        op("vector", lambda e: e.match_replace(ztop, mx8, ztop, NEG),
                           ["ztop", "mx8"], ["ztop"])
                    op("vector", lambda e, sc_sl=sc_sl: e.tensor_tensor(
                        wsel, sc_sl, ztop, ALU.is_gt), [rbufk, "ztop"], ["wsel"])
                    op("vector", lambda e, sc_sl=sc_sl: e.tensor_mul(
                        wsel, wsel, sc_sl), ["wsel", rbufk], ["wsel"])
                    wcol = wsum[:, hh:hh + 1]
                    op("vector", lambda e, wcol=wcol: e.reduce_sum(
                        wcol, wsel, axis=AX.X), ["wsel"], ["wsum"])
                    op("vector", lambda e, wcol=wcol: e.tensor_scalar_add(
                        wcol, wcol, 1e-9), ["wsum"], ["wsum"])
                    op("vector", lambda e, wcol=wcol: e.reciprocal(wcol, wcol),
                       ["wsum"], ["wsum"])
                    fsl = fb[:, hh * DK:(hh + 1) * DK]
                    op("vector", lambda e, fsl=fsl, wcol=wcol:
                       e.tensor_scalar(fsl, wsel, wcol, None, op0=ALU.mult),
                       ["wsel", "wsum"], [fbk])

            proj_one(wq, q_tok, "q_tok", AF.Silu)
            proj_one(wk, k_tok, "k_tok", AF.Silu)
            proj_one(wv, v_tok, "v_tok", AF.Silu)

            for s in range(SPT):
                qv = q_tok.rearrange("p (s f) -> p s f", f=D)[:, s]
                kv = k_tok.rearrange("p (s f) -> p s f", f=D)[:, s]
                op("scalar", lambda e, qv=qv: e.activation(sqsc[:, 0:D], qv, AF.Square),
                   ["q_tok"], ["sqsc"])
                op("vector", lambda e, s=s: e.reduce_sum(
                    rqk[:, s * 16:s * 16 + 8],
                    sqsc[:, 0:D].rearrange("p (hh kk) -> p hh kk", kk=DK),
                    axis=AX.X), ["sqsc"], ["rqk"])
                op("scalar", lambda e, kv=kv: e.activation(sqsc[:, 0:D], kv, AF.Square),
                   ["k_tok"], ["sqsc"])
                op("vector", lambda e, s=s: e.reduce_sum(
                    rqk[:, s * 16 + 8:s * 16 + 16],
                    sqsc[:, 0:D].rearrange("p (hh kk) -> p hh kk", kk=DK),
                    axis=AX.X), ["sqsc"], ["rqk"])

            for s in range(SPT):
                sl = rqk[:, s * 16:(s + 1) * 16]
                op("vector", lambda e, sl=sl: e.tensor_scalar(
                    sl, sl, 1.0 / DK, EPS, op0=ALU.mult, op1=ALU.add),
                   ["rqk"], ["rqk"])
                op("scalar", lambda e, sl=sl: e.activation(sl, sl, AF.Ln),
                   ["rqk"], ["rqk"])
                op("scalar", lambda e, sl=sl: e.activation(sl, sl, AF.Exp, scale=-0.5),
                   ["rqk"], ["rqk"])
            for s in range(SPT):
                qv = q_tok.rearrange("p (s f) -> p s f", f=D)[:, s].rearrange(
                    "p (hh kk) -> p hh kk", kk=DK)
                kv = k_tok.rearrange("p (s f) -> p s f", f=D)[:, s].rearrange(
                    "p (hh kk) -> p hh kk", kk=DK)
                rq = rqk[:, s * 16:s * 16 + 8]
                rk = rqk[:, s * 16 + 8:s * 16 + 16]
                op("vector", lambda e, qv=qv, rq=rq: e.tensor_tensor(
                    qv, qv, rq.to_broadcast([128, 8, DK]), ALU.mult),
                   ["q_tok", "rqk"], ["q_tok"])
                op("vector", lambda e, kv=kv, rk=rk: e.tensor_tensor(
                    kv, kv, rk.to_broadcast([128, 8, DK]), ALU.mult),
                   ["k_tok", "rqk"], ["k_tok"])

            for s in range(SPT):
                fb = f_t[s]
                fbk = f"f_t{s}"
                op("vector", lambda e, fb=fb, s=s: e.tensor_tensor(
                    fb.rearrange("p (hh kk) -> p hh kk", kk=DK),
                    fb.rearrange("p (hh kk) -> p hh kk", kk=DK),
                    fhead[:, s * H:(s + 1) * H].to_broadcast([128, H, DK]),
                    ALU.mult), [fbk, "fhead"], [fbk])
                op("scalar", lambda e, fb=fb: e.activation(
                    lam[:, 0:D], fb, AF.Exp), [fbk], ["lam"])
                op("vector", lambda e: e.tensor_scalar(
                    lam[:, 0:D], lam[:, 0:D], -1.0, 1.0, op0=ALU.mult, op1=ALU.add),
                   ["lam"], ["lam"])
                kv = k_tok.rearrange("p (s f) -> p s f", f=D)[:, s]
                op("vector", lambda e, kv=kv: e.tensor_mul(kv, kv, lam[:, 0:D]),
                   ["k_tok", "lam"], ["k_tok"])
                sp = s % 2
                bX, bY = pb[4 + 2 * sp], pb[5 + 2 * sp]
                kX, kY = KPB[4 + 2 * sp], KPB[5 + 2 * sp]
                fv = fb.rearrange("p (hh kk) -> p hh kk", kk=DK)
                q_s = q_tok.rearrange("p (s f) -> p s f", f=D)[:, s]
                k_s = k_tok.rearrange("p (s f) -> p s f", f=D)[:, s]
                v_s = v_tok.rearrange("p (s f) -> p s f", f=D)[:, s]
                v_sv = v_s.rearrange("p (hh kk) -> p hh kk", kk=DK)
                for half, (bk, kk) in enumerate(((bX, kX), (bY, kY))):
                    op("tensor", lambda e, bk=bk, half=half, fb=fb: e.matmul(
                        bk[0:1, 0:512], colsel[:, 1:2],
                        fb[:, half * 512:(half + 1) * 512],
                        start=True, stop=True), [fbk, "colsel"], [kk])
                    op("vector", lambda e, bk=bk, half=half: e.tensor_copy(
                        rbuf[0:1, half * 512:(half + 1) * 512], bk[0:1, 0:512]),
                       [kk], ["rbuf"])
                for hh in range(H):
                    op("tensor", lambda e, hh=hh, fv=fv, bX=bX: e.matmul(
                        bX[:, hh * 2:hh * 2 + 2], fv[:, hh], colsel,
                        start=(hh == 0), stop=(hh == 7)), [fbk, "colsel"], [kX])
                op("vector", lambda e, bX=bX: e.tensor_copy(dstage, bX[:, 0:16]),
                   [kX], ["dstage"])
                ds2 = dstage.rearrange("p (hh two) -> p hh two", two=2)
                op("vector", lambda e: e.tensor_copy(dex[:, 0:8], ds2[:, :, 0]),
                   ["dstage"], ["dex"])
                op("vector", lambda e: e.tensor_copy(dex[:, 16:24], ds2[:, :, 1]),
                   ["dstage"], ["dex"])
                op("vector", lambda e: e.tensor_sub(dex[:, 8:16], dex[:, 0:8],
                                                    dex[:, 16:24]), ["dex"], ["dex"])
                op("scalar", lambda e: e.activation(dex[:, 0:24], dex[:, 0:24],
                                                    AF.Exp), ["dex"], ["dex"])
                op("vector", lambda e: e.tensor_scalar(
                    gkn, dex[:, 8:16], qknw[:, 1:2], None, op0=ALU.mult),
                   ["dex", "qknw"], ["gkn"])
                op("vector", lambda e: e.tensor_mul(qcf, dex[:, 16:24], cumD),
                   ["dex", "cumD"], ["qcf"])
                for half, (bk, kk) in enumerate(((bX, kX), (bY, kY))):
                    op("tensor", lambda e, bk=bk, half=half, fb=fb:
                       e.matmul(bk[:, 0:512], tri,
                                fb[:, half * 512:(half + 1) * 512],
                                start=True, stop=False), [fbk, "tri"], [kk])
                    op("tensor", lambda e, bk=bk, half=half:
                       e.matmul(bk[:, 0:512], negr[0:1, 0:128],
                                rbuf[0:1, half * 512:(half + 1) * 512],
                                start=False, stop=True), ["negr", "rbuf"], [kk])
                    op("vector", lambda e, bk=bk, half=half: e.tensor_scalar(
                        ppt_s[:, half * 512:(half + 1) * 512], bk[:, 0:512],
                        CLAMP, -CLAMP, op0=ALU.min, op1=ALU.max), [kk], ["ppt_s"])
                op("scalar", lambda e: e.activation(eqs_s, ppt_s, AF.Exp),
                   ["ppt_s"], ["eqs_s"])
                op("scalar", lambda e: e.activation(ppt_s, ppt_s, AF.Exp, scale=-1.0),
                   ["ppt_s"], ["ppt_s"])
                op("vector", lambda e, q_s=q_s: e.tensor_mul(eqs_s, q_s, eqs_s),
                   ["q_tok", "eqs_s"], ["eqs_s"])
                op("vector", lambda e, k_s=k_s: e.tensor_mul(ppt_s, k_s, ppt_s),
                   ["k_tok", "ppt_s"], ["ppt_s"])
                for src_t, srck, dst_t, dstk, wcol in (
                        (eqs_s, "eqs_s", qfm_s, "qfm_s", qknw[:, 0:1]),
                        (ppt_s, "ppt_s", kfm_s, "kfm_s", qknw[:, 1:2])):
                    for half, (bk, kk) in enumerate(((bX, kX), (bY, kY))):
                        for j in range(4):
                            hh = half * 4 + j
                            op("tensor", lambda e, bk=bk, j=j, hh=hh, src_t=src_t,
                               f=(j == 0), l=(j == 3):
                               e.matmul(bk[:, j * 128:(j + 1) * 128],
                                        src_t[:, hh * 128:(hh + 1) * 128],
                                        ident, is_transpose=True,
                                        start=f, stop=l), [srck, "ident"], [kk])
                        op("scalar", lambda e, bk=bk, half=half, dst_t=dst_t,
                           wcol=wcol: e.activation(
                               dst_t[:, half * 512:(half + 1) * 512], bk[:, 0:512],
                               AF.Copy, scale=wcol), [kk, "qknw"], [dstk])
                for half, (bk, kk) in enumerate(((bX, kX), (bY, kY))):
                    for j in range(4):
                        hh = half * 4 + j
                        op("tensor", lambda e, bk=bk, j=j, hh=hh, f=(j == 0),
                           l=(j == 3): e.matmul(
                               bk[:, j * 128:(j + 1) * 128],
                               kfm_s[:, hh * 128:(hh + 1) * 128],
                               qfm_s[:, hh * 128:(hh + 1) * 128],
                               start=f, stop=l), ["kfm_s", "qfm_s"], [kk])
                    op("vector", lambda e, bk=bk, half=half: e.tensor_mul(
                        ams_s[:, half * 512:(half + 1) * 512], bk[:, 0:512], tri4),
                       [kk, "tri4"], ["ams_s"])
                op("vector", lambda e: e.tensor_tensor(
                    stv_s.rearrange("p (hh vv) -> p hh vv", vv=DK),
                    S_loc.rearrange("p (hh vv) -> p hh vv", vv=DK),
                    dex[:, 16:24].to_broadcast([128, 8, DK]), ALU.mult),
                   ["S_loc", "dex"], ["stv_s"])
                for half, (bk, kk) in enumerate(((bX, kX), (bY, kY))):
                    for j in range(4):
                        hh = half * 4 + j
                        op("tensor", lambda e, bk=bk, j=j, hh=hh, v_sv=v_sv,
                           f=(j == 0):
                           e.matmul(bk[:, j * 128:(j + 1) * 128], v_sv[:, hh],
                                    ams_s[:, hh * 128:(hh + 1) * 128],
                                    start=f, stop=False),
                           ["v_tok", "ams_s"], [kk])
                        op("tensor", lambda e, bk=bk, j=j, hh=hh, l=(j == 3):
                           e.matmul(bk[:, j * 128:(j + 1) * 128],
                                    stv_s[:, hh * 128:(hh + 1) * 128],
                                    qfm_s[:, hh * 128:(hh + 1) * 128],
                                    start=False, stop=l),
                           ["stv_s", "qfm_s"], [kk])
                    eng_e = "vector" if half == 0 else "scalar"
                    if half == 0:
                        op("vector", lambda e, bk=bk: e.tensor_copy(
                            o_sub[:, 0:512], bk[:, 0:512]), [kk], ["o_sub"])
                    else:
                        op("scalar", lambda e, bk=bk: e.copy(
                            o_sub[:, 512:1024], bk[:, 0:512]), [kk], ["o_sub"])
                for half, (bk, kk) in enumerate(((bX, kX), (bY, kY))):
                    for j in range(4):
                        hh = half * 4 + j
                        op("tensor", lambda e, bk=bk, j=j, hh=hh, v_sv=v_sv,
                           f=(j == 0),
                           l=(j == 3): e.matmul(
                               bk[:, j * 128:(j + 1) * 128],
                               ppt_s[:, hh * 128:(hh + 1) * 128], v_sv[:, hh],
                               start=f, stop=l), ["ppt_s", "v_tok"], [kk])
                    op("vector", lambda e, bk=bk, half=half: e.tensor_tensor(
                        stv_s.rearrange("p (hh vv) -> p hh vv", vv=DK)[
                            :, half * 4:(half + 1) * 4],
                        bk[:, 0:512].rearrange("p (j vv) -> p j vv", vv=DK),
                        gkn[:, half * 4:(half + 1) * 4].to_broadcast([128, 4, DK]),
                        ALU.mult), [kk, "gkn"], ["stv_s"])
                op("vector", lambda e: e.tensor_tensor(
                    qc_sub.rearrange("p (hh i) -> p hh i", i=CT),
                    qfm_s.rearrange("p (hh i) -> p hh i", i=CT),
                    qcf.to_broadcast([128, 8, CT]), ALU.mult),
                   ["qfm_s", "qcf"], ["qc_sub"])
                op("vector", lambda e: e.tensor_tensor(
                    S_loc.rearrange("p (hh vv) -> p hh vv", vv=DK),
                    S_loc.rearrange("p (hh vv) -> p hh vv", vv=DK),
                    dex[:, 0:8].to_broadcast([128, 8, DK]), ALU.mult),
                   ["S_loc", "dex"], ["S_loc"])
                op("vector", lambda e: e.tensor_add(S_loc, S_loc, stv_s),
                   ["S_loc", "stv_s"], ["S_loc"])
                op("vector", lambda e: e.tensor_mul(cumD, cumD, dex[:, 0:8]),
                   ["cumD", "dex"], ["cumD"])
                sub0 = tok0 + s * CT
                dma("scalar", o_sp.rearrange("hh v tt -> v hh tt")[:, :, sub0:sub0 + CT],
                    o_sub.rearrange("p (hh i) -> p hh i", i=CT), ["o_sub"], ["o_sp_d"])
                dma("scalar", qc_sp.rearrange("hh k tt -> k hh tt")[:, :, sub0:sub0 + CT],
                    qc_sub.rearrange("p (hh i) -> p hh i", i=CT), ["qc_sub"], ["qc_sp_d"])
                if t + 1 < NT:
                    ln1(t + 1, s)

        dma("gpsimd", ag_in[0:H * DK * DK].rearrange("(hh k c) -> k hh c",
                                                      k=DK, c=DK),
            S_loc.rearrange("p (hh vv) -> p hh vv", vv=DK), ["S_loc"], ["ag_in_d"])
        dma("gpsimd", ag_in[H * DK * DK:].rearrange("(k hh) -> k hh", hh=H),
            cumD, ["cumD"], ["ag_in_d"])
        tk.add("gpsimd", lambda e: e.collective_compute(
            "AllGather", ALU.bypass, replica_groups=[list(range(NCORE))],
            ins=[ag_in], outs=[ag_out]),
            reads=["ag_in_d"], writes=["ag_out_d"], cc=True)
        ar.mark()
        gS = A(H * DK)
        gD = A(H)
        op("vector", lambda e: e.memset(cumc, 1.0), [], ["cumc"])
        AGW = H * DK * DK + H * DK
        for j in range(NCORE - 1, -1, -1):
            base = j * AGW
            dma("sync", gS.rearrange("p (hh vv) -> p hh vv", vv=DK),
                ag_out[base:base + H * DK * DK]
                .rearrange("(hh k c) -> k hh c", k=DK, c=DK),
                ["ag_out_d"], ["gS"])
            dma("sync", gD,
                ag_out[base + H * DK * DK:base + AGW]
                .rearrange("(k hh) -> k hh", hh=H), ["ag_out_d"], ["gD"])
            pmj = pmb[:, j:j + 1]
            op("vector", lambda e, pmj=pmj: e.tensor_scalar(
                stv_s[:, 0:H], cumc[:, 0:H], pmj, None, op0=ALU.mult),
               ["pmb", "cumc"], ["stv_s"])
            op("vector", lambda e: e.tensor_tensor(
                ppt_s.rearrange("p (hh vv) -> p hh vv", vv=DK),
                gS.rearrange("p (hh vv) -> p hh vv", vv=DK),
                stv_s[:, 0:H].to_broadcast([128, H, DK]), ALU.mult),
               ["gS", "stv_s"], ["ppt_s"])
            op("vector", lambda e: e.tensor_add(S_in, S_in, ppt_s),
               ["S_in", "ppt_s"], ["S_in"])
            op("vector", lambda e: e.tensor_scalar_add(
                stv_s[:, 16:16 + H], gD[:, 0:H], -1.0), ["gD"], ["stv_s"])
            op("vector", lambda e, pmj=pmj: e.tensor_scalar(
                stv_s[:, 16:16 + H], stv_s[:, 16:16 + H], pmj, 1.0,
                op0=ALU.mult, op1=ALU.add), ["stv_s", "pmb"], ["stv_s"])
            op("vector", lambda e: e.tensor_mul(
                cumc[:, 0:H], cumc[:, 0:H], stv_s[:, 16:16 + H]),
               ["cumc", "stv_s"], ["cumc"])
        ar.release()

        tk.barrier()
        ar.release()

        ar.mark()
        o2 = A(H * 512)
        qc2 = A(H * 512)
        xtile = A(8 * 512)
        x1t = A(8 * 512)
        h2t = A(8 * 512)
        httok = [A(DW) for _ in range(2)]
        wo_s = A(8 * D)
        osq = [A(512) for _ in range(2)]
        gsc = A(8)
        mx8g = A(8)
        selw = A(16)
        ilv = A(16)
        dma("sync", wo_s.rearrange("p (c f) -> p c f", f=D),
            wo_d.rearrange("(c p) f -> p c f", p=128), [], ["wo_s"])

        for t in range(NT):
            tok0 = t * TILE
            dma("sync", o2.rearrange("p (hh tt) -> p hh tt", tt=512),
                o_sp.rearrange("hh v tt -> v hh tt")[:, :, tok0:tok0 + TILE],
                ["o_sp_d"], ["o2"])
            dma("sync", qc2.rearrange("p (hh tt) -> p hh tt", tt=512),
                qc_sp.rearrange("hh k tt -> k hh tt")[:, :, tok0:tok0 + TILE],
                ["qc_sp_d"], ["qc2"])
            dma("sync", xtile.rearrange("p (c tt) -> p c tt", tt=512),
                xin[:, tok0:tok0 + TILE].rearrange("(c p) tt -> p c tt", p=128),
                [], ["xtile"])
            for hh in range(H):
                par = hh % 2
                Ssl = S_in.rearrange("p (hh vv) -> p hh vv", vv=DK)[:, hh]
                qcs = qc2.rearrange("p (hh tt) -> p hh tt", tt=512)[:, hh]
                osl = o2.rearrange("p (hh tt) -> p hh tt", tt=512)[:, hh]
                op("tensor", lambda e, Ssl=Ssl, qcs=qcs, par=par: e.matmul(
                    pb[par][:, 0:512], Ssl, qcs, start=True, stop=True),
                   ["S_in", "qc2"], [KPB[par]])
                op("vector", lambda e, osl=osl, par=par: e.tensor_add(
                    osl, osl, pb[par][:, 0:512]), ["o2", KPB[par]], ["o2"])
            for hh in range(H):
                par = hh % 2
                osl = o2.rearrange("p (hh tt) -> p hh tt", tt=512)[:, hh]
                op("scalar", lambda e, osl=osl, par=par: e.activation(
                    osq[par], osl, AF.Square), ["o2"], [f"osq{par}"])
                op("tensor", lambda e, par=par, f=(hh == 0), l=(hh == 7): e.matmul(
                    pb[2][0:1, 0:512], colsel[:, 0:1], osq[par],
                    start=f, stop=l), [f"osq{par}", "colsel"], [KPB[2]])
            rr2 = prow[0:1, 0:512]
            op("vector", lambda e: e.tensor_scalar(
                rr2, pb[2][0:1, 0:512], 1.0 / D, EPS, op0=ALU.mult, op1=ALU.add),
               [KPB[2]], ["prow"])
            op("scalar", lambda e: e.activation(rr2, rr2, AF.Ln), ["prow"], ["prow"])
            op("scalar", lambda e: e.activation(rr2, rr2, AF.Exp, scale=-0.5),
               ["prow"], ["prow"])
            op("tensor", lambda e: e.matmul(
                pb[3][:, 0:512], onesr[0:1, 0:128], rr2, start=True, stop=True),
               ["prow", "onesr"], [KPB[3]])
            for hh in range(H):
                osl = o2.rearrange("p (hh tt) -> p hh tt", tt=512)[:, hh]
                op("vector", lambda e, osl=osl, hh=hh: e.scalar_tensor_tensor(
                    osl, osl, gnwc[:, hh:hh + 1], pb[3][:, 0:512],
                    op0=ALU.mult, op1=ALU.mult), ["o2", "gnwc", KPB[3]], ["o2"])
            for c in range(8):
                par = c % 2
                for hv in range(8):
                    op("tensor", lambda e, c=c, hv=hv, par=par, f=(hv == 0), l=(hv == 7):
                       e.matmul(pb[4 + par][:, 0:512],
                                wo_s.rearrange("p (cc f) -> p cc f", f=D)[
                                    :, hv, c * 128:(c + 1) * 128],
                                o2.rearrange("p (hh tt) -> p hh tt", tt=512)[:, hv],
                                start=f, stop=l), ["wo_s", "o2"], [KPB[4 + par]])
                x1c = x1t.rearrange("p (c tt) -> p c tt", tt=512)[:, c]
                xc = xtile.rearrange("p (c tt) -> p c tt", tt=512)[:, c]
                op("vector", lambda e, x1c=x1c, xc=xc, c=c, par=par:
                   e.scalar_tensor_tensor(x1c, pb[4 + par][:, 0:512], modcol(2, c),
                                          xc, op0=ALU.mult, op1=ALU.add),
                   [KPB[4 + par], "modc", "xtile"], ["x1t"])
            dma("scalar", x1_sp.rearrange("c p tt -> p c tt")[:, :, tok0:tok0 + TILE],
                x1t.rearrange("p (c tt) -> p c tt", tt=512), ["x1t"], ["x1_sp_d"])
            x1v = x1t.rearrange("p (c tt) -> p c tt", tt=512)
            for c in range(8):
                op("tensor", lambda e, c=c, x1v=x1v, f=(c == 0), l=(c == 7):
                   e.matmul(pb[2][0:1, 0:512], colsel[:, 0:1], x1v[:, c],
                            start=f, stop=l), ["x1t", "colsel"], [KPB[2]])
            for c in range(8):
                par = c % 2
                op("scalar", lambda e, c=c, x1v=x1v, par=par: e.activation(
                    osq[par], x1v[:, c], AF.Square), ["x1t"], [f"osq{par}"])
                op("tensor", lambda e, c=c, par=par, f=(c == 0), l=(c == 7):
                   e.matmul(pb[3][0:1, 0:512], colsel[:, 0:1], osq[par],
                            start=f, stop=l), [f"osq{par}", "colsel"], [KPB[3]])
            m2 = prow[0:1, 0:512]; r2 = prow[0:1, 512:1024]; mr2 = prow[0:1, 1024:1536]
            op("vector", lambda e: e.tensor_scalar_mul(m2, pb[2][0:1, 0:512], 1.0 / D),
               [KPB[2]], ["prow"])
            op("vector", lambda e: e.tensor_mul(r2, m2, m2), ["prow"], ["prow"])
            op("vector", lambda e: e.scalar_tensor_tensor(
                r2, pb[3][0:1, 0:512], 1.0 / D, r2, op0=ALU.mult, op1=ALU.subtract),
               [KPB[3], "prow"], ["prow"])
            op("scalar", lambda e: e.activation(r2, r2, AF.Ln, bias=EPS),
               ["prow"], ["prow"])
            op("scalar", lambda e: e.activation(r2, r2, AF.Exp, scale=-0.5),
               ["prow"], ["prow"])
            op("vector", lambda e: e.tensor_mul(mr2, m2, r2), ["prow"], ["prow"])
            for c in range(8):
                op("tensor", lambda e, c=c: e.matmul(
                    pb[2][:, 0:512], modrow_sl(4, c), r2, start=True, stop=True),
                   ["modrow", "prow"], [KPB[2]])
                op("tensor", lambda e, c=c: e.matmul(
                    pb[3][:, 0:512], modrow_sl(4, c), mr2, start=True, stop=False),
                   ["modrow", "prow"], [KPB[3]])
                op("tensor", lambda e, c=c: e.matmul(
                    pb[3][:, 0:512], modrow_sl(3, c), negr[0:1, 0:512],
                    start=False, stop=True), ["modrow", "negr"], [KPB[3]])
                h2c = h2t.rearrange("p (c tt) -> p c tt", tt=512)[:, c]
                op("vector", lambda e, h2c=h2c, c=c, x1v=x1v: e.tensor_mul(
                    h2c, x1v[:, c], pb[2][:, 0:512]), ["x1t", KPB[2]], ["h2t"])
                op("vector", lambda e, h2c=h2c: e.tensor_sub(
                    h2c, h2c, pb[3][:, 0:512]), ["h2t", KPB[3]], ["h2t"])
            dma("scalar", h2_sp.rearrange("c p tt -> p c tt")[:, :, tok0:tok0 + TILE],
                h2t.rearrange("p (c tt) -> p c tt", tt=512), ["h2t"], ["h2_sp_d"])
            h2vv = h2t.rearrange("p (c tt) -> p c tt", tt=512)
            for s in range(SPT):
                hpar = s % 2
                hb = httok[hpar]
                hbk = f"httok{hpar}"
                for c in range(8):
                    tp = c % 2
                    op("tensor", lambda e, c=c, s=s, tp=tp, h2vv=h2vv: e.matmul(
                        pb[tp][:, 0:128], h2vv[:, c, s * CT:(s + 1) * CT], ident,
                        is_transpose=True, start=True, stop=True),
                       ["h2t", "ident"], [KPB[tp]])
                    op("scalar", lambda e, c=c, tp=tp, hb=hb: e.copy(
                        hb[:, c * 128:(c + 1) * 128], pb[tp][:, 0:128]),
                       [KPB[tp]], [hbk])
                sub0 = tok0 + s * CT
                op("vector", lambda e, hb=hb, sub0=sub0: e.tensor_scalar_add(
                    hb[:, D:D + 1], iotac, float(sub0)), ["iotac"], [hbk])
                dma("scalar", h2tok_d[sub0:sub0 + CT, :], hb, [hbk], ["h2tok_dd"])
            h2v = h2t.rearrange("p (c tt) -> p c tt", tt=512)
            for s in range(SPT):
                for c in range(8):
                    op("tensor", lambda e, c=c, s=s, h2v=h2v, f=(c == 0):
                       e.matmul(pb[6][:, 0:H], h2v[:, c, s * CT:(s + 1) * CT],
                                gw_s.rearrange("p (c hh) -> p c hh", hh=H)[:, c],
                                start=f, stop=False), ["h2t", "gw_s"], [KPB[6]])
                op("tensor", lambda e: e.matmul(
                    pb[6][:, 0:H], onesr[0:1, 0:128], rowsc[0:1, 16:24],
                    start=False, stop=True), ["onesr", "rowsc"], [KPB[6]])
                op("scalar", lambda e: e.activation(gsc, pb[6][:, 0:H], AF.Sigmoid),
                   [KPB[6]], ["gsc"])
                op("vector", lambda e: e.max(mx8g, gsc), ["gsc"], ["mx8g"])
                op("vector", lambda e: e.tensor_scalar(
                    selw[:, 0:8], gsc, mx8g[:, 1:2], None, op0=ALU.is_ge),
                   ["gsc", "mx8g"], ["selw"])
                op("vector", lambda e: e.tensor_add(
                    selw[:, 8:9], mx8g[:, 0:1], mx8g[:, 1:2]), ["mx8g"], ["selw"])
                op("vector", lambda e: e.tensor_scalar_add(
                    selw[:, 8:9], selw[:, 8:9], 1e-8), ["selw"], ["selw"])
                op("vector", lambda e: e.reciprocal(selw[:, 9:10], selw[:, 8:9]),
                   ["selw"], ["selw"])
                ilv2 = ilv.rearrange("p (ee two) -> p ee two", two=2)
                op("vector", lambda e, ilv2=ilv2: e.tensor_mul(
                    ilv2[:, :, 0], selw[:, 0:8], gsc), ["selw", "gsc"], ["ilv"])
                op("vector", lambda e, ilv2=ilv2: e.tensor_scalar(
                    ilv2[:, :, 0], ilv2[:, :, 0], selw[:, 9:10], None, op0=ALU.mult),
                   ["ilv", "selw"], ["ilv"])
                op("vector", lambda e, ilv2=ilv2: e.tensor_copy(
                    ilv2[:, :, 1], selw[:, 0:8]), ["selw"], ["ilv"])
                fs = t * SPT + s
                op("vector", lambda e, fs=fs: e.tensor_copy(
                    selcols.rearrange("p (ee f) -> p ee f", f=NSUB)[:, :, fs],
                    selw[:, 0:8]), ["selw"], ["selcols"])
                first = (t == 0 and s == 0)
                last = (t == NT - 1 and s == SPT - 1)
                op("tensor", lambda e, f=first, l=last: e.matmul(
                    pb[7][0:16, 0:1], ilv, colsel[:, 0:1],
                    start=f, stop=l), ["ilv", "colsel"], [KPB[7]])

        op("vector", lambda e: e.tensor_copy(g16[0:16, 0:1], pb[7][0:16, 0:1]),
           [KPB[7]], ["g16"])
        dma("gpsimd", ar_in[:, None], g16[0:16, 0:1], ["g16"], ["ar_in_d"])
        tk.add("gpsimd", lambda e: e.collective_compute(
            "AllReduce", ALU.add, replica_groups=[list(range(NCORE))],
            ins=[ar_in], outs=[ar_out]), reads=["ar_in_d"], writes=["ar_out_d"],
            cc=True)
        tk.barrier()
        ar.release()

        ar.mark()
        sgw_s = A(8 * D); suw_s = A(8 * D); sdw_s = A(8 * D)
        h2l = A(8 * 512)
        zt = A(8 * 512)
        zg = [A(512) for _ in range(2)]
        ysh = [A(D) for _ in range(2)]
        csr2 = prow[:, 0:8 * NSUB]
        cexr = prow[:, 512:512 + 8 * NSUB]
        ccolT = A(NSUB)
        ccolF = A(8)
        cexf = A(8)
        postot = A(8 * NSUB)
        nfbc = A(8)
        nff_f = A(8)
        tailf = A(32)
        ztile = A(DW)
        rstage = [A(DW) for _ in range(2)]

        NW = 8 * NSUB
        BNDREG = {}

        def _gp_pro(eng):
            r = eng.alloc_register(name="bndcape")
            eng.reg_mov(r, CAPE - 1)
            BNDREG["r"] = r
        tk.prologue["gpsimd"] = _gp_pro
        op("tensor", lambda e: e.matmul(pb[1][0:1, 0:NW], colsel[:, 0:1],
                                        selcols, start=True, stop=True),
           ["selcols", "colsel"], [KPB[1]])
        op("vector", lambda e: e.tensor_copy(csr2[0:1, :], pb[1][0:1, 0:NW]),
           [KPB[1]], ["csr2", "prow"])
        op("vector", lambda e: e.reduce_sum(
            nff_f[0:1, 0:8],
            csr2[0:1, :].rearrange("o (ee f) -> o ee f", f=NSUB),
            axis=AX.X), ["csr2"], ["nff"])
        gdep_op = tk.add("vector", lambda e: e.tensor_copy(
            nfi8[0:1, 0:8], nff_f[0:1, 0:8]), reads=["nff"], writes=["nfi8"])
        tk.guard_src = lambda e_i: nfi8[0:1, e_i:e_i + 1]
        tk.guard_dep = gdep_op
        dma("scalar", rowx_d[None, :], csr2[0:1, :], ["csr2"], ["rowx"])
        dma("scalar", ccolT[0:8, :], rowx_d.rearrange("(p f) -> p f", p=8),
            ["rowx"], ["ccolT"])
        op("tensor", lambda e: e.matmul(
            pb[1][0:NSUB, 256:264], ccolT[0:8, :], ident[0:8, 0:8],
            is_transpose=True, start=True, stop=True), ["ccolT", "ident"], [KPB[1]])
        op("vector", lambda e: e.tensor_copy(ccolF[0:NSUB, 0:8],
                                             pb[1][0:NSUB, 256:264]),
           [KPB[1]], ["ccolF"])
        op("tensor", lambda e: e.matmul(
            pb[1][0:NSUB, 264:272], t32[0:NSUB, 0:NSUB], ccolF[0:NSUB, 0:8],
            start=True, stop=True), ["t32", "ccolF"], [KPB[1]])
        op("vector", lambda e: e.tensor_copy(cexf[0:NSUB, 0:8],
                                             pb[1][0:NSUB, 264:272]),
           [KPB[1]], ["cexf"])
        dma("scalar", rowx2_d.rearrange("(ee f) -> f ee", f=NSUB),
            cexf[0:NSUB, 0:8], ["cexf"], ["rowx2"], slow=True)
        dma("scalar", cexr[0:1, :], rowx2_d[None, :], ["rowx2"], ["cexr"])
        op("tensor", lambda e: e.matmul(pb[0][:, 0:NW], tri, selcols,
                                        start=True, stop=False),
           ["tri", "selcols"], [KPB[0]])
        op("tensor", lambda e: e.matmul(pb[0][:, 0:NW], onesr[0:1, 0:128],
                                        cexr[0:1, :], start=False, stop=True),
           ["onesr", "cexr"], [KPB[0]])
        op("vector", lambda e: e.tensor_copy(postot, pb[0][:, 0:NW]),
           [KPB[0]], ["postot"])
        op("vector", lambda e: e.tensor_sub(postot, postot, selcols),
           ["postot", "selcols"], ["postot"])
        op("vector", lambda e: e.scalar_tensor_tensor(
            postot, selcols, -BIGP, postot, op0=ALU.mult, op1=ALU.add),
           ["selcols", "postot"], ["postot"])
        op("vector", lambda e: e.tensor_scalar_add(postot, postot, BIGP),
           ["postot"], ["postot"])
        op("vector", lambda e: e.tensor_copy(idxi[:, :], postot),
           ["postot"], ["idxi"])
        op("tensor", lambda e: e.matmul(pb[1][:, 272:280], onesr[0:1, 0:128],
                                        nff_f[0:1, 0:8], start=True, stop=True),
           ["onesr", "nff"], [KPB[1]])
        op("vector", lambda e: e.tensor_copy(nfbc, pb[1][:, 272:280]),
           [KPB[1]], ["nfbc"])
        for e_i in range(8):
            for j in range(4):
                op("vector", lambda e, e_i=e_i, j=j: e.scalar_tensor_tensor(
                    tailf[:, e_i * 4 + j:e_i * 4 + j + 1], iotac, float(j * 128),
                    nfbc[:, e_i:e_i + 1], op0=ALU.add, op1=ALU.add),
                   ["iotac", "nfbc"], ["tailf"])
        op("vector", lambda e: e.tensor_copy(idxt[:, :], tailf),
           ["tailf"], ["idxt"])
        op("vector", lambda e: e.memset(ztile, 0.0), [], ["ztile"])
        op("vector", lambda e: e.memset(ztile[:, D:D + 1], BIGP), [], ["ztile"])
        op("vector", lambda e: e.memset(postot[:, 0:4], BIGP), ["postot"],
           ["postot"])
        op("vector", lambda e: e.tensor_copy(bigi[:, :], postot[:, 0:4]),
           ["postot"], ["bigi"])
        for fs in range(NSUB):
            rb = rstage[fs % 2]
            rbk = f"rstage{fs % 2}"
            dma("gpsimd", rb, h2tok_d[fs * CT:(fs + 1) * CT, :],
                ["h2tok_dd"], [rbk])
            for e_i in range(8):
                idxcol = idxi[:, e_i * NSUB + fs:e_i * NSUB + fs + 1]
                tk.add("gpsimd", lambda e, e_i=e_i, rb=rb, idxcol=idxcol:
                       e.indirect_dma_start(
                           out=compact_d[e_i],
                           out_offset=IndirectOffsetOnAxis(idxcol, 0),
                           in_=rb, in_offset=None,
                           bounds_check=BNDREG["r"], oob_is_err=False),
                       reads=[rbk, "idxi"], writes=[f"cp{e_i}"], dma=True,
                       tag=f"cp{e_i}")
        for e_i in range(8):
            for j in range(4):
                idc = idxt[:, e_i * 4 + j:e_i * 4 + j + 1]
                tk.add("gpsimd", lambda e, e_i=e_i, idc=idc:
                       e.indirect_dma_start(
                           out=compact_d[e_i],
                           out_offset=IndirectOffsetOnAxis(idc, 0),
                           in_=ztile, in_offset=None,
                           bounds_check=BNDREG["r"], oob_is_err=False),
                       reads=["ztile", "idxt"], writes=[f"cp{e_i}"], dma=True,
                       tag=f"cp{e_i}")
        dma("sync", sgw_s.rearrange("p (c f) -> p c f", f=D),
            sgwt.rearrange("(c p) f -> p c f", p=128), [], ["sgw_s"])
        dma("sync", suw_s.rearrange("p (c f) -> p c f", f=D),
            suwt.rearrange("(c p) f -> p c f", p=128), [], ["suw_s"])
        dma("sync", sdw_s.rearrange("p (c f) -> p c f", f=D),
            sdwt.rearrange("(c p) f -> p c f", p=128), [], ["sdw_s"])
        for t in range(NT):
            tok0 = t * TILE
            dma("sync", h2l.rearrange("p (c tt) -> p c tt", tt=512),
                h2_sp.rearrange("c p tt -> p c tt")[:, :, tok0:tok0 + TILE],
                ["h2_sp_d"], ["h2l"])
            h2v = h2l.rearrange("p (c tt) -> p c tt", tt=512)
            for ic in range(8):
                gp_ = ic % 2
                for c in range(8):
                    op("tensor", lambda e, ic=ic, c=c, h2v=h2v, gp_=gp_,
                       f=(c == 0), l=(c == 7):
                       e.matmul(pb[gp_][:, 0:512],
                                sgw_s.rearrange("p (cc f) -> p cc f", f=D)[
                                    :, c, ic * 128:(ic + 1) * 128],
                                h2v[:, c], start=f, stop=l),
                       ["sgw_s", "h2l"], [KPB[gp_]])
                op("scalar", lambda e, gp_=gp_: e.activation(
                    zg[gp_], pb[gp_][:, 0:512], AF.Silu), [KPB[gp_]], [f"zg{gp_}"])
                for c in range(8):
                    op("tensor", lambda e, ic=ic, c=c, h2v=h2v, gp_=gp_,
                       f=(c == 0), l=(c == 7):
                       e.matmul(pb[2 + gp_][:, 0:512],
                                suw_s.rearrange("p (cc f) -> p cc f", f=D)[
                                    :, c, ic * 128:(ic + 1) * 128],
                                h2v[:, c], start=f, stop=l),
                       ["suw_s", "h2l"], [KPB[2 + gp_]])
                zsl = zt.rearrange("p (ic tt) -> p ic tt", tt=512)[:, ic]
                op("vector", lambda e, zsl=zsl, gp_=gp_: e.tensor_mul(
                    zsl, zg[gp_], pb[2 + gp_][:, 0:512]),
                   [f"zg{gp_}", KPB[2 + gp_]], ["zt"])
            for s4 in range(SPT):
                ypar = s4 % 2
                yb = ysh[ypar]
                ybk = f"ysh{ypar}"
                for half in range(2):
                    par2 = (s4 * 2 + half) % 2
                    for ic in range(8):
                        op("tensor", lambda e, s4=s4, half=half, ic=ic, par2=par2,
                           f=(ic == 0), l=(ic == 7):
                           e.matmul(pb[4 + par2][:, 0:512],
                                    zt.rearrange("p (ic tt) -> p ic tt", tt=512)[
                                        :, ic, s4 * CT:(s4 + 1) * CT],
                                    sdw_s.rearrange("p (cc f) -> p cc f", f=D)[
                                        :, ic, half * 512:(half + 1) * 512],
                                    start=f, stop=l), ["zt", "sdw_s"],
                           [KPB[4 + par2]])
                    op("vector", lambda e, half=half, yb=yb, par2=par2:
                       e.tensor_copy(yb[:, half * 512:(half + 1) * 512],
                                     pb[4 + par2][:, 0:512]),
                       [KPB[4 + par2]], [ybk])
                sub0 = tok0 + s4 * CT
                dma("scalar", acc_tok_d[sub0:sub0 + CT, :], yb, [ybk],
                    ["acc_tok"])
        tk.barrier()
        ar.release()

        dma("gpsimd", wm2[0:8, :], ar_out.rearrange("(ee two) -> ee two", two=2),
            ["ar_out_d"], ["wm2"])
        op("vector", lambda e: e.tensor_scalar_max(wm2[0:8, 1:2], wm2[0:8, 1:2], 1.0),
           ["wm2"], ["wm2"])
        op("vector", lambda e: e.reciprocal(wm2[0:8, 1:2], wm2[0:8, 1:2]),
           ["wm2"], ["wm2"])
        op("vector", lambda e: e.tensor_mul(wm2[0:8, 0:1], wm2[0:8, 0:1],
                                            wm2[0:8, 1:2]), ["wm2"], ["wm2"])
        dma("gpsimd", wm_d[:, None], wm2[0:8, 0:1], ["wm2"], ["wm_dd"])
        for e_i in range(8):
            dma("gpsimd", wmcol[:, e_i:e_i + 1],
                wm_d[e_i:e_i + 1].partition_broadcast(128), ["wm_dd"], ["wmcol"])
        ar.mark()
        wgd = [A(8 * 512) for _ in range(2)]
        wud = [A(8 * 512) for _ in range(2)]
        wdd = [A(4 * D) for _ in range(2)]
        cblk = [A(4 * DW)]
        h2cm = [A(8 * 512)]
        zcm = [A(4 * 512)]
        zge = [A(512) for _ in range(2)]
        ysb = [A(D) for _ in range(4)]
        def ld_wg(e_i):
            p_ = e_i % 2
            dma("scalar", wgd[p_].rearrange("p (c f) -> p c f", f=512),
                egwt[e_i].rearrange("(c p) f -> p c f", p=128), [],
                [f"wg_s{p_}"])

        def ld_wu(e_i):
            p_ = e_i % 2
            dma("scalar", wud[p_].rearrange("p (c f) -> p c f", f=512),
                euwt[e_i].rearrange("(c p) f -> p c f", p=128), [],
                [f"wu_s{p_}"])

        def ld_wd(e_i):
            p_ = e_i % 2
            dma("scalar", wdd[p_].rearrange("p (c f) -> p c f", f=D),
                edwt[e_i].rearrange("(c p) f -> p c f", p=128), [],
                [f"wdn_s{p_}"])

        ld_wg(0); ld_wu(0); ld_wd(0)
        for e_i in range(8):
            epar = e_i % 2
            wg_s = wgd[epar]; wu_s = wud[epar]; wdn_s = wdd[epar]
            wgk = f"wg_s{epar}"; wuk = f"wu_s{epar}"; wdk = f"wdn_s{epar}"
            for b in range(NBLK):
                if e_i + 1 < 8:
                    if b == min(2, NBLK - 1):
                        ld_wg(e_i + 1)
                    if b == min(4, NBLK - 1):
                        ld_wu(e_i + 1)
                    if b == min(6, NBLK - 1):
                        ld_wd(e_i + 1)
                g = (e_i, b * 512)
                bpar = b % 2
                cb = cblk[0]; cbk = "cblk0"
                hc = h2cm[0]; hck = "h2cm0"
                zb = zcm[0]; zbk = "zcm0"
                tk.add("sync", lambda e, cb=cb, e_i=e_i, b=b: e.dma_start(
                    out=cb.rearrange("p (s f) -> p s f", f=DW),
                    in_=compact_d[e_i][b * 512:(b + 1) * 512, :]
                    .rearrange("(s p) f -> p s f", p=128)),
                    reads=[f"cp{e_i}"], writes=[cbk], dma=True, tag=cbk, guard=g)
                cbv = cb.rearrange("p (s f) -> p s f", f=DW)
                op("vector", lambda e, bpar=bpar: e.tensor_copy(
                    invi[:, bpar * 4:(bpar + 1) * 4], bigi[:, :]),
                   ["bigi"], ["invi"])
                for s4 in range(SPT):
                    for c in range(8):
                        tp = 6 + (c % 2)
                        op2 = tk.add("tensor", lambda e, s4=s4, c=c, tp=tp, cbv=cbv:
                                     e.matmul(pb[tp][:, 0:128],
                                              cbv[:, s4, c * 128:(c + 1) * 128],
                                              ident, is_transpose=True,
                                              start=True, stop=True),
                                     reads=[cbk, "ident"], writes=[KPB[tp]],
                                     guard=g)
                        tk.add("scalar", lambda e, s4=s4, c=c, tp=tp, hc=hc:
                               e.copy(hc.rearrange("p (cc tt) -> p cc tt", tt=512)[
                                   :, c, s4 * CT:(s4 + 1) * CT], pb[tp][:, 0:128]),
                               reads=[KPB[tp]], writes=[hck], guard=g)
                    tk.add("vector", lambda e, s4=s4, bpar=bpar, cbv=cbv:
                           e.tensor_copy(invi[:, bpar * 4 + s4:bpar * 4 + s4 + 1],
                                         cbv[:, s4, D:D + 1]),
                           reads=[cbk], writes=["invi"], guard=g)
                hcv = hc.rearrange("p (cc tt) -> p cc tt", tt=512)
                for ic in range(4):
                    gp_ = ic % 2
                    for c in range(8):
                        tk.add("tensor", lambda e, ic=ic, c=c, hcv=hcv, gp_=gp_,
                               wg_s=wg_s, f=(c == 0), l=(c == 7):
                               e.matmul(pb[gp_][:, 0:512],
                                        wg_s.rearrange("p (cc f) -> p cc f",
                                                       f=512)[
                                            :, c, ic * 128:(ic + 1) * 128],
                                        hcv[:, c], start=f, stop=l),
                               reads=[wgk, hck], writes=[KPB[gp_]], guard=g)
                    tk.add("scalar", lambda e, gp_=gp_: e.activation(
                        zge[gp_], pb[gp_][:, 0:512], AF.Silu),
                        reads=[KPB[gp_]], writes=[f"zge{gp_}"], guard=g)
                    for c in range(8):
                        tk.add("tensor", lambda e, ic=ic, c=c, hcv=hcv, gp_=gp_,
                               wu_s=wu_s, f=(c == 0), l=(c == 7):
                               e.matmul(pb[2 + gp_][:, 0:512],
                                        wu_s.rearrange("p (cc f) -> p cc f",
                                                       f=512)[
                                            :, c, ic * 128:(ic + 1) * 128],
                                        hcv[:, c], start=f, stop=l),
                               reads=[wuk, hck], writes=[KPB[2 + gp_]], guard=g)
                    zsl = zb.rearrange("p (ic tt) -> p ic tt", tt=512)[:, ic]
                    tk.add("vector", lambda e, zsl=zsl, gp_=gp_: e.tensor_mul(
                        zsl, zge[gp_], pb[2 + gp_][:, 0:512]),
                        reads=[f"zge{gp_}", KPB[2 + gp_]], writes=[zbk], guard=g)
                for s4 in range(SPT):
                    yb = ysb[s4]; ybk2 = f"ysb{s4}"
                    for half in range(2):
                        par2 = (s4 * 2 + half) % 2
                        for ic in range(4):
                            tk.add("tensor", lambda e, s4=s4, half=half, ic=ic,
                                   par2=par2, zb=zb, wdn_s=wdn_s,
                                   f=(ic == 0), l=(ic == 3):
                                   e.matmul(pb[4 + par2][:, 0:512],
                                            zb.rearrange("p (ic tt) -> p ic tt",
                                                         tt=512)[
                                                :, ic, s4 * CT:(s4 + 1) * CT],
                                            wdn_s.rearrange("p (cc f) -> p cc f",
                                                            f=D)[
                                                :, ic,
                                                half * 512:(half + 1) * 512],
                                            start=f, stop=l),
                                   reads=[zbk, wdk], writes=[KPB[4 + par2]],
                                   guard=g)
                        tk.add("scalar", lambda e, half=half, yb=yb, par2=par2,
                               e_i=e_i: e.activation(
                                   yb[:, half * 512:(half + 1) * 512],
                                   pb[4 + par2][:, 0:512], AF.Copy,
                                   scale=wmcol[:, e_i:e_i + 1]),
                               reads=[KPB[4 + par2], "wmcol"], writes=[ybk2],
                               guard=g)
                    iv = invi[:, bpar * 4 + s4:bpar * 4 + s4 + 1]
                    tk.add("gpsimd", lambda e, yb=yb, iv=iv:
                           e.indirect_dma_start(
                               out=acc_tok_d,
                               out_offset=IndirectOffsetOnAxis(iv, 0),
                               in_=yb, in_offset=None,
                               bounds_check=BNDREG["r"], oob_is_err=False,
                               compute_op=ALU.add),
                           reads=[ybk2, "invi"], writes=["acc_tok"], dma=True,
                           tag="accadd")
        tk.barrier()
        ar.release()

        ar.mark()
        x1l2 = [A(8 * 512) for _ in range(2)]
        accl2 = [A(4 * D) for _ in range(2)]
        gbbc = A(D)
        tmp4 = [A(D) for _ in range(2)]
        outt = [A(D) for _ in range(2)]
        dma("sync", gbbc, modb_d[5 * D:6 * D].partition_broadcast(128),
            ["modb"], ["gbbc"])
        for t in range(NT):
            tok0 = t * TILE
            tp4 = t % 2
            x1l = x1l2[tp4]
            accl = accl2[tp4]
            x1lk = f"x1l{tp4}"
            acclk = f"accl{tp4}"
            dma("sync", x1l.rearrange("p (c tt) -> p c tt", tt=512),
                x1_sp.rearrange("c p tt -> p c tt")[:, :, tok0:tok0 + TILE],
                ["x1_sp_d"], [x1lk])
            dma("sync", accl.rearrange("p (s f) -> p s f", f=D),
                acc_tok_d[tok0:tok0 + TILE, :].rearrange("(s p) f -> p s f",
                                                         p=128),
                ["acc_tok"], [acclk])
            accv = accl.rearrange("p (s f) -> p s f", f=D)
            x1v4 = x1l.rearrange("p (c tt) -> p c tt", tt=512)
            for s in range(SPT):
                par = s % 2
                ob = outt[par]
                obk = f"outt{par}"
                for c in range(8):
                    tp = 4 + (c % 2)
                    op("tensor", lambda e, c=c, s=s, tp=tp, x1v4=x1v4: e.matmul(
                        pb[tp][:, 0:128], x1v4[:, c, s * CT:(s + 1) * CT],
                        ident, is_transpose=True, start=True, stop=True),
                       [x1lk, "ident"], [KPB[tp]])
                    op("vector", lambda e, c=c, tp=tp, ob=ob: e.tensor_copy(
                        ob[:, c * 128:(c + 1) * 128], pb[tp][:, 0:128]),
                       [KPB[tp]], [obk])
                tb = tmp4[par]
                op("vector", lambda e, s=s, tb=tb, accv=accv: e.tensor_mul(
                    tb, accv[:, s], gbbc), [acclk, "gbbc"], [f"tmp4{par}"])
                op("vector", lambda e, ob=ob, tb=tb: e.tensor_add(ob, ob, tb),
                   [obk, f"tmp4{par}"], [obk])
                sub0 = tok0 + s * CT
                dma("gpsimd", yout[sub0:sub0 + CT, :], ob, [obk], ["yout_d"])
        tk.barrier()
        ar.release()

        globals()["_last_tk"] = tk
        semnames = sorted({o.sem for o in tk.ops})
        sems = {nm: ctx.enter_context(nc.semaphore(nm)) for nm in semnames}
        block = ctx.enter_context(nc.Block())
        tk.emit(block, sems)
    return nc


def _host_prep(inputs, TPC):
    x = np.ascontiguousarray(np.asarray(inputs["L0_attn_hidden_states"],
                                        dtype=np.float32))
    B, T, _ = x.shape
    ident = np.eye(128, dtype=np.float32)
    tri = np.triu(np.ones((128, 128), np.float32))
    tri4 = np.ascontiguousarray(np.tile(tri, (1, 4)))
    colsel = np.zeros((128, 2), np.float32)
    colsel[:, 0] = 1.0
    colsel[:CT // 2, 1] = 1.0
    onesr = np.ones((1, 512), np.float32)
    negr = -np.ones((1, 512), np.float32)
    k32 = np.arange(32)
    t32 = (k32[:, None] < k32[None, :]).astype(np.float32)
    iotac = np.arange(128, dtype=np.float32)[:, None]
    f32c = lambda a: np.ascontiguousarray(np.asarray(a, dtype=np.float32))
    common = {
        "wq": f32c(inputs["L0_attn_Wq"]), "wk": f32c(inputs["L0_attn_Wk"]),
        "wv": f32c(inputs["L0_attn_Wv"]), "wr": f32c(inputs["L0_attn_Wr"]),
        "wa": f32c(inputs["L0_attn_Wa"]), "wo": f32c(inputs["L0_attn_Wo"]),
        "a_log": f32c(inputs["L0_attn_A_log"]),
        "dt_b": f32c(inputs["L0_attn_dt_bias"]),
        "qnw": f32c(inputs["L0_attn_q_norm_w"]),
        "knw": f32c(inputs["L0_attn_k_norm_w"]),
        "gnw": f32c(inputs["L0_attn_g_norm_w"]),
        "gate_wt": f32c(np.transpose(inputs["L0_ffn_gate_w"])),
        "lb_b": f32c(inputs["L0_ffn_lb_bias"]),
        "egwt": f32c(np.transpose(inputs["L0_ffn_expert_gate_w"], (0, 2, 1))),
        "euwt": f32c(np.transpose(inputs["L0_ffn_expert_up_w"], (0, 2, 1))),
        "edwt": f32c(np.transpose(inputs["L0_ffn_expert_down_w"], (0, 2, 1))),
        "sgwt": f32c(np.transpose(inputs["L0_ffn_shared_gate_w"])),
        "suwt": f32c(np.transpose(inputs["L0_ffn_shared_up_w"])),
        "sdwt": f32c(np.transpose(inputs["L0_ffn_shared_down_w"])),
        "wmod": f32c(inputs["L0_wmod"]),
        "ident": ident, "tri": tri, "tri4": tri4, "colsel": colsel,
        "onesr": onesr, "negr": negr, "t32": t32, "iotac": iotac,
    }
    cfull = f32c(inputs["L0_c"])
    in_maps = []
    for core in range(NCORE):
        b = core // 4
        p = core % 4
        xs = np.ascontiguousarray(x[b, p * TPC:(p + 1) * TPC, :].T)
        pmask = np.zeros(NCORE, np.float32)
        for j in range(NCORE):
            if j // 4 == b and (j % 4) < p:
                pmask[j] = 1.0
        m = dict(common)
        m["xin"] = xs
        m["c_in"] = np.ascontiguousarray(cfull[b:b + 1, :])
        m["pmask"] = pmask
        in_maps.append(m)
    return in_maps, (B, T)


def kernel(**inputs):
    T = int(np.asarray(inputs["L0_attn_hidden_states"]).shape[1])
    TPC = 2 * T // NCORE
    nc = build_program(TPC)
    in_maps, (B, T) = _host_prep(inputs, TPC)
    res = run_bass_kernel_spmd(nc, in_maps, core_ids=list(range(NCORE)))
    out = np.zeros((B, T, D), np.float32)
    for core in range(NCORE):
        b = core // 4
        p = core % 4
        out[b, p * TPC:(p + 1) * TPC, :] = res.results[core]["yout"]
    return out


# version 30
# speedup vs baseline: 1.0247x; 1.0145x over previous
import numpy as np
import concourse.bass as bass
import concourse.mybir as mybir
from concourse.bass_utils import run_bass_kernel_spmd
from concourse.bass_core_rust import IndirectOffsetOnAxis

F32 = mybir.dt.float32
I32 = mybir.dt.int32
BIGP = 100000.0
AF = mybir.ActivationFunctionType
ALU = mybir.AluOpType
AX = mybir.AxisListType

D = 1024
H = 8
DK = 128
NCORE = 8
CT = 128
EPS = 1e-5
CLAMP = 85.0
NEG = -1.0


class _Op:
    __slots__ = ("eng", "fn", "deps", "need", "sem", "val", "inc", "guard",
                 "chain")

    def __init__(self, eng, fn, sem, inc):
        self.eng, self.fn, self.sem, self.inc = eng, fn, sem, inc
        self.deps = []
        self.need = False
        self.val = None
        self.guard = None
        self.chain = None


class Tracker:
    ENGS = ("sync", "tensor", "vector", "scalar", "gpsimd")

    def __init__(self):
        self.ops = []
        self.last_w = {}
        self.readers = {}
        self.forced = {}
        self.sem_last = {}
        self.dma_last = {}
        self.ncc = 0
        self.guard_src = None
        self.guard_dep = None
        self.prologue = {}

    def add(self, eng, fn, reads=(), writes=(), dma=False, cc=False, tag=None,
            guard=None):
        if cc:
            sem, inc = f"cc{self.ncc}", 1
            self.ncc += 1
        elif dma:
            sem, inc = "dq_" + (tag or "misc"), 16
        else:
            sem, inc = "c_" + eng, 1
        op = _Op(eng, fn, sem, inc)
        deps = set()
        for k in reads:
            w = self.last_w.get(k)
            if w is not None:
                deps.add(w)
        for k in writes:
            w = self.last_w.get(k)
            if w is not None:
                deps.add(w)
            for r in self.readers.get(k, ()):
                deps.add(r)
        fb = self.forced.get(eng)
        if fb is not None:
            deps.add(fb)
        if dma:
            prev = self.dma_last.get(sem)
            if prev is not None:
                deps.add(prev)
                op.chain = prev
            self.dma_last[sem] = op
        deps.discard(op)
        if eng == "tensor":
            deps = {d for d in deps if d.eng != "tensor"}
        op.deps = list(deps)
        op.guard = guard
        for d in op.deps:
            d.need = True
        for k in reads:
            self.readers.setdefault(k, []).append(op)
        for k in writes:
            self.last_w[k] = op
            self.readers[k] = [op]
        self.ops.append(op)
        self.sem_last[sem] = op
        return op

    def barrier(self):
        snap = list(self.sem_last.values())
        for eng in self.ENGS:
            op = _Op(eng, lambda e: e.nop(), "c_" + eng, 1)
            op.deps = list(snap)
            for d in op.deps:
                d.need = True
            self.ops.append(op)
            self.forced[eng] = op
            self.sem_last[op.sem] = op

    def emit(self, block, handles):
        for op in self.ops:
            if op.inc == 16 or op.sem == "cc":
                op.need = True
        counts = {}
        for op in self.ops:
            if op.need:
                counts[op.sem] = counts.get(op.sem, 0) + op.inc
                op.val = counts[op.sem]
        per_eng = {e: [] for e in self.ENGS}
        for op in self.ops:
            per_eng[op.eng].append(op)

        gsrc, gdep = self.guard_src, self.guard_dep
        waited = {}
        for o in self.ops:
            bysem = {}
            for d in o.deps:
                if d.val is not None:
                    if bysem.get(d.sem, 0) < d.val:
                        bysem[d.sem] = d.val
            for sm, v in bysem.items():
                waited.setdefault(sm, set()).add(v)
        if gdep is not None and gdep.val is not None:
            waited.setdefault(gdep.sem, set()).add(gdep.val)

        def mk(ops, ename):
            def body(eng):
                pro = self.prologue.get(ename)
                if pro is not None:
                    pro(eng)
                hwm = {}
                state = {"g": None, "cm": None, "pend": None, "reg": None,
                         "ge": None}

                def close():
                    if state["cm"] is None:
                        return
                    finals = {}
                    starts = {}
                    kinds = {}
                    for o2 in state["pend"]:
                        if o2.sem not in starts:
                            starts[o2.sem] = o2.val - o2.inc
                            kinds[o2.sem] = (o2.eng == "gpsimd"
                                             and o2.inc == 16)
                            if o2.chain is not None \
                                    and o2.chain.val is not None \
                                    and o2.chain not in state["pend"]:
                                starts[o2.sem] = (o2.val - o2.inc, o2.chain)
                        finals[o2.sem] = o2.val
                    for sm, v in finals.items():
                        eng.wait_ge(handles[sm], v)
                    state["cm"].__exit__(None, None, None)
                    el = eng.Else()
                    el.__enter__()
                    for sm, st0 in starts.items():
                        if isinstance(st0, tuple):
                            st0, ch = st0
                            eng.wait_ge(handles[ch.sem], ch.val)
                        fin = finals[sm]
                        steps = sorted(v for v in waited.get(sm, ())
                                       if st0 < v < fin)
                        prev = st0
                        swdge = kinds[sm]
                        for v in steps + [fin]:
                            if v > prev:
                                if swdge:
                                    eng.inc_swdge_sem([handles[sm]],
                                                      [v - prev])
                                else:
                                    eng.sem_inc(handles[sm], v - prev)
                                prev = v
                    for sm, v in finals.items():
                        eng.wait_ge(handles[sm], v)
                    el.__exit__(None, None, None)
                    state["cm"] = None
                    state["g"] = None
                    state["pend"] = None

                for o in ops:
                    if o.guard != state["g"]:
                        close()
                    if o.guard is not None and state["cm"] is None:
                        if (gdep is not None and gdep.val is not None
                                and hwm.get(gdep.sem, 0) < gdep.val):
                            eng.wait_ge(handles[gdep.sem], gdep.val)
                            hwm[gdep.sem] = gdep.val
                        if state["reg"] is None:
                            state["reg"] = eng.alloc_register(name="nfguard")
                        e_i, thresh = o.guard
                        if state["ge"] != e_i:
                            eng.reg_load(state["reg"], gsrc(e_i))
                            state["ge"] = e_i
                        state["cm"] = eng.If_cmp(state["reg"], thresh, "IS_GT")
                        state["cm"].__enter__()
                        state["g"] = o.guard
                        state["pend"] = []
                    bysem = {}
                    for d in o.deps:
                        if d.val is None:
                            continue
                        if bysem.get(d.sem, 0) < d.val:
                            bysem[d.sem] = d.val
                    for sm, v in bysem.items():
                        if hwm.get(sm, 0) >= v:
                            continue
                        eng.wait_ge(handles[sm], v)
                        if state["cm"] is None:
                            hwm[sm] = v
                    ins = o.fn(eng)
                    if o.need:
                        if ins is None:
                            ins = eng.nop()
                        ins.then_inc(handles[o.sem], o.inc)
                        if state["cm"] is not None:
                            state["pend"].append(o)
                close()
            return body

        for e in self.ENGS:
            if per_eng[e]:
                getattr(block, e)(mk(per_eng[e], e))
        return counts


class Arena:
    def __init__(self, t, total):
        self.t, self.total = t, total
        self.off = 0
        self.marks = []

    def alloc(self, n):
        a = self.off
        self.off += n
        assert self.off <= self.total, f"arena overflow {self.off} > {self.total}"
        return self.t[:, a:a + n]

    def mark(self):
        self.marks.append(self.off)

    def release(self):
        self.off = self.marks.pop()


def build_program(TPC):
    TILE = 512
    SPT = TILE // CT
    NT = TPC // TILE
    assert TPC % TILE == 0

    nc = bass.Bass("TRN2", target_bir_lowering=False)

    def din(name, shape):
        return nc.dram_tensor(name, list(shape), F32, kind="ExternalInput").ap()

    def dint(name, shape):
        return nc.dram_tensor(name, list(shape), F32).ap()

    xin = din("xin", [D, TPC])
    wq = din("wq", [D, D]); wk = din("wk", [D, D])
    wv = din("wv", [D, D]); wr = din("wr", [D, D])
    wa_d = din("wa", [D, H])
    wo_d = din("wo", [D, D])
    a_log = din("a_log", [H]); dt_b = din("dt_b", [H])
    qnw_d = din("qnw", [DK]); knw_d = din("knw", [DK]); gnw_d = din("gnw", [D])
    gate_wt = din("gate_wt", [D, H])
    lb_b = din("lb_b", [H])
    egwt = din("egwt", [8, D, 512]); euwt = din("euwt", [8, D, 512])
    edwt = din("edwt", [8, 512, D])
    sgwt = din("sgwt", [D, D]); suwt = din("suwt", [D, D]); sdwt = din("sdwt", [D, D])
    c_in = din("c_in", [1, D])
    wmod = din("wmod", [D, 6 * D])
    ident_d = din("ident", [128, 128])
    tri_d = din("tri", [128, 128])
    tri4_d = din("tri4", [128, 512])
    colsel_d = din("colsel", [128, 2])
    ones_d = din("onesr", [1, 512])
    negr_d = din("negr", [1, 512])
    pmask_d = din("pmask", [NCORE])
    t32_d = din("t32", [32, 32])
    iotac_d = din("iotac", [128, 1])

    NSUB = TPC // CT
    NBLK = TPC // 512
    CAPE = TPC + 512
    DW = D + 1
    o_sp = dint("o_sp", [H, DK, TPC])
    qc_sp = dint("qc_sp", [H, DK, TPC])
    x1_sp = dint("x1_sp", [8, 128, TPC])
    h2_sp = dint("h2_sp", [8, 128, TPC])
    h2tok_d = dint("h2tok", [TPC, DW])
    compact_d = [dint(f"compact{i}", [CAPE, DW]) for i in range(8)]
    acc_tok_d = dint("acc_tok", [TPC, D])
    rowx_d = dint("rowx", [8 * NSUB])
    rowx2_d = dint("rowx2", [8 * NSUB])
    modb_d = dint("modb", [6 * D])
    ag_in = dint("ag_in", [H * DK * DK + H * DK])
    ag_out = dint("ag_out", [NCORE * (H * DK * DK + H * DK)])
    ar_in = dint("ar_in", [16])
    ar_out = dint("ar_out", [16])
    wm_d = dint("wm_d", [8])
    yout = nc.dram_tensor("yout", [TPC, D], F32, kind="ExternalOutput").ap()

    tk = Tracker()
    from contextlib import ExitStack

    with ExitStack() as ctx:
        ARENA_F = 52850
        arena_t = ctx.enter_context(nc.sbuf_tensor("arena", [128, ARENA_F], F32))
        ar = Arena(arena_t, ARENA_F)
        A = ar.alloc
        idxi = ctx.enter_context(nc.sbuf_tensor("idxi", [128, 8 * NSUB], I32))
        idxt = ctx.enter_context(nc.sbuf_tensor("idxt", [128, 32], I32))
        nfi8 = ctx.enter_context(nc.sbuf_tensor("nfi8", [1, 8], I32))
        invi = ctx.enter_context(nc.sbuf_tensor("invi", [128, 8], I32))
        bigi = ctx.enter_context(nc.sbuf_tensor("bigi", [128, 4], I32))

        ident = A(128); tri = A(128); tri4 = A(512); colsel = A(2)
        onesr = A(512); negr = A(512)
        modc = A(48)
        qknw = A(2); gnwc = A(8)
        naec = A(8)
        rowsc = A(24)
        pmb = A(8)
        S_loc = A(H * DK); S_in = A(H * DK)
        cumD = A(H); cumc = A(H)
        wm2 = A(2)
        modrow = A(6 * D)
        wa_s = A(8 * H)
        gw_s = A(8 * H)
        prow = A(3 * 512)
        g16 = A(1)
        t32 = A(32)
        iotac = A(1)
        selcols = A(8 * NSUB)
        wmcol = A(8)

        cvals = (EPS, 1.0, -0.5, -1.0)
        cbuf = A(len(cvals))
        pb = [ctx.enter_context(nc.psum_tensor(f"pb{i}", [128, 512], F32))
              for i in range(8)]
        KPB = [f"pb{i}" for i in range(8)]

        def dma(eng, out, in_, rk, wk_, slow=False):
            tk.add(eng, lambda e, o=out, i=in_, s=slow: e.dma_start(
                out=o, in_=i, allow_slow_non_contiguous=s),
                   reads=rk, writes=wk_, dma=True, tag=wk_[0])

        def dma_acc(out, in_, rk, wk_):
            tk.add("gpsimd", lambda e, o=out, i=in_: e.dma_start(
                out=o, in_=i, accum_op=ALU.add), reads=rk, writes=wk_, dma=True,
                tag=wk_[0])

        def op(eng, fn, rk, wk_):
            tk.add(eng, fn, reads=rk, writes=wk_)

        def modrow_sl(w, c):
            return modrow[0:1, w * D + c * 128: w * D + (c + 1) * 128]

        def modcol(w, c):
            return modc[:, w * 8 + c:w * 8 + c + 1]

        for ci, cv in enumerate(cvals):
            op("vector", lambda e, ci=ci, cv=cv: e.memset(cbuf[:, ci:ci + 1], cv),
               [], ["cbuf"])
            nc.const_aps.register(F32, cv, cbuf[:, ci:ci + 1])
        dma("sync", ident, ident_d, ["i_d"], ["ident"])
        dma("sync", tri, tri_d, ["t_d"], ["tri"])
        dma("sync", tri4, tri4_d, ["t4_d"], ["tri4"])
        dma("sync", colsel, colsel_d, ["cs_d"], ["colsel"])
        dma("sync", onesr[0:1, :], ones_d, ["o_d"], ["onesr"])
        dma("sync", negr[0:1, :], negr_d, ["n_d"], ["negr"])
        dma("sync", qknw[:, 0:1], qnw_d[:, None], ["qn_d"], ["qknw"])
        dma("sync", qknw[:, 1:2], knw_d[:, None], ["kn_d"], ["qknw"])
        dma("sync", gnwc[:, :, None],
            gnw_d.rearrange("(c p) -> p c", p=128)[:, :, None], ["gn_d"], ["gnwc"],
            slow=True)
        dma("sync", pmb, pmask_d.partition_broadcast(128), ["pm_d"], ["pmb"])
        dma("sync", t32[0:32, :], t32_d, ["t32_d"], ["t32"])
        dma("sync", iotac, iotac_d, ["io_d"], ["iotac"])
        dma("sync", rowsc[0:1, 0:8], a_log[None, :], ["al_d"], ["rowsc"])
        dma("sync", rowsc[0:1, 8:16], dt_b[None, :], ["dt_d"], ["rowsc"])
        dma("sync", rowsc[0:1, 16:24], lb_b[None, :], ["lb_d"], ["rowsc"])
        op("scalar", lambda e: e.activation(rowsc[0:1, 0:8], rowsc[0:1, 0:8], AF.Exp),
           ["rowsc"], ["rowsc"])
        op("vector", lambda e: e.tensor_scalar_mul(rowsc[0:1, 0:8],
                                                   rowsc[0:1, 0:8], -1.0),
           ["rowsc"], ["rowsc"])
        op("tensor", lambda e: e.matmul(pb[0][:, 0:8], onesr[0:1, 0:128],
                                        rowsc[0:1, 0:8], start=True, stop=True),
           ["onesr", "rowsc"], [KPB[0]])
        op("vector", lambda e: e.tensor_copy(naec, pb[0][:, 0:8]), [KPB[0]], ["naec"])
        op("vector", lambda e: e.memset(S_loc, 0.0), [], ["S_loc"])
        op("vector", lambda e: e.memset(S_in, 0.0), [], ["S_in"])
        op("vector", lambda e: e.memset(cumD, 1.0), [], ["cumD"])
        dma("sync", wa_s.rearrange("p (c hh) -> p c hh", hh=H),
            wa_d.rearrange("(c p) hh -> p c hh", p=128), ["wa_d"], ["wa_s"])
        dma("sync", gw_s.rearrange("p (c hh) -> p c hh", hh=H),
            gate_wt.rearrange("(c p) hh -> p c hh", p=128), ["gwt_d"], ["gw_s"])

        ar.mark()
        csil = A(8)
        wstr = [A(8 * 512) for _ in range(2)]
        dma("sync", csil[:, :, None],
            c_in.rearrange("one (c p) -> p c", p=128)[:, :, None], ["c_d"], ["csil"],
            slow=True)
        op("scalar", lambda e: e.activation(csil, csil, AF.Silu), ["csil"], ["csil"])
        for og in range(12):
            par = og % 2
            wb = wstr[par]
            wkk = f"wstr{par}"
            dma("sync", wb.rearrange("p (c t) -> p c t", t=512),
                wmod[:, og * 512:(og + 1) * 512].rearrange("(c p) t -> p c t", p=128),
                [], [wkk])
            bank = pb[4 + par]
            for c in range(8):
                op("tensor", lambda e, c=c, wb=wb, bank=bank, f=(c == 0), l=(c == 7):
                   e.matmul(bank[0:1, 0:512], csil[:, c:c + 1],
                            wb.rearrange("p (c t) -> p c t", t=512)[:, c],
                            start=f, stop=l), ["csil", wkk], [KPB[4 + par]])
            op("vector", lambda e, og=og, bank=bank: e.tensor_copy(
                modrow[0:1, og * 512:(og + 1) * 512], bank[0:1, 0:512]),
               [KPB[4 + par]], ["modrow"])
        for w in (1, 4):
            sl = modrow[0:1, w * D:(w + 1) * D]
            op("vector", lambda e, sl=sl: e.tensor_scalar_add(sl, sl, 1.0),
               ["modrow"], ["modrow"])
        sgn = A(D)
        for w in (2, 5):
            sl = modrow[0:1, w * D:(w + 1) * D]
            m_ = sgn[0:1, 0:D]
            op("vector", lambda e, sl=sl, m_=m_: e.tensor_scalar(
                m_, sl, 0.0, None, op0=ALU.is_ge), ["modrow"], ["sgn"])
            op("vector", lambda e, m_=m_: e.tensor_scalar(
                m_, m_, 2.0, -1.0, op0=ALU.mult, op1=ALU.add), ["sgn"], ["sgn"])
            op("scalar", lambda e, sl=sl: e.activation(sl, sl, AF.Abs),
               ["modrow"], ["modrow"])
            op("vector", lambda e, sl=sl: e.tensor_scalar_max(sl, sl, 0.25),
               ["modrow"], ["modrow"])
            op("vector", lambda e, sl=sl, m_=m_: e.tensor_mul(sl, sl, m_),
               ["modrow", "sgn"], ["modrow"])
        dma("scalar", modb_d[None, :], modrow[0:1, :], ["modrow"], ["modb"])
        dma("sync", modc.rearrange("p (w c) -> p w c", c=8),
            modb_d.rearrange("(w c p) -> p w c", p=128, c=8), ["modb"], ["modc"],
            slow=True)
        ar.release()

        tk.barrier()

        ar.mark()
        xt = [A(8 * CT) for _ in range(2)]
        ht = A(8 * TILE)
        wst = [A(512) for _ in range(2)]
        rbuf = A(D)
        q_tok = A(SPT * D)
        k_tok = A(SPT * D)
        v_tok = A(SPT * D)
        r_tok = A(SPT * D)
        f_t = [A(D) for _ in range(SPT)]
        lam = A(D)
        sqsc = A(D)
        sqsc2 = A(2 * CT)
        acol = A(SPT * H)
        fhead = A(SPT * H)
        rqk = A(SPT * 2 * H)
        mx8 = A(8)
        ztop = A(128); wsel = A(128)
        wsum = A(H)
        ppt_s = A(D)
        eqs_s = A(D)
        qfm_s = A(D)
        kfm_s = A(D)
        ams_s = A(D)
        stv_s = A(D)
        dex = A(24)
        gkn = A(8); qcf = A(8)
        dstage = A(16)
        o_sub = A(H * CT)
        qc_sub = A(H * CT)

        WLIST = ((wr, "r", None, None, AF.Sigmoid),
                 (wq, "q", q_tok, "q_tok", AF.Silu),
                 (wk, "k", k_tok, "k_tok", AF.Silu),
                 (wv, "v", v_tok, "v_tok", AF.Silu))

        hvw = ht.rearrange("p (c tt) -> p c tt", tt=TILE)

        def ln1(t2, s):
            tok2 = t2 * TILE
            xpar = (t2 * SPT + s) % 2
            xb = xt[xpar]
            xk = f"xt{xpar}"
            dma("sync", xb.rearrange("p (c t2) -> p c t2", t2=CT),
                xin[:, tok2 + s * CT: tok2 + (s + 1) * CT]
                .rearrange("(c p) t2 -> p c t2", p=128), [], [xk])
            xv = xb.rearrange("p (c t2) -> p c t2", t2=CT)
            for c in range(8):
                op("tensor", lambda e, c=c, xv=xv, f=(c == 0), l=(c == 7):
                   e.matmul(pb[2][0:1, 0:CT], colsel[:, 0:1], xv[:, c],
                            start=f, stop=l), [xk, "colsel"], [KPB[2]])
            for c in range(8):
                cp2 = c % 2
                op("scalar", lambda e, c=c, xv=xv, cp2=cp2: e.activation(
                    sqsc2[:, cp2 * CT:(cp2 + 1) * CT], xv[:, c], AF.Square),
                   [xk], [f"sq2{cp2}"])
                op("tensor", lambda e, c=c, cp2=cp2, f=(c == 0), l=(c == 7):
                   e.matmul(pb[3][0:1, 0:CT], colsel[:, 0:1],
                            sqsc2[:, cp2 * CT:(cp2 + 1) * CT], start=f, stop=l),
                   [f"sq2{cp2}", "colsel"], [KPB[3]])
            mrow = prow[0:1, 0:CT]
            rrow = prow[0:1, 512:512 + CT]
            mrr = prow[0:1, 1024:1024 + CT]
            op("vector", lambda e, mrow=mrow: e.tensor_scalar_mul(
                mrow, pb[2][0:1, 0:CT], 1.0 / D), [KPB[2]], ["prow"])
            op("vector", lambda e, rrow=rrow, mrow=mrow: e.tensor_mul(
                rrow, mrow, mrow), ["prow"], ["prow"])
            op("vector", lambda e, rrow=rrow: e.scalar_tensor_tensor(
                rrow, pb[3][0:1, 0:CT], 1.0 / D, rrow,
                op0=ALU.mult, op1=ALU.subtract), [KPB[3], "prow"], ["prow"])
            op("scalar", lambda e, rrow=rrow: e.activation(
                rrow, rrow, AF.Ln, bias=EPS), ["prow"], ["prow"])
            op("scalar", lambda e, rrow=rrow: e.activation(
                rrow, rrow, AF.Exp, scale=-0.5), ["prow"], ["prow"])
            op("vector", lambda e, mrr=mrr, mrow=mrow, rrow=rrow: e.tensor_mul(
                mrr, mrow, rrow), ["prow"], ["prow"])
            for c in range(8):
                op("tensor", lambda e, c=c, rrow=rrow: e.matmul(
                    pb[2][:, 0:CT], modrow_sl(1, c), rrow, start=True, stop=True),
                   ["modrow", "prow"], [KPB[2]])
                op("tensor", lambda e, c=c, mrr=mrr: e.matmul(
                    pb[3][:, 0:CT], modrow_sl(1, c), mrr, start=True, stop=False),
                   ["modrow", "prow"], [KPB[3]])
                op("tensor", lambda e, c=c: e.matmul(
                    pb[3][:, 0:CT], modrow_sl(0, c), negr[0:1, 0:CT],
                    start=False, stop=True), ["modrow", "negr"], [KPB[3]])
                hc = hvw[:, c, s * CT:(s + 1) * CT]
                op("vector", lambda e, hc=hc, xv=xv, c=c: e.tensor_mul(
                    hc, xv[:, c], pb[2][:, 0:CT]), [xk, KPB[2]], ["ht"])
                op("vector", lambda e, hc=hc: e.tensor_sub(
                    hc, hc, pb[3][:, 0:CT]), ["ht", KPB[3]], ["ht"])

        for s in range(SPT):
            ln1(0, s)

        for t in range(NT):
            tok0 = t * TILE
            rbufk = "r_tok"

            def proj_one(wd, dstt, dkey, func):
                for half in range(2):
                    for c in range(8):
                        par = c % 2
                        wb = wst[par]
                        wkk = f"wst{par}"
                        dma("sync", wb,
                            wd[c * 128:(c + 1) * 128, half * 512:(half + 1) * 512],
                            [], [wkk])
                        for s in range(SPT):
                            op("tensor",
                               lambda e, c=c, s=s, wb=wb, f=(c == 0), l=(c == 7):
                               e.matmul(pb[4 + s][:, 0:512],
                                        hvw[:, c, s * CT:(s + 1) * CT], wb,
                                        start=f, stop=l),
                               ["ht", wkk], [KPB[4 + s]])
                    for s in range(SPT):
                        dv = dstt.rearrange("p (s f) -> p s f", f=D)[
                            :, s, half * 512:(half + 1) * 512]
                        op("scalar", lambda e, dv=dv, s=s, func=func:
                           e.activation(dv, pb[4 + s][:, 0:512], func),
                           [KPB[4 + s]], [dkey])

            proj_one(wr, r_tok, rbufk, AF.Sigmoid)

            for s in range(SPT):
                for c in range(8):
                    op("tensor", lambda e, c=c, s=s, f=(c == 0):
                       e.matmul(pb[2][:, 0:H], hvw[:, c, s * CT:(s + 1) * CT],
                                wa_s.rearrange("p (c hh) -> p c hh", hh=H)[:, c],
                                start=f, stop=False), ["ht", "wa_s"], [KPB[2]])
                op("tensor", lambda e: e.matmul(
                    pb[2][:, 0:H], onesr[0:1, 0:128], rowsc[0:1, 8:16],
                    start=False, stop=True), ["onesr", "rowsc"], [KPB[2]])
                op("vector", lambda e, s=s: e.tensor_copy(
                    acol[:, s * H:(s + 1) * H], pb[2][:, 0:H]), [KPB[2]], ["acol"])

            nsh = SPT * H
            op("scalar", lambda e: e.activation(fhead[:, 0:nsh], acol, AF.Abs),
               ["acol"], ["fhead"])
            op("scalar", lambda e: e.activation(fhead[:, 0:nsh], fhead[:, 0:nsh],
                                                AF.Exp, scale=-1.0),
               ["fhead"], ["fhead"])
            op("scalar", lambda e: e.activation(fhead[:, 0:nsh], fhead[:, 0:nsh],
                                                AF.Ln, bias=1.0), ["fhead"], ["fhead"])
            op("vector", lambda e: e.tensor_scalar_max(sqsc[:, 0:nsh], acol, 0.0),
               ["acol"], ["sqsc"])
            op("vector", lambda e: e.tensor_add(fhead[:, 0:nsh], fhead[:, 0:nsh],
                                                sqsc[:, 0:nsh]),
               ["fhead", "sqsc"], ["fhead"])
            op("vector", lambda e: e.tensor_tensor(
                fhead[:, 0:nsh].rearrange("p (s hh) -> p s hh", hh=H),
                fhead[:, 0:nsh].rearrange("p (s hh) -> p s hh", hh=H),
                naec.to_broadcast([128, H, SPT]).rearrange("p hh s -> p s hh"),
                ALU.mult), ["fhead", "naec"], ["fhead"])

            for s in range(SPT):
                fb = f_t[s]
                fbk = f"f_t{s}"
                for hh in range(H):
                    sc_sl = r_tok.rearrange("p (s f) -> p s f", f=D)[
                        :, s, hh * DK:(hh + 1) * DK]
                    op("vector", lambda e, sc_sl=sc_sl: e.tensor_copy(ztop, sc_sl),
                       [rbufk], ["ztop"])
                    for _ in range(4):
                        op("vector", lambda e: e.max(mx8, ztop), ["ztop"], ["mx8"])
                        op("vector", lambda e: e.match_replace(ztop, mx8, ztop, NEG),
                           ["ztop", "mx8"], ["ztop"])
                    op("vector", lambda e, sc_sl=sc_sl: e.tensor_tensor(
                        wsel, sc_sl, ztop, ALU.is_gt), [rbufk, "ztop"], ["wsel"])
                    op("vector", lambda e, sc_sl=sc_sl: e.tensor_mul(
                        wsel, wsel, sc_sl), ["wsel", rbufk], ["wsel"])
                    wcol = wsum[:, hh:hh + 1]
                    op("vector", lambda e, wcol=wcol: e.reduce_sum(
                        wcol, wsel, axis=AX.X), ["wsel"], ["wsum"])
                    op("vector", lambda e, wcol=wcol: e.tensor_scalar_add(
                        wcol, wcol, 1e-9), ["wsum"], ["wsum"])
                    op("vector", lambda e, wcol=wcol: e.reciprocal(wcol, wcol),
                       ["wsum"], ["wsum"])
                    fsl = fb[:, hh * DK:(hh + 1) * DK]
                    op("vector", lambda e, fsl=fsl, wcol=wcol:
                       e.tensor_scalar(fsl, wsel, wcol, None, op0=ALU.mult),
                       ["wsel", "wsum"], [fbk])

            proj_one(wq, q_tok, "q_tok", AF.Silu)
            proj_one(wk, k_tok, "k_tok", AF.Silu)
            proj_one(wv, v_tok, "v_tok", AF.Silu)

            for s in range(SPT):
                qv = q_tok.rearrange("p (s f) -> p s f", f=D)[:, s]
                kv = k_tok.rearrange("p (s f) -> p s f", f=D)[:, s]
                op("scalar", lambda e, qv=qv: e.activation(sqsc[:, 0:D], qv, AF.Square),
                   ["q_tok"], ["sqsc"])
                op("vector", lambda e, s=s: e.reduce_sum(
                    rqk[:, s * 16:s * 16 + 8],
                    sqsc[:, 0:D].rearrange("p (hh kk) -> p hh kk", kk=DK),
                    axis=AX.X), ["sqsc"], ["rqk"])
                op("scalar", lambda e, kv=kv: e.activation(sqsc[:, 0:D], kv, AF.Square),
                   ["k_tok"], ["sqsc"])
                op("vector", lambda e, s=s: e.reduce_sum(
                    rqk[:, s * 16 + 8:s * 16 + 16],
                    sqsc[:, 0:D].rearrange("p (hh kk) -> p hh kk", kk=DK),
                    axis=AX.X), ["sqsc"], ["rqk"])

            for s in range(SPT):
                sl = rqk[:, s * 16:(s + 1) * 16]
                op("vector", lambda e, sl=sl: e.tensor_scalar(
                    sl, sl, 1.0 / DK, EPS, op0=ALU.mult, op1=ALU.add),
                   ["rqk"], ["rqk"])
                op("scalar", lambda e, sl=sl: e.activation(sl, sl, AF.Ln),
                   ["rqk"], ["rqk"])
                op("scalar", lambda e, sl=sl: e.activation(sl, sl, AF.Exp, scale=-0.5),
                   ["rqk"], ["rqk"])
            for s in range(SPT):
                qv = q_tok.rearrange("p (s f) -> p s f", f=D)[:, s].rearrange(
                    "p (hh kk) -> p hh kk", kk=DK)
                kv = k_tok.rearrange("p (s f) -> p s f", f=D)[:, s].rearrange(
                    "p (hh kk) -> p hh kk", kk=DK)
                rq = rqk[:, s * 16:s * 16 + 8]
                rk = rqk[:, s * 16 + 8:s * 16 + 16]
                op("vector", lambda e, qv=qv, rq=rq: e.tensor_tensor(
                    qv, qv, rq.to_broadcast([128, 8, DK]), ALU.mult),
                   ["q_tok", "rqk"], ["q_tok"])
                op("vector", lambda e, kv=kv, rk=rk: e.tensor_tensor(
                    kv, kv, rk.to_broadcast([128, 8, DK]), ALU.mult),
                   ["k_tok", "rqk"], ["k_tok"])

            for s in range(SPT):
                fb = f_t[s]
                fbk = f"f_t{s}"
                op("vector", lambda e, fb=fb, s=s: e.tensor_tensor(
                    fb.rearrange("p (hh kk) -> p hh kk", kk=DK),
                    fb.rearrange("p (hh kk) -> p hh kk", kk=DK),
                    fhead[:, s * H:(s + 1) * H].to_broadcast([128, H, DK]),
                    ALU.mult), [fbk, "fhead"], [fbk])
                op("scalar", lambda e, fb=fb: e.activation(
                    lam[:, 0:D], fb, AF.Exp), [fbk], ["lam"])
                op("vector", lambda e: e.tensor_scalar(
                    lam[:, 0:D], lam[:, 0:D], -1.0, 1.0, op0=ALU.mult, op1=ALU.add),
                   ["lam"], ["lam"])
                kv = k_tok.rearrange("p (s f) -> p s f", f=D)[:, s]
                op("vector", lambda e, kv=kv: e.tensor_mul(kv, kv, lam[:, 0:D]),
                   ["k_tok", "lam"], ["k_tok"])
                sp = s % 2
                bX, bY = pb[4 + 2 * sp], pb[5 + 2 * sp]
                kX, kY = KPB[4 + 2 * sp], KPB[5 + 2 * sp]
                fv = fb.rearrange("p (hh kk) -> p hh kk", kk=DK)
                q_s = q_tok.rearrange("p (s f) -> p s f", f=D)[:, s]
                k_s = k_tok.rearrange("p (s f) -> p s f", f=D)[:, s]
                v_s = v_tok.rearrange("p (s f) -> p s f", f=D)[:, s]
                v_sv = v_s.rearrange("p (hh kk) -> p hh kk", kk=DK)
                for half, (bk, kk) in enumerate(((bX, kX), (bY, kY))):
                    op("tensor", lambda e, bk=bk, half=half, fb=fb: e.matmul(
                        bk[0:1, 0:512], colsel[:, 1:2],
                        fb[:, half * 512:(half + 1) * 512],
                        start=True, stop=True), [fbk, "colsel"], [kk])
                    op("vector", lambda e, bk=bk, half=half: e.tensor_copy(
                        rbuf[0:1, half * 512:(half + 1) * 512], bk[0:1, 0:512]),
                       [kk], ["rbuf"])
                for hh in range(H):
                    op("tensor", lambda e, hh=hh, fv=fv, bX=bX: e.matmul(
                        bX[:, hh * 2:hh * 2 + 2], fv[:, hh], colsel,
                        start=(hh == 0), stop=(hh == 7)), [fbk, "colsel"], [kX])
                op("vector", lambda e, bX=bX: e.tensor_copy(dstage, bX[:, 0:16]),
                   [kX], ["dstage"])
                ds2 = dstage.rearrange("p (hh two) -> p hh two", two=2)
                op("vector", lambda e: e.tensor_copy(dex[:, 0:8], ds2[:, :, 0]),
                   ["dstage"], ["dex"])
                op("vector", lambda e: e.tensor_copy(dex[:, 16:24], ds2[:, :, 1]),
                   ["dstage"], ["dex"])
                op("vector", lambda e: e.tensor_sub(dex[:, 8:16], dex[:, 0:8],
                                                    dex[:, 16:24]), ["dex"], ["dex"])
                op("scalar", lambda e: e.activation(dex[:, 0:24], dex[:, 0:24],
                                                    AF.Exp), ["dex"], ["dex"])
                op("vector", lambda e: e.tensor_scalar(
                    gkn, dex[:, 8:16], qknw[:, 1:2], None, op0=ALU.mult),
                   ["dex", "qknw"], ["gkn"])
                op("vector", lambda e: e.tensor_mul(qcf, dex[:, 16:24], cumD),
                   ["dex", "cumD"], ["qcf"])
                for half, (bk, kk) in enumerate(((bX, kX), (bY, kY))):
                    op("tensor", lambda e, bk=bk, half=half, fb=fb:
                       e.matmul(bk[:, 0:512], tri,
                                fb[:, half * 512:(half + 1) * 512],
                                start=True, stop=False), [fbk, "tri"], [kk])
                    op("tensor", lambda e, bk=bk, half=half:
                       e.matmul(bk[:, 0:512], negr[0:1, 0:128],
                                rbuf[0:1, half * 512:(half + 1) * 512],
                                start=False, stop=True), ["negr", "rbuf"], [kk])
                    op("vector", lambda e, bk=bk, half=half: e.tensor_scalar(
                        ppt_s[:, half * 512:(half + 1) * 512], bk[:, 0:512],
                        CLAMP, -CLAMP, op0=ALU.min, op1=ALU.max), [kk], ["ppt_s"])
                op("scalar", lambda e: e.activation(eqs_s, ppt_s, AF.Exp),
                   ["ppt_s"], ["eqs_s"])
                op("scalar", lambda e: e.activation(ppt_s, ppt_s, AF.Exp, scale=-1.0),
                   ["ppt_s"], ["ppt_s"])
                op("vector", lambda e, q_s=q_s: e.tensor_mul(eqs_s, q_s, eqs_s),
                   ["q_tok", "eqs_s"], ["eqs_s"])
                op("vector", lambda e, k_s=k_s: e.tensor_mul(ppt_s, k_s, ppt_s),
                   ["k_tok", "ppt_s"], ["ppt_s"])
                for src_t, srck, dst_t, dstk, wcol in (
                        (eqs_s, "eqs_s", qfm_s, "qfm_s", qknw[:, 0:1]),
                        (ppt_s, "ppt_s", kfm_s, "kfm_s", qknw[:, 1:2])):
                    for half, (bk, kk) in enumerate(((bX, kX), (bY, kY))):
                        for j in range(4):
                            hh = half * 4 + j
                            op("tensor", lambda e, bk=bk, j=j, hh=hh, src_t=src_t,
                               f=(j == 0), l=(j == 3):
                               e.matmul(bk[:, j * 128:(j + 1) * 128],
                                        src_t[:, hh * 128:(hh + 1) * 128],
                                        ident, is_transpose=True,
                                        start=f, stop=l), [srck, "ident"], [kk])
                        op("scalar", lambda e, bk=bk, half=half, dst_t=dst_t,
                           wcol=wcol: e.activation(
                               dst_t[:, half * 512:(half + 1) * 512], bk[:, 0:512],
                               AF.Copy, scale=wcol), [kk, "qknw"], [dstk])
                for half, (bk, kk) in enumerate(((bX, kX), (bY, kY))):
                    for j in range(4):
                        hh = half * 4 + j
                        op("tensor", lambda e, bk=bk, j=j, hh=hh, f=(j == 0),
                           l=(j == 3): e.matmul(
                               bk[:, j * 128:(j + 1) * 128],
                               kfm_s[:, hh * 128:(hh + 1) * 128],
                               qfm_s[:, hh * 128:(hh + 1) * 128],
                               start=f, stop=l), ["kfm_s", "qfm_s"], [kk])
                    op("vector", lambda e, bk=bk, half=half: e.tensor_mul(
                        ams_s[:, half * 512:(half + 1) * 512], bk[:, 0:512], tri4),
                       [kk, "tri4"], ["ams_s"])
                op("vector", lambda e: e.tensor_tensor(
                    stv_s.rearrange("p (hh vv) -> p hh vv", vv=DK),
                    S_loc.rearrange("p (hh vv) -> p hh vv", vv=DK),
                    dex[:, 16:24].to_broadcast([128, 8, DK]), ALU.mult),
                   ["S_loc", "dex"], ["stv_s"])
                for half, (bk, kk) in enumerate(((bX, kX), (bY, kY))):
                    for j in range(4):
                        hh = half * 4 + j
                        op("tensor", lambda e, bk=bk, j=j, hh=hh, v_sv=v_sv,
                           f=(j == 0):
                           e.matmul(bk[:, j * 128:(j + 1) * 128], v_sv[:, hh],
                                    ams_s[:, hh * 128:(hh + 1) * 128],
                                    start=f, stop=False),
                           ["v_tok", "ams_s"], [kk])
                        op("tensor", lambda e, bk=bk, j=j, hh=hh, l=(j == 3):
                           e.matmul(bk[:, j * 128:(j + 1) * 128],
                                    stv_s[:, hh * 128:(hh + 1) * 128],
                                    qfm_s[:, hh * 128:(hh + 1) * 128],
                                    start=False, stop=l),
                           ["stv_s", "qfm_s"], [kk])
                    eng_e = "vector" if half == 0 else "scalar"
                    if half == 0:
                        op("vector", lambda e, bk=bk: e.tensor_copy(
                            o_sub[:, 0:512], bk[:, 0:512]), [kk], ["o_sub"])
                    else:
                        op("scalar", lambda e, bk=bk: e.copy(
                            o_sub[:, 512:1024], bk[:, 0:512]), [kk], ["o_sub"])
                for half, (bk, kk) in enumerate(((bX, kX), (bY, kY))):
                    for j in range(4):
                        hh = half * 4 + j
                        op("tensor", lambda e, bk=bk, j=j, hh=hh, v_sv=v_sv,
                           f=(j == 0),
                           l=(j == 3): e.matmul(
                               bk[:, j * 128:(j + 1) * 128],
                               ppt_s[:, hh * 128:(hh + 1) * 128], v_sv[:, hh],
                               start=f, stop=l), ["ppt_s", "v_tok"], [kk])
                    op("vector", lambda e, bk=bk, half=half: e.tensor_tensor(
                        stv_s.rearrange("p (hh vv) -> p hh vv", vv=DK)[
                            :, half * 4:(half + 1) * 4],
                        bk[:, 0:512].rearrange("p (j vv) -> p j vv", vv=DK),
                        gkn[:, half * 4:(half + 1) * 4].to_broadcast([128, 4, DK]),
                        ALU.mult), [kk, "gkn"], ["stv_s"])
                op("vector", lambda e: e.tensor_tensor(
                    qc_sub.rearrange("p (hh i) -> p hh i", i=CT),
                    qfm_s.rearrange("p (hh i) -> p hh i", i=CT),
                    qcf.to_broadcast([128, 8, CT]), ALU.mult),
                   ["qfm_s", "qcf"], ["qc_sub"])
                op("vector", lambda e: e.tensor_tensor(
                    S_loc.rearrange("p (hh vv) -> p hh vv", vv=DK),
                    S_loc.rearrange("p (hh vv) -> p hh vv", vv=DK),
                    dex[:, 0:8].to_broadcast([128, 8, DK]), ALU.mult),
                   ["S_loc", "dex"], ["S_loc"])
                op("vector", lambda e: e.tensor_add(S_loc, S_loc, stv_s),
                   ["S_loc", "stv_s"], ["S_loc"])
                op("vector", lambda e: e.tensor_mul(cumD, cumD, dex[:, 0:8]),
                   ["cumD", "dex"], ["cumD"])
                sub0 = tok0 + s * CT
                dma("scalar", o_sp.rearrange("hh v tt -> v hh tt")[:, :, sub0:sub0 + CT],
                    o_sub.rearrange("p (hh i) -> p hh i", i=CT), ["o_sub"], ["o_sp_d"])
                dma("scalar", qc_sp.rearrange("hh k tt -> k hh tt")[:, :, sub0:sub0 + CT],
                    qc_sub.rearrange("p (hh i) -> p hh i", i=CT), ["qc_sub"], ["qc_sp_d"])
                if t + 1 < NT:
                    ln1(t + 1, s)

        dma("gpsimd", ag_in[0:H * DK * DK].rearrange("(hh k c) -> k hh c",
                                                      k=DK, c=DK),
            S_loc.rearrange("p (hh vv) -> p hh vv", vv=DK), ["S_loc"], ["ag_in_d"])
        dma("gpsimd", ag_in[H * DK * DK:].rearrange("(k hh) -> k hh", hh=H),
            cumD, ["cumD"], ["ag_in_d"])
        tk.add("gpsimd", lambda e: e.collective_compute(
            "AllGather", ALU.bypass, replica_groups=[list(range(NCORE))],
            ins=[ag_in], outs=[ag_out]),
            reads=["ag_in_d"], writes=["ag_out_d"], cc=True)
        ar.mark()
        gS = A(H * DK)
        gD = A(H)
        op("vector", lambda e: e.memset(cumc, 1.0), [], ["cumc"])
        AGW = H * DK * DK + H * DK
        for j in range(NCORE - 1, -1, -1):
            base = j * AGW
            dma("sync", gS.rearrange("p (hh vv) -> p hh vv", vv=DK),
                ag_out[base:base + H * DK * DK]
                .rearrange("(hh k c) -> k hh c", k=DK, c=DK),
                ["ag_out_d"], ["gS"])
            dma("sync", gD,
                ag_out[base + H * DK * DK:base + AGW]
                .rearrange("(k hh) -> k hh", hh=H), ["ag_out_d"], ["gD"])
            pmj = pmb[:, j:j + 1]
            op("vector", lambda e, pmj=pmj: e.tensor_scalar(
                stv_s[:, 0:H], cumc[:, 0:H], pmj, None, op0=ALU.mult),
               ["pmb", "cumc"], ["stv_s"])
            op("vector", lambda e: e.tensor_tensor(
                ppt_s.rearrange("p (hh vv) -> p hh vv", vv=DK),
                gS.rearrange("p (hh vv) -> p hh vv", vv=DK),
                stv_s[:, 0:H].to_broadcast([128, H, DK]), ALU.mult),
               ["gS", "stv_s"], ["ppt_s"])
            op("vector", lambda e: e.tensor_add(S_in, S_in, ppt_s),
               ["S_in", "ppt_s"], ["S_in"])
            op("vector", lambda e: e.tensor_scalar_add(
                stv_s[:, 16:16 + H], gD[:, 0:H], -1.0), ["gD"], ["stv_s"])
            op("vector", lambda e, pmj=pmj: e.tensor_scalar(
                stv_s[:, 16:16 + H], stv_s[:, 16:16 + H], pmj, 1.0,
                op0=ALU.mult, op1=ALU.add), ["stv_s", "pmb"], ["stv_s"])
            op("vector", lambda e: e.tensor_mul(
                cumc[:, 0:H], cumc[:, 0:H], stv_s[:, 16:16 + H]),
               ["cumc", "stv_s"], ["cumc"])
        ar.release()

        tk.barrier()
        ar.release()

        ar.mark()
        o2 = A(H * 512)
        qc2 = A(H * 512)
        xtile = A(8 * 512)
        x1t = A(8 * 512)
        h2t = A(8 * 512)
        httok = [A(DW) for _ in range(2)]
        wo_s = A(8 * D)
        osq = [A(512) for _ in range(2)]
        gsc = A(8)
        mx8g = A(8)
        selw = A(16)
        ilv = A(16)
        dma("sync", wo_s.rearrange("p (c f) -> p c f", f=D),
            wo_d.rearrange("(c p) f -> p c f", p=128), [], ["wo_s"])

        for t in range(NT):
            tok0 = t * TILE
            dma("sync", o2.rearrange("p (hh tt) -> p hh tt", tt=512),
                o_sp.rearrange("hh v tt -> v hh tt")[:, :, tok0:tok0 + TILE],
                ["o_sp_d"], ["o2"])
            dma("sync", qc2.rearrange("p (hh tt) -> p hh tt", tt=512),
                qc_sp.rearrange("hh k tt -> k hh tt")[:, :, tok0:tok0 + TILE],
                ["qc_sp_d"], ["qc2"])
            dma("sync", xtile.rearrange("p (c tt) -> p c tt", tt=512),
                xin[:, tok0:tok0 + TILE].rearrange("(c p) tt -> p c tt", p=128),
                [], ["xtile"])
            for hh in range(H):
                par = hh % 2
                Ssl = S_in.rearrange("p (hh vv) -> p hh vv", vv=DK)[:, hh]
                qcs = qc2.rearrange("p (hh tt) -> p hh tt", tt=512)[:, hh]
                osl = o2.rearrange("p (hh tt) -> p hh tt", tt=512)[:, hh]
                op("tensor", lambda e, Ssl=Ssl, qcs=qcs, par=par: e.matmul(
                    pb[par][:, 0:512], Ssl, qcs, start=True, stop=True),
                   ["S_in", "qc2"], [KPB[par]])
                op("vector", lambda e, osl=osl, par=par: e.tensor_add(
                    osl, osl, pb[par][:, 0:512]), ["o2", KPB[par]], ["o2"])
            for hh in range(H):
                par = hh % 2
                osl = o2.rearrange("p (hh tt) -> p hh tt", tt=512)[:, hh]
                op("scalar", lambda e, osl=osl, par=par: e.activation(
                    osq[par], osl, AF.Square), ["o2"], [f"osq{par}"])
                op("tensor", lambda e, par=par, f=(hh == 0), l=(hh == 7): e.matmul(
                    pb[2][0:1, 0:512], colsel[:, 0:1], osq[par],
                    start=f, stop=l), [f"osq{par}", "colsel"], [KPB[2]])
            rr2 = prow[0:1, 0:512]
            op("vector", lambda e: e.tensor_scalar(
                rr2, pb[2][0:1, 0:512], 1.0 / D, EPS, op0=ALU.mult, op1=ALU.add),
               [KPB[2]], ["prow"])
            op("scalar", lambda e: e.activation(rr2, rr2, AF.Ln), ["prow"], ["prow"])
            op("scalar", lambda e: e.activation(rr2, rr2, AF.Exp, scale=-0.5),
               ["prow"], ["prow"])
            op("tensor", lambda e: e.matmul(
                pb[3][:, 0:512], onesr[0:1, 0:128], rr2, start=True, stop=True),
               ["prow", "onesr"], [KPB[3]])
            for hh in range(H):
                osl = o2.rearrange("p (hh tt) -> p hh tt", tt=512)[:, hh]
                op("vector", lambda e, osl=osl, hh=hh: e.scalar_tensor_tensor(
                    osl, osl, gnwc[:, hh:hh + 1], pb[3][:, 0:512],
                    op0=ALU.mult, op1=ALU.mult), ["o2", "gnwc", KPB[3]], ["o2"])
            for c in range(8):
                par = c % 2
                for hv in range(8):
                    op("tensor", lambda e, c=c, hv=hv, par=par, f=(hv == 0), l=(hv == 7):
                       e.matmul(pb[4 + par][:, 0:512],
                                wo_s.rearrange("p (cc f) -> p cc f", f=D)[
                                    :, hv, c * 128:(c + 1) * 128],
                                o2.rearrange("p (hh tt) -> p hh tt", tt=512)[:, hv],
                                start=f, stop=l), ["wo_s", "o2"], [KPB[4 + par]])
                x1c = x1t.rearrange("p (c tt) -> p c tt", tt=512)[:, c]
                xc = xtile.rearrange("p (c tt) -> p c tt", tt=512)[:, c]
                op("vector", lambda e, x1c=x1c, xc=xc, c=c, par=par:
                   e.scalar_tensor_tensor(x1c, pb[4 + par][:, 0:512], modcol(2, c),
                                          xc, op0=ALU.mult, op1=ALU.add),
                   [KPB[4 + par], "modc", "xtile"], ["x1t"])
            dma("scalar", x1_sp.rearrange("c p tt -> p c tt")[:, :, tok0:tok0 + TILE],
                x1t.rearrange("p (c tt) -> p c tt", tt=512), ["x1t"], ["x1_sp_d"])
            x1v = x1t.rearrange("p (c tt) -> p c tt", tt=512)
            for c in range(8):
                op("tensor", lambda e, c=c, x1v=x1v, f=(c == 0), l=(c == 7):
                   e.matmul(pb[2][0:1, 0:512], colsel[:, 0:1], x1v[:, c],
                            start=f, stop=l), ["x1t", "colsel"], [KPB[2]])
            for c in range(8):
                par = c % 2
                op("scalar", lambda e, c=c, x1v=x1v, par=par: e.activation(
                    osq[par], x1v[:, c], AF.Square), ["x1t"], [f"osq{par}"])
                op("tensor", lambda e, c=c, par=par, f=(c == 0), l=(c == 7):
                   e.matmul(pb[3][0:1, 0:512], colsel[:, 0:1], osq[par],
                            start=f, stop=l), [f"osq{par}", "colsel"], [KPB[3]])
            m2 = prow[0:1, 0:512]; r2 = prow[0:1, 512:1024]; mr2 = prow[0:1, 1024:1536]
            op("vector", lambda e: e.tensor_scalar_mul(m2, pb[2][0:1, 0:512], 1.0 / D),
               [KPB[2]], ["prow"])
            op("vector", lambda e: e.tensor_mul(r2, m2, m2), ["prow"], ["prow"])
            op("vector", lambda e: e.scalar_tensor_tensor(
                r2, pb[3][0:1, 0:512], 1.0 / D, r2, op0=ALU.mult, op1=ALU.subtract),
               [KPB[3], "prow"], ["prow"])
            op("scalar", lambda e: e.activation(r2, r2, AF.Ln, bias=EPS),
               ["prow"], ["prow"])
            op("scalar", lambda e: e.activation(r2, r2, AF.Exp, scale=-0.5),
               ["prow"], ["prow"])
            op("vector", lambda e: e.tensor_mul(mr2, m2, r2), ["prow"], ["prow"])
            for c in range(8):
                op("tensor", lambda e, c=c: e.matmul(
                    pb[2][:, 0:512], modrow_sl(4, c), r2, start=True, stop=True),
                   ["modrow", "prow"], [KPB[2]])
                op("tensor", lambda e, c=c: e.matmul(
                    pb[3][:, 0:512], modrow_sl(4, c), mr2, start=True, stop=False),
                   ["modrow", "prow"], [KPB[3]])
                op("tensor", lambda e, c=c: e.matmul(
                    pb[3][:, 0:512], modrow_sl(3, c), negr[0:1, 0:512],
                    start=False, stop=True), ["modrow", "negr"], [KPB[3]])
                h2c = h2t.rearrange("p (c tt) -> p c tt", tt=512)[:, c]
                op("vector", lambda e, h2c=h2c, c=c, x1v=x1v: e.tensor_mul(
                    h2c, x1v[:, c], pb[2][:, 0:512]), ["x1t", KPB[2]], ["h2t"])
                op("vector", lambda e, h2c=h2c: e.tensor_sub(
                    h2c, h2c, pb[3][:, 0:512]), ["h2t", KPB[3]], ["h2t"])
            dma("scalar", h2_sp.rearrange("c p tt -> p c tt")[:, :, tok0:tok0 + TILE],
                h2t.rearrange("p (c tt) -> p c tt", tt=512), ["h2t"], ["h2_sp_d"])
            h2vv = h2t.rearrange("p (c tt) -> p c tt", tt=512)
            for s in range(SPT):
                hpar = s % 2
                hb = httok[hpar]
                hbk = f"httok{hpar}"
                for c in range(8):
                    tp = c % 2
                    op("tensor", lambda e, c=c, s=s, tp=tp, h2vv=h2vv: e.matmul(
                        pb[tp][:, 0:128], h2vv[:, c, s * CT:(s + 1) * CT], ident,
                        is_transpose=True, start=True, stop=True),
                       ["h2t", "ident"], [KPB[tp]])
                    op("scalar", lambda e, c=c, tp=tp, hb=hb: e.copy(
                        hb[:, c * 128:(c + 1) * 128], pb[tp][:, 0:128]),
                       [KPB[tp]], [hbk])
                sub0 = tok0 + s * CT
                op("vector", lambda e, hb=hb, sub0=sub0: e.tensor_scalar_add(
                    hb[:, D:D + 1], iotac, float(sub0)), ["iotac"], [hbk])
                dma("scalar", h2tok_d[sub0:sub0 + CT, :], hb, [hbk], ["h2tok_dd"])
            h2v = h2t.rearrange("p (c tt) -> p c tt", tt=512)
            for s in range(SPT):
                for c in range(8):
                    op("tensor", lambda e, c=c, s=s, h2v=h2v, f=(c == 0):
                       e.matmul(pb[6][:, 0:H], h2v[:, c, s * CT:(s + 1) * CT],
                                gw_s.rearrange("p (c hh) -> p c hh", hh=H)[:, c],
                                start=f, stop=False), ["h2t", "gw_s"], [KPB[6]])
                op("tensor", lambda e: e.matmul(
                    pb[6][:, 0:H], onesr[0:1, 0:128], rowsc[0:1, 16:24],
                    start=False, stop=True), ["onesr", "rowsc"], [KPB[6]])
                op("scalar", lambda e: e.activation(gsc, pb[6][:, 0:H], AF.Sigmoid),
                   [KPB[6]], ["gsc"])
                op("vector", lambda e: e.max(mx8g, gsc), ["gsc"], ["mx8g"])
                op("vector", lambda e: e.tensor_scalar(
                    selw[:, 0:8], gsc, mx8g[:, 1:2], None, op0=ALU.is_ge),
                   ["gsc", "mx8g"], ["selw"])
                op("vector", lambda e: e.tensor_add(
                    selw[:, 8:9], mx8g[:, 0:1], mx8g[:, 1:2]), ["mx8g"], ["selw"])
                op("vector", lambda e: e.tensor_scalar_add(
                    selw[:, 8:9], selw[:, 8:9], 1e-8), ["selw"], ["selw"])
                op("vector", lambda e: e.reciprocal(selw[:, 9:10], selw[:, 8:9]),
                   ["selw"], ["selw"])
                ilv2 = ilv.rearrange("p (ee two) -> p ee two", two=2)
                op("vector", lambda e, ilv2=ilv2: e.tensor_mul(
                    ilv2[:, :, 0], selw[:, 0:8], gsc), ["selw", "gsc"], ["ilv"])
                op("vector", lambda e, ilv2=ilv2: e.tensor_scalar(
                    ilv2[:, :, 0], ilv2[:, :, 0], selw[:, 9:10], None, op0=ALU.mult),
                   ["ilv", "selw"], ["ilv"])
                op("vector", lambda e, ilv2=ilv2: e.tensor_copy(
                    ilv2[:, :, 1], selw[:, 0:8]), ["selw"], ["ilv"])
                fs = t * SPT + s
                op("vector", lambda e, fs=fs: e.tensor_copy(
                    selcols.rearrange("p (ee f) -> p ee f", f=NSUB)[:, :, fs],
                    selw[:, 0:8]), ["selw"], ["selcols"])
                first = (t == 0 and s == 0)
                last = (t == NT - 1 and s == SPT - 1)
                op("tensor", lambda e, f=first, l=last: e.matmul(
                    pb[7][0:16, 0:1], ilv, colsel[:, 0:1],
                    start=f, stop=l), ["ilv", "colsel"], [KPB[7]])

        op("vector", lambda e: e.tensor_copy(g16[0:16, 0:1], pb[7][0:16, 0:1]),
           [KPB[7]], ["g16"])
        dma("gpsimd", ar_in[:, None], g16[0:16, 0:1], ["g16"], ["ar_in_d"])
        tk.add("gpsimd", lambda e: e.collective_compute(
            "AllReduce", ALU.add, replica_groups=[list(range(NCORE))],
            ins=[ar_in], outs=[ar_out]), reads=["ar_in_d"], writes=["ar_out_d"],
            cc=True)
        tk.barrier()
        ar.release()

        ar.mark()
        sgw_s = A(8 * D); suw_s = A(8 * D); sdw_s = A(8 * D)
        h2l = A(8 * 512)
        zt = A(8 * 512)
        zg = [A(512) for _ in range(2)]
        ysh = [A(D) for _ in range(2)]
        csr2 = prow[:, 0:8 * NSUB]
        cexr = prow[:, 512:512 + 8 * NSUB]
        ccolT = A(NSUB)
        ccolF = A(8)
        cexf = A(8)
        postot = A(8 * NSUB)
        nfbc = A(8)
        nff_f = A(8)
        tailf = A(32)
        ztile = A(DW)
        rstage = [A(DW) for _ in range(2)]

        NW = 8 * NSUB
        BNDREG = {}

        def _gp_pro(eng):
            r = eng.alloc_register(name="bndcape")
            eng.reg_mov(r, CAPE - 1)
            BNDREG["r"] = r
        tk.prologue["gpsimd"] = _gp_pro
        op("tensor", lambda e: e.matmul(pb[1][0:1, 0:NW], colsel[:, 0:1],
                                        selcols, start=True, stop=True),
           ["selcols", "colsel"], [KPB[1]])
        op("vector", lambda e: e.tensor_copy(csr2[0:1, :], pb[1][0:1, 0:NW]),
           [KPB[1]], ["csr2", "prow"])
        op("vector", lambda e: e.reduce_sum(
            nff_f[0:1, 0:8],
            csr2[0:1, :].rearrange("o (ee f) -> o ee f", f=NSUB),
            axis=AX.X), ["csr2"], ["nff"])
        gdep_op = tk.add("vector", lambda e: e.tensor_copy(
            nfi8[0:1, 0:8], nff_f[0:1, 0:8]), reads=["nff"], writes=["nfi8"])
        tk.guard_src = lambda e_i: nfi8[0:1, e_i:e_i + 1]
        tk.guard_dep = gdep_op
        dma("scalar", rowx_d[None, :], csr2[0:1, :], ["csr2"], ["rowx"])
        dma("scalar", ccolT[0:8, :], rowx_d.rearrange("(p f) -> p f", p=8),
            ["rowx"], ["ccolT"])
        op("tensor", lambda e: e.matmul(
            pb[1][0:NSUB, 256:264], ccolT[0:8, :], ident[0:8, 0:8],
            is_transpose=True, start=True, stop=True), ["ccolT", "ident"], [KPB[1]])
        op("vector", lambda e: e.tensor_copy(ccolF[0:NSUB, 0:8],
                                             pb[1][0:NSUB, 256:264]),
           [KPB[1]], ["ccolF"])
        op("tensor", lambda e: e.matmul(
            pb[1][0:NSUB, 264:272], t32[0:NSUB, 0:NSUB], ccolF[0:NSUB, 0:8],
            start=True, stop=True), ["t32", "ccolF"], [KPB[1]])
        op("vector", lambda e: e.tensor_copy(cexf[0:NSUB, 0:8],
                                             pb[1][0:NSUB, 264:272]),
           [KPB[1]], ["cexf"])
        dma("scalar", rowx2_d.rearrange("(ee f) -> f ee", f=NSUB),
            cexf[0:NSUB, 0:8], ["cexf"], ["rowx2"], slow=True)
        dma("scalar", cexr[0:1, :], rowx2_d[None, :], ["rowx2"], ["cexr"])
        op("tensor", lambda e: e.matmul(pb[0][:, 0:NW], tri, selcols,
                                        start=True, stop=False),
           ["tri", "selcols"], [KPB[0]])
        op("tensor", lambda e: e.matmul(pb[0][:, 0:NW], onesr[0:1, 0:128],
                                        cexr[0:1, :], start=False, stop=True),
           ["onesr", "cexr"], [KPB[0]])
        op("vector", lambda e: e.tensor_copy(postot, pb[0][:, 0:NW]),
           [KPB[0]], ["postot"])
        op("vector", lambda e: e.tensor_sub(postot, postot, selcols),
           ["postot", "selcols"], ["postot"])
        op("vector", lambda e: e.scalar_tensor_tensor(
            postot, selcols, -BIGP, postot, op0=ALU.mult, op1=ALU.add),
           ["selcols", "postot"], ["postot"])
        op("vector", lambda e: e.tensor_scalar_add(postot, postot, BIGP),
           ["postot"], ["postot"])
        op("vector", lambda e: e.tensor_copy(idxi[:, :], postot),
           ["postot"], ["idxi"])
        op("tensor", lambda e: e.matmul(pb[1][:, 272:280], onesr[0:1, 0:128],
                                        nff_f[0:1, 0:8], start=True, stop=True),
           ["onesr", "nff"], [KPB[1]])
        op("vector", lambda e: e.tensor_copy(nfbc, pb[1][:, 272:280]),
           [KPB[1]], ["nfbc"])
        for e_i in range(8):
            for j in range(4):
                op("vector", lambda e, e_i=e_i, j=j: e.scalar_tensor_tensor(
                    tailf[:, e_i * 4 + j:e_i * 4 + j + 1], iotac, float(j * 128),
                    nfbc[:, e_i:e_i + 1], op0=ALU.add, op1=ALU.add),
                   ["iotac", "nfbc"], ["tailf"])
        op("vector", lambda e: e.tensor_copy(idxt[:, :], tailf),
           ["tailf"], ["idxt"])
        op("vector", lambda e: e.memset(ztile, 0.0), [], ["ztile"])
        op("vector", lambda e: e.memset(ztile[:, D:D + 1], BIGP), [], ["ztile"])
        op("vector", lambda e: e.memset(postot[:, 0:4], BIGP), ["postot"],
           ["postot"])
        op("vector", lambda e: e.tensor_copy(bigi[:, :], postot[:, 0:4]),
           ["postot"], ["bigi"])
        for fs in range(NSUB):
            rb = rstage[fs % 2]
            rbk = f"rstage{fs % 2}"
            dma("gpsimd", rb, h2tok_d[fs * CT:(fs + 1) * CT, :],
                ["h2tok_dd"], [rbk])
            for e_i in range(8):
                idxcol = idxi[:, e_i * NSUB + fs:e_i * NSUB + fs + 1]
                tk.add("gpsimd", lambda e, e_i=e_i, rb=rb, idxcol=idxcol:
                       e.indirect_dma_start(
                           out=compact_d[e_i],
                           out_offset=IndirectOffsetOnAxis(idxcol, 0),
                           in_=rb, in_offset=None,
                           bounds_check=BNDREG["r"], oob_is_err=False),
                       reads=[rbk, "idxi"], writes=[f"cp{e_i}"], dma=True,
                       tag=f"cp{e_i}")
        for e_i in range(8):
            for j in range(4):
                idc = idxt[:, e_i * 4 + j:e_i * 4 + j + 1]
                tk.add("gpsimd", lambda e, e_i=e_i, idc=idc:
                       e.indirect_dma_start(
                           out=compact_d[e_i],
                           out_offset=IndirectOffsetOnAxis(idc, 0),
                           in_=ztile, in_offset=None,
                           bounds_check=BNDREG["r"], oob_is_err=False),
                       reads=["ztile", "idxt"], writes=[f"cp{e_i}"], dma=True,
                       tag=f"cp{e_i}")
        dma("sync", sgw_s.rearrange("p (c f) -> p c f", f=D),
            sgwt.rearrange("(c p) f -> p c f", p=128), [], ["sgw_s"])
        dma("sync", suw_s.rearrange("p (c f) -> p c f", f=D),
            suwt.rearrange("(c p) f -> p c f", p=128), [], ["suw_s"])
        dma("sync", sdw_s.rearrange("p (c f) -> p c f", f=D),
            sdwt.rearrange("(c p) f -> p c f", p=128), [], ["sdw_s"])
        for t in range(NT):
            tok0 = t * TILE
            dma("sync", h2l.rearrange("p (c tt) -> p c tt", tt=512),
                h2_sp.rearrange("c p tt -> p c tt")[:, :, tok0:tok0 + TILE],
                ["h2_sp_d"], ["h2l"])
            h2v = h2l.rearrange("p (c tt) -> p c tt", tt=512)
            for ic in range(8):
                gp_ = ic % 2
                for c in range(8):
                    op("tensor", lambda e, ic=ic, c=c, h2v=h2v, gp_=gp_,
                       f=(c == 0), l=(c == 7):
                       e.matmul(pb[gp_][:, 0:512],
                                sgw_s.rearrange("p (cc f) -> p cc f", f=D)[
                                    :, c, ic * 128:(ic + 1) * 128],
                                h2v[:, c], start=f, stop=l),
                       ["sgw_s", "h2l"], [KPB[gp_]])
                op("scalar", lambda e, gp_=gp_: e.activation(
                    zg[gp_], pb[gp_][:, 0:512], AF.Silu), [KPB[gp_]], [f"zg{gp_}"])
                for c in range(8):
                    op("tensor", lambda e, ic=ic, c=c, h2v=h2v, gp_=gp_,
                       f=(c == 0), l=(c == 7):
                       e.matmul(pb[2 + gp_][:, 0:512],
                                suw_s.rearrange("p (cc f) -> p cc f", f=D)[
                                    :, c, ic * 128:(ic + 1) * 128],
                                h2v[:, c], start=f, stop=l),
                       ["suw_s", "h2l"], [KPB[2 + gp_]])
                zsl = zt.rearrange("p (ic tt) -> p ic tt", tt=512)[:, ic]
                op("vector", lambda e, zsl=zsl, gp_=gp_: e.tensor_mul(
                    zsl, zg[gp_], pb[2 + gp_][:, 0:512]),
                   [f"zg{gp_}", KPB[2 + gp_]], ["zt"])
            for s4 in range(SPT):
                ypar = s4 % 2
                yb = ysh[ypar]
                ybk = f"ysh{ypar}"
                for half in range(2):
                    par2 = (s4 * 2 + half) % 2
                    for ic in range(8):
                        op("tensor", lambda e, s4=s4, half=half, ic=ic, par2=par2,
                           f=(ic == 0), l=(ic == 7):
                           e.matmul(pb[4 + par2][:, 0:512],
                                    zt.rearrange("p (ic tt) -> p ic tt", tt=512)[
                                        :, ic, s4 * CT:(s4 + 1) * CT],
                                    sdw_s.rearrange("p (cc f) -> p cc f", f=D)[
                                        :, ic, half * 512:(half + 1) * 512],
                                    start=f, stop=l), ["zt", "sdw_s"],
                           [KPB[4 + par2]])
                    op("vector", lambda e, half=half, yb=yb, par2=par2:
                       e.tensor_copy(yb[:, half * 512:(half + 1) * 512],
                                     pb[4 + par2][:, 0:512]),
                       [KPB[4 + par2]], [ybk])
                sub0 = tok0 + s4 * CT
                dma("scalar", acc_tok_d[sub0:sub0 + CT, :], yb, [ybk],
                    ["acc_tok"])
        tk.barrier()
        ar.release()

        dma("gpsimd", wm2[0:8, :], ar_out.rearrange("(ee two) -> ee two", two=2),
            ["ar_out_d"], ["wm2"])
        op("vector", lambda e: e.tensor_scalar_max(wm2[0:8, 1:2], wm2[0:8, 1:2], 1.0),
           ["wm2"], ["wm2"])
        op("vector", lambda e: e.reciprocal(wm2[0:8, 1:2], wm2[0:8, 1:2]),
           ["wm2"], ["wm2"])
        op("vector", lambda e: e.tensor_mul(wm2[0:8, 0:1], wm2[0:8, 0:1],
                                            wm2[0:8, 1:2]), ["wm2"], ["wm2"])
        dma("gpsimd", wm_d[:, None], wm2[0:8, 0:1], ["wm2"], ["wm_dd"])
        for e_i in range(8):
            dma("gpsimd", wmcol[:, e_i:e_i + 1],
                wm_d[e_i:e_i + 1].partition_broadcast(128), ["wm_dd"], ["wmcol"])
        ar.mark()
        wgd = [A(8 * 512) for _ in range(2)]
        wud = [A(8 * 512) for _ in range(2)]
        wdd = [A(4 * D) for _ in range(2)]
        cblk = [A(4 * DW)]
        h2cm = [A(8 * 512)]
        zcm = [A(4 * 512)]
        zge = [A(512) for _ in range(2)]
        ysb = [A(D) for _ in range(4)]
        def ld_wg(e_i):
            p_ = e_i % 2
            dma("scalar", wgd[p_].rearrange("p (c f) -> p c f", f=512),
                egwt[e_i].rearrange("(c p) f -> p c f", p=128), [],
                [f"wg_s{p_}"])

        def ld_wu(e_i):
            p_ = e_i % 2
            dma("scalar", wud[p_].rearrange("p (c f) -> p c f", f=512),
                euwt[e_i].rearrange("(c p) f -> p c f", p=128), [],
                [f"wu_s{p_}"])

        def ld_wd(e_i):
            p_ = e_i % 2
            dma("scalar", wdd[p_].rearrange("p (c f) -> p c f", f=D),
                edwt[e_i].rearrange("(c p) f -> p c f", p=128), [],
                [f"wdn_s{p_}"])

        ld_wg(0); ld_wu(0); ld_wd(0)
        for e_i in range(8):
            epar = e_i % 2
            wg_s = wgd[epar]; wu_s = wud[epar]; wdn_s = wdd[epar]
            wgk = f"wg_s{epar}"; wuk = f"wu_s{epar}"; wdk = f"wdn_s{epar}"
            tk.add("gpsimd", lambda e: e.nop(),
                   reads=["acc_tok", "acca0", "acca1"],
                   writes=[f"afence{e_i}"])
            for b in range(NBLK):
                if e_i + 1 < 8:
                    if b == min(2, NBLK - 1):
                        ld_wg(e_i + 1)
                    if b == min(4, NBLK - 1):
                        ld_wu(e_i + 1)
                    if b == min(6, NBLK - 1):
                        ld_wd(e_i + 1)
                g = (e_i, b * 512)
                bpar = b % 2
                cb = cblk[0]; cbk = "cblk0"
                hc = h2cm[0]; hck = "h2cm0"
                zb = zcm[0]; zbk = "zcm0"
                tk.add("sync", lambda e, cb=cb, e_i=e_i, b=b: e.dma_start(
                    out=cb.rearrange("p (s f) -> p s f", f=DW),
                    in_=compact_d[e_i][b * 512:(b + 1) * 512, :]
                    .rearrange("(s p) f -> p s f", p=128)),
                    reads=[f"cp{e_i}"], writes=[cbk], dma=True, tag=cbk, guard=g)
                cbv = cb.rearrange("p (s f) -> p s f", f=DW)
                op("vector", lambda e, bpar=bpar: e.tensor_copy(
                    invi[:, bpar * 4:(bpar + 1) * 4], bigi[:, :]),
                   ["bigi"], ["invi"])
                for s4 in range(SPT):
                    for c in range(8):
                        tp = 6 + (c % 2)
                        op2 = tk.add("tensor", lambda e, s4=s4, c=c, tp=tp, cbv=cbv:
                                     e.matmul(pb[tp][:, 0:128],
                                              cbv[:, s4, c * 128:(c + 1) * 128],
                                              ident, is_transpose=True,
                                              start=True, stop=True),
                                     reads=[cbk, "ident"], writes=[KPB[tp]],
                                     guard=g)
                        tk.add("scalar", lambda e, s4=s4, c=c, tp=tp, hc=hc:
                               e.copy(hc.rearrange("p (cc tt) -> p cc tt", tt=512)[
                                   :, c, s4 * CT:(s4 + 1) * CT], pb[tp][:, 0:128]),
                               reads=[KPB[tp]], writes=[hck], guard=g)
                    tk.add("vector", lambda e, s4=s4, bpar=bpar, cbv=cbv:
                           e.tensor_copy(invi[:, bpar * 4 + s4:bpar * 4 + s4 + 1],
                                         cbv[:, s4, D:D + 1]),
                           reads=[cbk], writes=["invi"], guard=g)
                hcv = hc.rearrange("p (cc tt) -> p cc tt", tt=512)
                for ic in range(4):
                    gp_ = ic % 2
                    for c in range(8):
                        tk.add("tensor", lambda e, ic=ic, c=c, hcv=hcv, gp_=gp_,
                               wg_s=wg_s, f=(c == 0), l=(c == 7):
                               e.matmul(pb[gp_][:, 0:512],
                                        wg_s.rearrange("p (cc f) -> p cc f",
                                                       f=512)[
                                            :, c, ic * 128:(ic + 1) * 128],
                                        hcv[:, c], start=f, stop=l),
                               reads=[wgk, hck], writes=[KPB[gp_]], guard=g)
                    tk.add("scalar", lambda e, gp_=gp_: e.activation(
                        zge[gp_], pb[gp_][:, 0:512], AF.Silu),
                        reads=[KPB[gp_]], writes=[f"zge{gp_}"], guard=g)
                    for c in range(8):
                        tk.add("tensor", lambda e, ic=ic, c=c, hcv=hcv, gp_=gp_,
                               wu_s=wu_s, f=(c == 0), l=(c == 7):
                               e.matmul(pb[2 + gp_][:, 0:512],
                                        wu_s.rearrange("p (cc f) -> p cc f",
                                                       f=512)[
                                            :, c, ic * 128:(ic + 1) * 128],
                                        hcv[:, c], start=f, stop=l),
                               reads=[wuk, hck], writes=[KPB[2 + gp_]], guard=g)
                    zsl = zb.rearrange("p (ic tt) -> p ic tt", tt=512)[:, ic]
                    tk.add("vector", lambda e, zsl=zsl, gp_=gp_: e.tensor_mul(
                        zsl, zge[gp_], pb[2 + gp_][:, 0:512]),
                        reads=[f"zge{gp_}", KPB[2 + gp_]], writes=[zbk], guard=g)
                for s4 in range(SPT):
                    yb = ysb[s4]; ybk2 = f"ysb{s4}"
                    for half in range(2):
                        par2 = (s4 * 2 + half) % 2
                        for ic in range(4):
                            tk.add("tensor", lambda e, s4=s4, half=half, ic=ic,
                                   par2=par2, zb=zb, wdn_s=wdn_s,
                                   f=(ic == 0), l=(ic == 3):
                                   e.matmul(pb[4 + par2][:, 0:512],
                                            zb.rearrange("p (ic tt) -> p ic tt",
                                                         tt=512)[
                                                :, ic, s4 * CT:(s4 + 1) * CT],
                                            wdn_s.rearrange("p (cc f) -> p cc f",
                                                            f=D)[
                                                :, ic,
                                                half * 512:(half + 1) * 512],
                                            start=f, stop=l),
                                   reads=[zbk, wdk], writes=[KPB[4 + par2]],
                                   guard=g)
                        tk.add("scalar", lambda e, half=half, yb=yb, par2=par2,
                               e_i=e_i: e.activation(
                                   yb[:, half * 512:(half + 1) * 512],
                                   pb[4 + par2][:, 0:512], AF.Copy,
                                   scale=wmcol[:, e_i:e_i + 1]),
                               reads=[KPB[4 + par2], "wmcol"], writes=[ybk2],
                               guard=g)
                    iv = invi[:, bpar * 4 + s4:bpar * 4 + s4 + 1]
                    ack = f"acca{s4 % 2}"
                    tk.add("gpsimd", lambda e, yb=yb, iv=iv:
                           e.indirect_dma_start(
                               out=acc_tok_d,
                               out_offset=IndirectOffsetOnAxis(iv, 0),
                               in_=yb, in_offset=None,
                               bounds_check=BNDREG["r"], oob_is_err=False,
                               compute_op=ALU.add),
                           reads=[ybk2, "invi", f"afence{e_i}"], writes=[ack],
                           dma=True, tag=ack)
        tk.barrier()
        ar.release()

        ar.mark()
        x1l2 = [A(8 * 512) for _ in range(2)]
        accl2 = [A(4 * D) for _ in range(2)]
        gbbc = A(D)
        tmp4 = [A(D) for _ in range(2)]
        outt = [A(D) for _ in range(2)]
        dma("sync", gbbc, modb_d[5 * D:6 * D].partition_broadcast(128),
            ["modb"], ["gbbc"])
        for t in range(NT):
            tok0 = t * TILE
            tp4 = t % 2
            x1l = x1l2[tp4]
            accl = accl2[tp4]
            x1lk = f"x1l{tp4}"
            acclk = f"accl{tp4}"
            dma("sync", x1l.rearrange("p (c tt) -> p c tt", tt=512),
                x1_sp.rearrange("c p tt -> p c tt")[:, :, tok0:tok0 + TILE],
                ["x1_sp_d"], [x1lk])
            dma("sync", accl.rearrange("p (s f) -> p s f", f=D),
                acc_tok_d[tok0:tok0 + TILE, :].rearrange("(s p) f -> p s f",
                                                         p=128),
                ["acc_tok"], [acclk])
            accv = accl.rearrange("p (s f) -> p s f", f=D)
            x1v4 = x1l.rearrange("p (c tt) -> p c tt", tt=512)
            for s in range(SPT):
                par = s % 2
                ob = outt[par]
                obk = f"outt{par}"
                for c in range(8):
                    tp = 4 + (c % 2)
                    op("tensor", lambda e, c=c, s=s, tp=tp, x1v4=x1v4: e.matmul(
                        pb[tp][:, 0:128], x1v4[:, c, s * CT:(s + 1) * CT],
                        ident, is_transpose=True, start=True, stop=True),
                       [x1lk, "ident"], [KPB[tp]])
                    op("vector", lambda e, c=c, tp=tp, ob=ob: e.tensor_copy(
                        ob[:, c * 128:(c + 1) * 128], pb[tp][:, 0:128]),
                       [KPB[tp]], [obk])
                tb = tmp4[par]
                op("vector", lambda e, s=s, tb=tb, accv=accv: e.tensor_mul(
                    tb, accv[:, s], gbbc), [acclk, "gbbc"], [f"tmp4{par}"])
                op("vector", lambda e, ob=ob, tb=tb: e.tensor_add(ob, ob, tb),
                   [obk, f"tmp4{par}"], [obk])
                sub0 = tok0 + s * CT
                dma("gpsimd", yout[sub0:sub0 + CT, :], ob, [obk], ["yout_d"])
        tk.barrier()
        ar.release()

        globals()["_last_tk"] = tk
        semnames = sorted({o.sem for o in tk.ops})
        sems = {nm: ctx.enter_context(nc.semaphore(nm)) for nm in semnames}
        block = ctx.enter_context(nc.Block())
        tk.emit(block, sems)
    return nc


def _host_prep(inputs, TPC):
    x = np.ascontiguousarray(np.asarray(inputs["L0_attn_hidden_states"],
                                        dtype=np.float32))
    B, T, _ = x.shape
    ident = np.eye(128, dtype=np.float32)
    tri = np.triu(np.ones((128, 128), np.float32))
    tri4 = np.ascontiguousarray(np.tile(tri, (1, 4)))
    colsel = np.zeros((128, 2), np.float32)
    colsel[:, 0] = 1.0
    colsel[:CT // 2, 1] = 1.0
    onesr = np.ones((1, 512), np.float32)
    negr = -np.ones((1, 512), np.float32)
    k32 = np.arange(32)
    t32 = (k32[:, None] < k32[None, :]).astype(np.float32)
    iotac = np.arange(128, dtype=np.float32)[:, None]
    f32c = lambda a: np.ascontiguousarray(np.asarray(a, dtype=np.float32))
    common = {
        "wq": f32c(inputs["L0_attn_Wq"]), "wk": f32c(inputs["L0_attn_Wk"]),
        "wv": f32c(inputs["L0_attn_Wv"]), "wr": f32c(inputs["L0_attn_Wr"]),
        "wa": f32c(inputs["L0_attn_Wa"]), "wo": f32c(inputs["L0_attn_Wo"]),
        "a_log": f32c(inputs["L0_attn_A_log"]),
        "dt_b": f32c(inputs["L0_attn_dt_bias"]),
        "qnw": f32c(inputs["L0_attn_q_norm_w"]),
        "knw": f32c(inputs["L0_attn_k_norm_w"]),
        "gnw": f32c(inputs["L0_attn_g_norm_w"]),
        "gate_wt": f32c(np.transpose(inputs["L0_ffn_gate_w"])),
        "lb_b": f32c(inputs["L0_ffn_lb_bias"]),
        "egwt": f32c(np.transpose(inputs["L0_ffn_expert_gate_w"], (0, 2, 1))),
        "euwt": f32c(np.transpose(inputs["L0_ffn_expert_up_w"], (0, 2, 1))),
        "edwt": f32c(np.transpose(inputs["L0_ffn_expert_down_w"], (0, 2, 1))),
        "sgwt": f32c(np.transpose(inputs["L0_ffn_shared_gate_w"])),
        "suwt": f32c(np.transpose(inputs["L0_ffn_shared_up_w"])),
        "sdwt": f32c(np.transpose(inputs["L0_ffn_shared_down_w"])),
        "wmod": f32c(inputs["L0_wmod"]),
        "ident": ident, "tri": tri, "tri4": tri4, "colsel": colsel,
        "onesr": onesr, "negr": negr, "t32": t32, "iotac": iotac,
    }
    cfull = f32c(inputs["L0_c"])
    in_maps = []
    for core in range(NCORE):
        b = core // 4
        p = core % 4
        xs = np.ascontiguousarray(x[b, p * TPC:(p + 1) * TPC, :].T)
        pmask = np.zeros(NCORE, np.float32)
        for j in range(NCORE):
            if j // 4 == b and (j % 4) < p:
                pmask[j] = 1.0
        m = dict(common)
        m["xin"] = xs
        m["c_in"] = np.ascontiguousarray(cfull[b:b + 1, :])
        m["pmask"] = pmask
        in_maps.append(m)
    return in_maps, (B, T)


def kernel(**inputs):
    T = int(np.asarray(inputs["L0_attn_hidden_states"]).shape[1])
    TPC = 2 * T // NCORE
    nc = build_program(TPC)
    in_maps, (B, T) = _host_prep(inputs, TPC)
    res = run_bass_kernel_spmd(nc, in_maps, core_ids=list(range(NCORE)))
    out = np.zeros((B, T, D), np.float32)
    for core in range(NCORE):
        b = core // 4
        p = core % 4
        out[b, p * TPC:(p + 1) * TPC, :] = res.results[core]["yout"]
    return out


# version 31 (final)
# speedup vs baseline: 1.0359x; 1.0109x over previous
import numpy as np
import concourse.bass as bass
import concourse.mybir as mybir
from concourse.bass_utils import run_bass_kernel_spmd
from concourse.bass_core_rust import IndirectOffsetOnAxis

F32 = mybir.dt.float32
I32 = mybir.dt.int32
BIGP = 100000.0
AF = mybir.ActivationFunctionType
ALU = mybir.AluOpType
AX = mybir.AxisListType

D = 1024
H = 8
DK = 128
NCORE = 8
CT = 128
EPS = 1e-5
CLAMP = 85.0
NEG = -1.0


class _Op:
    __slots__ = ("eng", "fn", "deps", "need", "sem", "val", "inc", "guard",
                 "chain")

    def __init__(self, eng, fn, sem, inc):
        self.eng, self.fn, self.sem, self.inc = eng, fn, sem, inc
        self.deps = []
        self.need = False
        self.val = None
        self.guard = None
        self.chain = None


class Tracker:
    ENGS = ("sync", "tensor", "vector", "scalar", "gpsimd")

    def __init__(self):
        self.ops = []
        self.last_w = {}
        self.readers = {}
        self.forced = {}
        self.sem_last = {}
        self.dma_last = {}
        self.ncc = 0
        self.guard_src = None
        self.guard_dep = None
        self.prologue = {}

    def add(self, eng, fn, reads=(), writes=(), dma=False, cc=False, tag=None,
            guard=None):
        if cc:
            sem, inc = f"cc{self.ncc}", 1
            self.ncc += 1
        elif dma:
            sem, inc = "dq_" + (tag or "misc"), 16
        else:
            sem, inc = "c_" + eng, 1
        op = _Op(eng, fn, sem, inc)
        deps = set()
        for k in reads:
            w = self.last_w.get(k)
            if w is not None:
                deps.add(w)
        for k in writes:
            w = self.last_w.get(k)
            if w is not None:
                deps.add(w)
            for r in self.readers.get(k, ()):
                deps.add(r)
        fb = self.forced.get(eng)
        if fb is not None:
            deps.add(fb)
        if dma:
            prev = self.dma_last.get(sem)
            if prev is not None:
                deps.add(prev)
                op.chain = prev
            self.dma_last[sem] = op
        deps.discard(op)
        if eng == "tensor":
            deps = {d for d in deps if d.eng != "tensor"}
        op.deps = list(deps)
        op.guard = guard
        for d in op.deps:
            d.need = True
        for k in reads:
            self.readers.setdefault(k, []).append(op)
        for k in writes:
            self.last_w[k] = op
            self.readers[k] = [op]
        self.ops.append(op)
        self.sem_last[sem] = op
        return op

    def barrier(self):
        snap = list(self.sem_last.values())
        for eng in self.ENGS:
            op = _Op(eng, lambda e: e.nop(), "c_" + eng, 1)
            op.deps = list(snap)
            for d in op.deps:
                d.need = True
            self.ops.append(op)
            self.forced[eng] = op
            self.sem_last[op.sem] = op

    def emit(self, block, handles):
        for op in self.ops:
            if op.inc == 16 or op.sem == "cc":
                op.need = True
        counts = {}
        for op in self.ops:
            if op.need:
                counts[op.sem] = counts.get(op.sem, 0) + op.inc
                op.val = counts[op.sem]
        per_eng = {e: [] for e in self.ENGS}
        for op in self.ops:
            per_eng[op.eng].append(op)

        gsrc, gdep = self.guard_src, self.guard_dep
        waited = {}
        for o in self.ops:
            bysem = {}
            for d in o.deps:
                if d.val is not None:
                    if bysem.get(d.sem, 0) < d.val:
                        bysem[d.sem] = d.val
            for sm, v in bysem.items():
                waited.setdefault(sm, set()).add(v)
        if gdep is not None and gdep.val is not None:
            waited.setdefault(gdep.sem, set()).add(gdep.val)

        def mk(ops, ename):
            def body(eng):
                pro = self.prologue.get(ename)
                if pro is not None:
                    pro(eng)
                hwm = {}
                state = {"g": None, "cm": None, "pend": None, "reg": None,
                         "ge": None}

                def close():
                    if state["cm"] is None:
                        return
                    finals = {}
                    starts = {}
                    kinds = {}
                    for o2 in state["pend"]:
                        if o2.sem not in starts:
                            starts[o2.sem] = o2.val - o2.inc
                            kinds[o2.sem] = (o2.eng == "gpsimd"
                                             and o2.inc == 16)
                            if o2.chain is not None \
                                    and o2.chain.val is not None \
                                    and o2.chain not in state["pend"]:
                                starts[o2.sem] = (o2.val - o2.inc, o2.chain)
                        finals[o2.sem] = o2.val
                    for sm, v in finals.items():
                        eng.wait_ge(handles[sm], v)
                    state["cm"].__exit__(None, None, None)
                    el = eng.Else()
                    el.__enter__()
                    for sm, st0 in starts.items():
                        if isinstance(st0, tuple):
                            st0, ch = st0
                            eng.wait_ge(handles[ch.sem], ch.val)
                        fin = finals[sm]
                        steps = sorted(v for v in waited.get(sm, ())
                                       if st0 < v < fin)
                        prev = st0
                        swdge = kinds[sm]
                        for v in steps + [fin]:
                            if v > prev:
                                if swdge:
                                    eng.inc_swdge_sem([handles[sm]],
                                                      [v - prev])
                                else:
                                    eng.sem_inc(handles[sm], v - prev)
                                prev = v
                    for sm, v in finals.items():
                        eng.wait_ge(handles[sm], v)
                    el.__exit__(None, None, None)
                    state["cm"] = None
                    state["g"] = None
                    state["pend"] = None

                for o in ops:
                    if o.guard != state["g"]:
                        close()
                    if o.guard is not None and state["cm"] is None:
                        if (gdep is not None and gdep.val is not None
                                and hwm.get(gdep.sem, 0) < gdep.val):
                            eng.wait_ge(handles[gdep.sem], gdep.val)
                            hwm[gdep.sem] = gdep.val
                        if state["reg"] is None:
                            state["reg"] = eng.alloc_register(name="nfguard")
                        e_i, thresh = o.guard
                        if state["ge"] != e_i:
                            eng.reg_load(state["reg"], gsrc(e_i))
                            state["ge"] = e_i
                        state["cm"] = eng.If_cmp(state["reg"], thresh, "IS_GT")
                        state["cm"].__enter__()
                        state["g"] = o.guard
                        state["pend"] = []
                    bysem = {}
                    for d in o.deps:
                        if d.val is None:
                            continue
                        if bysem.get(d.sem, 0) < d.val:
                            bysem[d.sem] = d.val
                    for sm, v in bysem.items():
                        if hwm.get(sm, 0) >= v:
                            continue
                        eng.wait_ge(handles[sm], v)
                        if state["cm"] is None:
                            hwm[sm] = v
                    ins = o.fn(eng)
                    if o.need:
                        if ins is None:
                            ins = eng.nop()
                        ins.then_inc(handles[o.sem], o.inc)
                        if state["cm"] is not None:
                            state["pend"].append(o)
                close()
            return body

        for e in self.ENGS:
            if per_eng[e]:
                getattr(block, e)(mk(per_eng[e], e))
        return counts


class Arena:
    def __init__(self, t, total):
        self.t, self.total = t, total
        self.off = 0
        self.marks = []

    def alloc(self, n):
        a = self.off
        self.off += n
        assert self.off <= self.total, f"arena overflow {self.off} > {self.total}"
        return self.t[:, a:a + n]

    def mark(self):
        self.marks.append(self.off)

    def release(self):
        self.off = self.marks.pop()


def build_program(TPC):
    TILE = 512
    SPT = TILE // CT
    NT = TPC // TILE
    assert TPC % TILE == 0

    nc = bass.Bass("TRN2", target_bir_lowering=False)

    def din(name, shape):
        return nc.dram_tensor(name, list(shape), F32, kind="ExternalInput").ap()

    def dint(name, shape):
        return nc.dram_tensor(name, list(shape), F32).ap()

    xin = din("xin", [D, TPC])
    wq = din("wq", [D, D]); wk = din("wk", [D, D])
    wv = din("wv", [D, D]); wr = din("wr", [D, D])
    wa_d = din("wa", [D, H])
    wo_d = din("wo", [D, D])
    a_log = din("a_log", [H]); dt_b = din("dt_b", [H])
    qnw_d = din("qnw", [DK]); knw_d = din("knw", [DK]); gnw_d = din("gnw", [D])
    gate_wt = din("gate_wt", [D, H])
    lb_b = din("lb_b", [H])
    egwt = din("egwt", [8, D, 512]); euwt = din("euwt", [8, D, 512])
    edwt = din("edwt", [8, 512, D])
    sgwt = din("sgwt", [D, D]); suwt = din("suwt", [D, D]); sdwt = din("sdwt", [D, D])
    c_in = din("c_in", [1, D])
    wmod = din("wmod", [D, 6 * D])
    ident_d = din("ident", [128, 128])
    tri_d = din("tri", [128, 128])
    tri4_d = din("tri4", [128, 512])
    colsel_d = din("colsel", [128, 2])
    ones_d = din("onesr", [1, 512])
    negr_d = din("negr", [1, 512])
    pmask_d = din("pmask", [NCORE])
    t32_d = din("t32", [32, 32])
    iotac_d = din("iotac", [128, 1])

    NSUB = TPC // CT
    NBLK = TPC // 512
    CAPE = TPC + 512
    DW = D + 1
    o_sp = dint("o_sp", [H, DK, TPC])
    qc_sp = dint("qc_sp", [H, DK, TPC])
    x1_sp = dint("x1_sp", [8, 128, TPC])
    h2_sp = dint("h2_sp", [8, 128, TPC])
    h2tok_d = dint("h2tok", [TPC, DW])
    compact_d = [dint(f"compact{i}", [CAPE, DW]) for i in range(8)]
    acc_tok_d = dint("acc_tok", [TPC, D])
    rowx_d = dint("rowx", [8 * NSUB])
    rowx2_d = dint("rowx2", [8 * NSUB])
    modb_d = dint("modb", [6 * D])
    ag_in = dint("ag_in", [H * DK * DK + H * DK])
    ag_out = dint("ag_out", [NCORE * (H * DK * DK + H * DK)])
    ar_in = dint("ar_in", [16])
    ar_out = dint("ar_out", [16])
    wm_d = dint("wm_d", [8])
    yout = nc.dram_tensor("yout", [TPC, D], F32, kind="ExternalOutput").ap()

    tk = Tracker()
    from contextlib import ExitStack

    with ExitStack() as ctx:
        ARENA_F = 52850
        arena_t = ctx.enter_context(nc.sbuf_tensor("arena", [128, ARENA_F], F32))
        ar = Arena(arena_t, ARENA_F)
        A = ar.alloc
        idxi = ctx.enter_context(nc.sbuf_tensor("idxi", [128, 8 * NSUB], I32))
        idxt = ctx.enter_context(nc.sbuf_tensor("idxt", [128, 32], I32))
        nfi8 = ctx.enter_context(nc.sbuf_tensor("nfi8", [1, 8], I32))
        invi = ctx.enter_context(nc.sbuf_tensor("invi", [128, 8], I32))
        bigi = ctx.enter_context(nc.sbuf_tensor("bigi", [128, 4], I32))

        ident = A(128); tri = A(128); tri4 = A(512); colsel = A(2)
        onesr = A(512); negr = A(512)
        modc = A(48)
        qknw = A(2); gnwc = A(8)
        naec = A(8)
        rowsc = A(24)
        pmb = A(8)
        S_loc = A(H * DK); S_in = A(H * DK)
        cumD = A(H); cumc = A(H)
        wm2 = A(2)
        modrow = A(6 * D)
        wa_s = A(8 * H)
        gw_s = A(8 * H)
        prow = A(3 * 512)
        g16 = A(1)
        t32 = A(32)
        iotac = A(1)
        selcols = A(8 * NSUB)
        wmcol = A(8)

        cvals = (EPS, 1.0, -0.5, -1.0)
        cbuf = A(len(cvals))
        pb = [ctx.enter_context(nc.psum_tensor(f"pb{i}", [128, 512], F32))
              for i in range(8)]
        KPB = [f"pb{i}" for i in range(8)]

        def dma(eng, out, in_, rk, wk_, slow=False):
            tk.add(eng, lambda e, o=out, i=in_, s=slow: e.dma_start(
                out=o, in_=i, allow_slow_non_contiguous=s),
                   reads=rk, writes=wk_, dma=True, tag=wk_[0])

        def dma_acc(out, in_, rk, wk_):
            tk.add("gpsimd", lambda e, o=out, i=in_: e.dma_start(
                out=o, in_=i, accum_op=ALU.add), reads=rk, writes=wk_, dma=True,
                tag=wk_[0])

        def op(eng, fn, rk, wk_):
            tk.add(eng, fn, reads=rk, writes=wk_)

        def modrow_sl(w, c):
            return modrow[0:1, w * D + c * 128: w * D + (c + 1) * 128]

        def modcol(w, c):
            return modc[:, w * 8 + c:w * 8 + c + 1]

        for ci, cv in enumerate(cvals):
            op("vector", lambda e, ci=ci, cv=cv: e.memset(cbuf[:, ci:ci + 1], cv),
               [], ["cbuf"])
            nc.const_aps.register(F32, cv, cbuf[:, ci:ci + 1])
        dma("sync", ident, ident_d, ["i_d"], ["ident"])
        dma("sync", tri, tri_d, ["t_d"], ["tri"])
        dma("sync", tri4, tri4_d, ["t4_d"], ["tri4"])
        dma("sync", colsel, colsel_d, ["cs_d"], ["colsel"])
        dma("sync", onesr[0:1, :], ones_d, ["o_d"], ["onesr"])
        dma("sync", negr[0:1, :], negr_d, ["n_d"], ["negr"])
        dma("sync", qknw[:, 0:1], qnw_d[:, None], ["qn_d"], ["qknw"])
        dma("sync", qknw[:, 1:2], knw_d[:, None], ["kn_d"], ["qknw"])
        dma("sync", gnwc[:, :, None],
            gnw_d.rearrange("(c p) -> p c", p=128)[:, :, None], ["gn_d"], ["gnwc"],
            slow=True)
        dma("sync", pmb, pmask_d.partition_broadcast(128), ["pm_d"], ["pmb"])
        dma("sync", t32[0:32, :], t32_d, ["t32_d"], ["t32"])
        dma("sync", iotac, iotac_d, ["io_d"], ["iotac"])
        dma("sync", rowsc[0:1, 0:8], a_log[None, :], ["al_d"], ["rowsc"])
        dma("sync", rowsc[0:1, 8:16], dt_b[None, :], ["dt_d"], ["rowsc"])
        dma("sync", rowsc[0:1, 16:24], lb_b[None, :], ["lb_d"], ["rowsc"])
        op("scalar", lambda e: e.activation(rowsc[0:1, 0:8], rowsc[0:1, 0:8], AF.Exp),
           ["rowsc"], ["rowsc"])
        op("vector", lambda e: e.tensor_scalar_mul(rowsc[0:1, 0:8],
                                                   rowsc[0:1, 0:8], -1.0),
           ["rowsc"], ["rowsc"])
        op("tensor", lambda e: e.matmul(pb[0][:, 0:8], onesr[0:1, 0:128],
                                        rowsc[0:1, 0:8], start=True, stop=True),
           ["onesr", "rowsc"], [KPB[0]])
        op("vector", lambda e: e.tensor_copy(naec, pb[0][:, 0:8]), [KPB[0]], ["naec"])
        op("vector", lambda e: e.memset(S_loc, 0.0), [], ["S_loc"])
        op("vector", lambda e: e.memset(S_in, 0.0), [], ["S_in"])
        op("vector", lambda e: e.memset(cumD, 1.0), [], ["cumD"])
        dma("sync", wa_s.rearrange("p (c hh) -> p c hh", hh=H),
            wa_d.rearrange("(c p) hh -> p c hh", p=128), ["wa_d"], ["wa_s"])
        dma("sync", gw_s.rearrange("p (c hh) -> p c hh", hh=H),
            gate_wt.rearrange("(c p) hh -> p c hh", p=128), ["gwt_d"], ["gw_s"])

        ar.mark()
        csil = A(8)
        wstr = [A(8 * 512) for _ in range(2)]
        dma("sync", csil[:, :, None],
            c_in.rearrange("one (c p) -> p c", p=128)[:, :, None], ["c_d"], ["csil"],
            slow=True)
        op("scalar", lambda e: e.activation(csil, csil, AF.Silu), ["csil"], ["csil"])
        for og in range(12):
            par = og % 2
            wb = wstr[par]
            wkk = f"wstr{par}"
            dma("sync", wb.rearrange("p (c t) -> p c t", t=512),
                wmod[:, og * 512:(og + 1) * 512].rearrange("(c p) t -> p c t", p=128),
                [], [wkk])
            bank = pb[4 + par]
            for c in range(8):
                op("tensor", lambda e, c=c, wb=wb, bank=bank, f=(c == 0), l=(c == 7):
                   e.matmul(bank[0:1, 0:512], csil[:, c:c + 1],
                            wb.rearrange("p (c t) -> p c t", t=512)[:, c],
                            start=f, stop=l), ["csil", wkk], [KPB[4 + par]])
            op("vector", lambda e, og=og, bank=bank: e.tensor_copy(
                modrow[0:1, og * 512:(og + 1) * 512], bank[0:1, 0:512]),
               [KPB[4 + par]], ["modrow"])
        for w in (1, 4):
            sl = modrow[0:1, w * D:(w + 1) * D]
            op("vector", lambda e, sl=sl: e.tensor_scalar_add(sl, sl, 1.0),
               ["modrow"], ["modrow"])
        sgn = A(D)
        for w in (2, 5):
            sl = modrow[0:1, w * D:(w + 1) * D]
            m_ = sgn[0:1, 0:D]
            op("vector", lambda e, sl=sl, m_=m_: e.tensor_scalar(
                m_, sl, 0.0, None, op0=ALU.is_ge), ["modrow"], ["sgn"])
            op("vector", lambda e, m_=m_: e.tensor_scalar(
                m_, m_, 2.0, -1.0, op0=ALU.mult, op1=ALU.add), ["sgn"], ["sgn"])
            op("scalar", lambda e, sl=sl: e.activation(sl, sl, AF.Abs),
               ["modrow"], ["modrow"])
            op("vector", lambda e, sl=sl: e.tensor_scalar_max(sl, sl, 0.25),
               ["modrow"], ["modrow"])
            op("vector", lambda e, sl=sl, m_=m_: e.tensor_mul(sl, sl, m_),
               ["modrow", "sgn"], ["modrow"])
        dma("scalar", modb_d[None, :], modrow[0:1, :], ["modrow"], ["modb"])
        dma("sync", modc.rearrange("p (w c) -> p w c", c=8),
            modb_d.rearrange("(w c p) -> p w c", p=128, c=8), ["modb"], ["modc"],
            slow=True)
        ar.release()

        tk.barrier()

        ar.mark()
        xt = [A(8 * CT) for _ in range(2)]
        ht = A(8 * TILE)
        wst = [A(512) for _ in range(2)]
        rbuf = A(D)
        q_tok = A(SPT * D)
        k_tok = A(SPT * D)
        v_tok = A(SPT * D)
        r_tok = A(SPT * D)
        f_t = [A(D) for _ in range(SPT)]
        lam = A(D)
        sqsc = A(D)
        sqsc2 = A(2 * CT)
        acol = A(SPT * H)
        fhead = A(SPT * H)
        rqk = A(SPT * 2 * H)
        mx8 = A(8)
        ztop = A(128); wsel = A(128)
        wsum = A(H)
        ppt_s = A(D)
        eqs_s = A(D)
        qfm_s = A(D)
        kfm_s = A(D)
        ams_s = A(D)
        stv_s = A(D)
        dex = A(24)
        gkn = A(8); qcf = A(8)
        dstage = A(16)
        o_sub = A(H * CT)
        qc_sub = A(H * CT)

        WLIST = ((wr, "r", None, None, AF.Sigmoid),
                 (wq, "q", q_tok, "q_tok", AF.Silu),
                 (wk, "k", k_tok, "k_tok", AF.Silu),
                 (wv, "v", v_tok, "v_tok", AF.Silu))

        hvw = ht.rearrange("p (c tt) -> p c tt", tt=TILE)

        def ln1(t2, s):
            tok2 = t2 * TILE
            xpar = (t2 * SPT + s) % 2
            xb = xt[xpar]
            xk = f"xt{xpar}"
            dma("sync", xb.rearrange("p (c t2) -> p c t2", t2=CT),
                xin[:, tok2 + s * CT: tok2 + (s + 1) * CT]
                .rearrange("(c p) t2 -> p c t2", p=128), [], [xk])
            xv = xb.rearrange("p (c t2) -> p c t2", t2=CT)
            for c in range(8):
                op("tensor", lambda e, c=c, xv=xv, f=(c == 0), l=(c == 7):
                   e.matmul(pb[2][0:1, 0:CT], colsel[:, 0:1], xv[:, c],
                            start=f, stop=l), [xk, "colsel"], [KPB[2]])
            for c in range(8):
                cp2 = c % 2
                op("scalar", lambda e, c=c, xv=xv, cp2=cp2: e.activation(
                    sqsc2[:, cp2 * CT:(cp2 + 1) * CT], xv[:, c], AF.Square),
                   [xk], [f"sq2{cp2}"])
                op("tensor", lambda e, c=c, cp2=cp2, f=(c == 0), l=(c == 7):
                   e.matmul(pb[3][0:1, 0:CT], colsel[:, 0:1],
                            sqsc2[:, cp2 * CT:(cp2 + 1) * CT], start=f, stop=l),
                   [f"sq2{cp2}", "colsel"], [KPB[3]])
            mrow = prow[0:1, 0:CT]
            rrow = prow[0:1, 512:512 + CT]
            mrr = prow[0:1, 1024:1024 + CT]
            op("vector", lambda e, mrow=mrow: e.tensor_scalar_mul(
                mrow, pb[2][0:1, 0:CT], 1.0 / D), [KPB[2]], ["prow"])
            op("vector", lambda e, rrow=rrow, mrow=mrow: e.tensor_mul(
                rrow, mrow, mrow), ["prow"], ["prow"])
            op("vector", lambda e, rrow=rrow: e.scalar_tensor_tensor(
                rrow, pb[3][0:1, 0:CT], 1.0 / D, rrow,
                op0=ALU.mult, op1=ALU.subtract), [KPB[3], "prow"], ["prow"])
            op("scalar", lambda e, rrow=rrow: e.activation(
                rrow, rrow, AF.Ln, bias=EPS), ["prow"], ["prow"])
            op("scalar", lambda e, rrow=rrow: e.activation(
                rrow, rrow, AF.Exp, scale=-0.5), ["prow"], ["prow"])
            op("vector", lambda e, mrr=mrr, mrow=mrow, rrow=rrow: e.tensor_mul(
                mrr, mrow, rrow), ["prow"], ["prow"])
            for c in range(8):
                op("tensor", lambda e, c=c, rrow=rrow: e.matmul(
                    pb[2][:, 0:CT], modrow_sl(1, c), rrow, start=True, stop=True),
                   ["modrow", "prow"], [KPB[2]])
                op("tensor", lambda e, c=c, mrr=mrr: e.matmul(
                    pb[3][:, 0:CT], modrow_sl(1, c), mrr, start=True, stop=False),
                   ["modrow", "prow"], [KPB[3]])
                op("tensor", lambda e, c=c: e.matmul(
                    pb[3][:, 0:CT], modrow_sl(0, c), negr[0:1, 0:CT],
                    start=False, stop=True), ["modrow", "negr"], [KPB[3]])
                hc = hvw[:, c, s * CT:(s + 1) * CT]
                op("vector", lambda e, hc=hc, xv=xv, c=c: e.tensor_mul(
                    hc, xv[:, c], pb[2][:, 0:CT]), [xk, KPB[2]], ["ht"])
                op("vector", lambda e, hc=hc: e.tensor_sub(
                    hc, hc, pb[3][:, 0:CT]), ["ht", KPB[3]], ["ht"])

        for s in range(SPT):
            ln1(0, s)

        for t in range(NT):
            tok0 = t * TILE
            rbufk = "r_tok"

            def proj_one(wd, dstt, dkey, func):
                for half in range(2):
                    for c in range(8):
                        par = c % 2
                        wb = wst[par]
                        wkk = f"wst{par}"
                        dma("sync", wb,
                            wd[c * 128:(c + 1) * 128, half * 512:(half + 1) * 512],
                            [], [wkk])
                        for s in range(SPT):
                            op("tensor",
                               lambda e, c=c, s=s, wb=wb, f=(c == 0), l=(c == 7):
                               e.matmul(pb[4 + s][:, 0:512],
                                        hvw[:, c, s * CT:(s + 1) * CT], wb,
                                        start=f, stop=l),
                               ["ht", wkk], [KPB[4 + s]])
                    for s in range(SPT):
                        dv = dstt.rearrange("p (s f) -> p s f", f=D)[
                            :, s, half * 512:(half + 1) * 512]
                        op("scalar", lambda e, dv=dv, s=s, func=func:
                           e.activation(dv, pb[4 + s][:, 0:512], func),
                           [KPB[4 + s]], [dkey])

            proj_one(wr, r_tok, rbufk, AF.Sigmoid)

            for s in range(SPT):
                for c in range(8):
                    op("tensor", lambda e, c=c, s=s, f=(c == 0):
                       e.matmul(pb[2][:, 0:H], hvw[:, c, s * CT:(s + 1) * CT],
                                wa_s.rearrange("p (c hh) -> p c hh", hh=H)[:, c],
                                start=f, stop=False), ["ht", "wa_s"], [KPB[2]])
                op("tensor", lambda e: e.matmul(
                    pb[2][:, 0:H], onesr[0:1, 0:128], rowsc[0:1, 8:16],
                    start=False, stop=True), ["onesr", "rowsc"], [KPB[2]])
                op("vector", lambda e, s=s: e.tensor_copy(
                    acol[:, s * H:(s + 1) * H], pb[2][:, 0:H]), [KPB[2]], ["acol"])

            nsh = SPT * H
            op("scalar", lambda e: e.activation(fhead[:, 0:nsh], acol, AF.Abs),
               ["acol"], ["fhead"])
            op("scalar", lambda e: e.activation(fhead[:, 0:nsh], fhead[:, 0:nsh],
                                                AF.Exp, scale=-1.0),
               ["fhead"], ["fhead"])
            op("scalar", lambda e: e.activation(fhead[:, 0:nsh], fhead[:, 0:nsh],
                                                AF.Ln, bias=1.0), ["fhead"], ["fhead"])
            op("vector", lambda e: e.tensor_scalar_max(sqsc[:, 0:nsh], acol, 0.0),
               ["acol"], ["sqsc"])
            op("vector", lambda e: e.tensor_add(fhead[:, 0:nsh], fhead[:, 0:nsh],
                                                sqsc[:, 0:nsh]),
               ["fhead", "sqsc"], ["fhead"])
            op("vector", lambda e: e.tensor_tensor(
                fhead[:, 0:nsh].rearrange("p (s hh) -> p s hh", hh=H),
                fhead[:, 0:nsh].rearrange("p (s hh) -> p s hh", hh=H),
                naec.to_broadcast([128, H, SPT]).rearrange("p hh s -> p s hh"),
                ALU.mult), ["fhead", "naec"], ["fhead"])

            for s in range(SPT):
                fb = f_t[s]
                fbk = f"f_t{s}"
                for hh in range(H):
                    sc_sl = r_tok.rearrange("p (s f) -> p s f", f=D)[
                        :, s, hh * DK:(hh + 1) * DK]
                    op("vector", lambda e, sc_sl=sc_sl: e.tensor_copy(ztop, sc_sl),
                       [rbufk], ["ztop"])
                    for _ in range(4):
                        op("vector", lambda e: e.max(mx8, ztop), ["ztop"], ["mx8"])
                        op("vector", lambda e: e.match_replace(ztop, mx8, ztop, NEG),
                           ["ztop", "mx8"], ["ztop"])
                    op("vector", lambda e, sc_sl=sc_sl: e.tensor_tensor(
                        wsel, sc_sl, ztop, ALU.is_gt), [rbufk, "ztop"], ["wsel"])
                    op("vector", lambda e, sc_sl=sc_sl: e.tensor_mul(
                        wsel, wsel, sc_sl), ["wsel", rbufk], ["wsel"])
                    wcol = wsum[:, hh:hh + 1]
                    op("vector", lambda e, wcol=wcol: e.reduce_sum(
                        wcol, wsel, axis=AX.X), ["wsel"], ["wsum"])
                    op("vector", lambda e, wcol=wcol: e.tensor_scalar_add(
                        wcol, wcol, 1e-9), ["wsum"], ["wsum"])
                    op("vector", lambda e, wcol=wcol: e.reciprocal(wcol, wcol),
                       ["wsum"], ["wsum"])
                    fsl = fb[:, hh * DK:(hh + 1) * DK]
                    op("vector", lambda e, fsl=fsl, wcol=wcol:
                       e.tensor_scalar(fsl, wsel, wcol, None, op0=ALU.mult),
                       ["wsel", "wsum"], [fbk])

            proj_one(wq, q_tok, "q_tok", AF.Silu)
            proj_one(wk, k_tok, "k_tok", AF.Silu)
            proj_one(wv, v_tok, "v_tok", AF.Silu)

            for s in range(SPT):
                qv = q_tok.rearrange("p (s f) -> p s f", f=D)[:, s]
                kv = k_tok.rearrange("p (s f) -> p s f", f=D)[:, s]
                op("scalar", lambda e, qv=qv: e.activation(sqsc[:, 0:D], qv, AF.Square),
                   ["q_tok"], ["sqsc"])
                op("vector", lambda e, s=s: e.reduce_sum(
                    rqk[:, s * 16:s * 16 + 8],
                    sqsc[:, 0:D].rearrange("p (hh kk) -> p hh kk", kk=DK),
                    axis=AX.X), ["sqsc"], ["rqk"])
                op("scalar", lambda e, kv=kv: e.activation(sqsc[:, 0:D], kv, AF.Square),
                   ["k_tok"], ["sqsc"])
                op("vector", lambda e, s=s: e.reduce_sum(
                    rqk[:, s * 16 + 8:s * 16 + 16],
                    sqsc[:, 0:D].rearrange("p (hh kk) -> p hh kk", kk=DK),
                    axis=AX.X), ["sqsc"], ["rqk"])

            for s in range(SPT):
                sl = rqk[:, s * 16:(s + 1) * 16]
                op("vector", lambda e, sl=sl: e.tensor_scalar(
                    sl, sl, 1.0 / DK, EPS, op0=ALU.mult, op1=ALU.add),
                   ["rqk"], ["rqk"])
                op("scalar", lambda e, sl=sl: e.activation(sl, sl, AF.Ln),
                   ["rqk"], ["rqk"])
                op("scalar", lambda e, sl=sl: e.activation(sl, sl, AF.Exp, scale=-0.5),
                   ["rqk"], ["rqk"])
            for s in range(SPT):
                qv = q_tok.rearrange("p (s f) -> p s f", f=D)[:, s].rearrange(
                    "p (hh kk) -> p hh kk", kk=DK)
                kv = k_tok.rearrange("p (s f) -> p s f", f=D)[:, s].rearrange(
                    "p (hh kk) -> p hh kk", kk=DK)
                rq = rqk[:, s * 16:s * 16 + 8]
                rk = rqk[:, s * 16 + 8:s * 16 + 16]
                op("vector", lambda e, qv=qv, rq=rq: e.tensor_tensor(
                    qv, qv, rq.to_broadcast([128, 8, DK]), ALU.mult),
                   ["q_tok", "rqk"], ["q_tok"])
                op("vector", lambda e, kv=kv, rk=rk: e.tensor_tensor(
                    kv, kv, rk.to_broadcast([128, 8, DK]), ALU.mult),
                   ["k_tok", "rqk"], ["k_tok"])

            if t + 1 < NT:
                ln1(t + 1, 0)

            for s in range(SPT):
                fb = f_t[s]
                fbk = f"f_t{s}"
                op("vector", lambda e, fb=fb, s=s: e.tensor_tensor(
                    fb.rearrange("p (hh kk) -> p hh kk", kk=DK),
                    fb.rearrange("p (hh kk) -> p hh kk", kk=DK),
                    fhead[:, s * H:(s + 1) * H].to_broadcast([128, H, DK]),
                    ALU.mult), [fbk, "fhead"], [fbk])
                op("scalar", lambda e, fb=fb: e.activation(
                    lam[:, 0:D], fb, AF.Exp), [fbk], ["lam"])
                op("vector", lambda e: e.tensor_scalar(
                    lam[:, 0:D], lam[:, 0:D], -1.0, 1.0, op0=ALU.mult, op1=ALU.add),
                   ["lam"], ["lam"])
                kv = k_tok.rearrange("p (s f) -> p s f", f=D)[:, s]
                op("vector", lambda e, kv=kv: e.tensor_mul(kv, kv, lam[:, 0:D]),
                   ["k_tok", "lam"], ["k_tok"])
                sp = s % 2
                bX, bY = pb[4 + 2 * sp], pb[5 + 2 * sp]
                kX, kY = KPB[4 + 2 * sp], KPB[5 + 2 * sp]
                fv = fb.rearrange("p (hh kk) -> p hh kk", kk=DK)
                q_s = q_tok.rearrange("p (s f) -> p s f", f=D)[:, s]
                k_s = k_tok.rearrange("p (s f) -> p s f", f=D)[:, s]
                v_s = v_tok.rearrange("p (s f) -> p s f", f=D)[:, s]
                v_sv = v_s.rearrange("p (hh kk) -> p hh kk", kk=DK)
                for half, (bk, kk) in enumerate(((bX, kX), (bY, kY))):
                    op("tensor", lambda e, bk=bk, half=half, fb=fb: e.matmul(
                        bk[0:1, 0:512], colsel[:, 1:2],
                        fb[:, half * 512:(half + 1) * 512],
                        start=True, stop=True), [fbk, "colsel"], [kk])
                    op("vector", lambda e, bk=bk, half=half: e.tensor_copy(
                        rbuf[0:1, half * 512:(half + 1) * 512], bk[0:1, 0:512]),
                       [kk], ["rbuf"])
                for hh in range(H):
                    op("tensor", lambda e, hh=hh, fv=fv, bX=bX: e.matmul(
                        bX[:, hh * 2:hh * 2 + 2], fv[:, hh], colsel,
                        start=(hh == 0), stop=(hh == 7)), [fbk, "colsel"], [kX])
                op("vector", lambda e, bX=bX: e.tensor_copy(dstage, bX[:, 0:16]),
                   [kX], ["dstage"])
                ds2 = dstage.rearrange("p (hh two) -> p hh two", two=2)
                op("vector", lambda e: e.tensor_copy(dex[:, 0:8], ds2[:, :, 0]),
                   ["dstage"], ["dex"])
                op("vector", lambda e: e.tensor_copy(dex[:, 16:24], ds2[:, :, 1]),
                   ["dstage"], ["dex"])
                op("vector", lambda e: e.tensor_sub(dex[:, 8:16], dex[:, 0:8],
                                                    dex[:, 16:24]), ["dex"], ["dex"])
                op("scalar", lambda e: e.activation(dex[:, 0:24], dex[:, 0:24],
                                                    AF.Exp), ["dex"], ["dex"])
                op("vector", lambda e: e.tensor_scalar(
                    gkn, dex[:, 8:16], qknw[:, 1:2], None, op0=ALU.mult),
                   ["dex", "qknw"], ["gkn"])
                op("vector", lambda e: e.tensor_mul(qcf, dex[:, 16:24], cumD),
                   ["dex", "cumD"], ["qcf"])
                for half, (bk, kk) in enumerate(((bX, kX), (bY, kY))):
                    op("tensor", lambda e, bk=bk, half=half, fb=fb:
                       e.matmul(bk[:, 0:512], tri,
                                fb[:, half * 512:(half + 1) * 512],
                                start=True, stop=False), [fbk, "tri"], [kk])
                    op("tensor", lambda e, bk=bk, half=half:
                       e.matmul(bk[:, 0:512], negr[0:1, 0:128],
                                rbuf[0:1, half * 512:(half + 1) * 512],
                                start=False, stop=True), ["negr", "rbuf"], [kk])
                    op("vector", lambda e, bk=bk, half=half: e.tensor_scalar(
                        ppt_s[:, half * 512:(half + 1) * 512], bk[:, 0:512],
                        CLAMP, -CLAMP, op0=ALU.min, op1=ALU.max), [kk], ["ppt_s"])
                op("scalar", lambda e: e.activation(eqs_s, ppt_s, AF.Exp),
                   ["ppt_s"], ["eqs_s"])
                op("scalar", lambda e: e.activation(ppt_s, ppt_s, AF.Exp, scale=-1.0),
                   ["ppt_s"], ["ppt_s"])
                op("vector", lambda e, q_s=q_s: e.tensor_mul(eqs_s, q_s, eqs_s),
                   ["q_tok", "eqs_s"], ["eqs_s"])
                op("vector", lambda e, k_s=k_s: e.tensor_mul(ppt_s, k_s, ppt_s),
                   ["k_tok", "ppt_s"], ["ppt_s"])
                for src_t, srck, dst_t, dstk, wcol in (
                        (eqs_s, "eqs_s", qfm_s, "qfm_s", qknw[:, 0:1]),
                        (ppt_s, "ppt_s", kfm_s, "kfm_s", qknw[:, 1:2])):
                    for half, (bk, kk) in enumerate(((bX, kX), (bY, kY))):
                        for j in range(4):
                            hh = half * 4 + j
                            op("tensor", lambda e, bk=bk, j=j, hh=hh, src_t=src_t,
                               f=(j == 0), l=(j == 3):
                               e.matmul(bk[:, j * 128:(j + 1) * 128],
                                        src_t[:, hh * 128:(hh + 1) * 128],
                                        ident, is_transpose=True,
                                        start=f, stop=l), [srck, "ident"], [kk])
                        op("scalar", lambda e, bk=bk, half=half, dst_t=dst_t,
                           wcol=wcol: e.activation(
                               dst_t[:, half * 512:(half + 1) * 512], bk[:, 0:512],
                               AF.Copy, scale=wcol), [kk, "qknw"], [dstk])
                for half, (bk, kk) in enumerate(((bX, kX), (bY, kY))):
                    for j in range(4):
                        hh = half * 4 + j
                        op("tensor", lambda e, bk=bk, j=j, hh=hh, f=(j == 0),
                           l=(j == 3): e.matmul(
                               bk[:, j * 128:(j + 1) * 128],
                               kfm_s[:, hh * 128:(hh + 1) * 128],
                               qfm_s[:, hh * 128:(hh + 1) * 128],
                               start=f, stop=l), ["kfm_s", "qfm_s"], [kk])
                    op("vector", lambda e, bk=bk, half=half: e.tensor_mul(
                        ams_s[:, half * 512:(half + 1) * 512], bk[:, 0:512], tri4),
                       [kk, "tri4"], ["ams_s"])
                op("vector", lambda e: e.tensor_tensor(
                    stv_s.rearrange("p (hh vv) -> p hh vv", vv=DK),
                    S_loc.rearrange("p (hh vv) -> p hh vv", vv=DK),
                    dex[:, 16:24].to_broadcast([128, 8, DK]), ALU.mult),
                   ["S_loc", "dex"], ["stv_s"])
                for half, (bk, kk) in enumerate(((bX, kX), (bY, kY))):
                    for j in range(4):
                        hh = half * 4 + j
                        op("tensor", lambda e, bk=bk, j=j, hh=hh, v_sv=v_sv,
                           f=(j == 0):
                           e.matmul(bk[:, j * 128:(j + 1) * 128], v_sv[:, hh],
                                    ams_s[:, hh * 128:(hh + 1) * 128],
                                    start=f, stop=False),
                           ["v_tok", "ams_s"], [kk])
                        op("tensor", lambda e, bk=bk, j=j, hh=hh, l=(j == 3):
                           e.matmul(bk[:, j * 128:(j + 1) * 128],
                                    stv_s[:, hh * 128:(hh + 1) * 128],
                                    qfm_s[:, hh * 128:(hh + 1) * 128],
                                    start=False, stop=l),
                           ["stv_s", "qfm_s"], [kk])
                    eng_e = "vector" if half == 0 else "scalar"
                    if half == 0:
                        op("vector", lambda e, bk=bk: e.tensor_copy(
                            o_sub[:, 0:512], bk[:, 0:512]), [kk], ["o_sub"])
                    else:
                        op("scalar", lambda e, bk=bk: e.copy(
                            o_sub[:, 512:1024], bk[:, 0:512]), [kk], ["o_sub"])
                for half, (bk, kk) in enumerate(((bX, kX), (bY, kY))):
                    for j in range(4):
                        hh = half * 4 + j
                        op("tensor", lambda e, bk=bk, j=j, hh=hh, v_sv=v_sv,
                           f=(j == 0),
                           l=(j == 3): e.matmul(
                               bk[:, j * 128:(j + 1) * 128],
                               ppt_s[:, hh * 128:(hh + 1) * 128], v_sv[:, hh],
                               start=f, stop=l), ["ppt_s", "v_tok"], [kk])
                    op("vector", lambda e, bk=bk, half=half: e.tensor_tensor(
                        stv_s.rearrange("p (hh vv) -> p hh vv", vv=DK)[
                            :, half * 4:(half + 1) * 4],
                        bk[:, 0:512].rearrange("p (j vv) -> p j vv", vv=DK),
                        gkn[:, half * 4:(half + 1) * 4].to_broadcast([128, 4, DK]),
                        ALU.mult), [kk, "gkn"], ["stv_s"])
                op("vector", lambda e: e.tensor_tensor(
                    qc_sub.rearrange("p (hh i) -> p hh i", i=CT),
                    qfm_s.rearrange("p (hh i) -> p hh i", i=CT),
                    qcf.to_broadcast([128, 8, CT]), ALU.mult),
                   ["qfm_s", "qcf"], ["qc_sub"])
                op("vector", lambda e: e.tensor_tensor(
                    S_loc.rearrange("p (hh vv) -> p hh vv", vv=DK),
                    S_loc.rearrange("p (hh vv) -> p hh vv", vv=DK),
                    dex[:, 0:8].to_broadcast([128, 8, DK]), ALU.mult),
                   ["S_loc", "dex"], ["S_loc"])
                op("vector", lambda e: e.tensor_add(S_loc, S_loc, stv_s),
                   ["S_loc", "stv_s"], ["S_loc"])
                op("vector", lambda e: e.tensor_mul(cumD, cumD, dex[:, 0:8]),
                   ["cumD", "dex"], ["cumD"])
                sub0 = tok0 + s * CT
                dma("scalar", o_sp.rearrange("hh v tt -> v hh tt")[:, :, sub0:sub0 + CT],
                    o_sub.rearrange("p (hh i) -> p hh i", i=CT), ["o_sub"], ["o_sp_d"])
                dma("scalar", qc_sp.rearrange("hh k tt -> k hh tt")[:, :, sub0:sub0 + CT],
                    qc_sub.rearrange("p (hh i) -> p hh i", i=CT), ["qc_sub"], ["qc_sp_d"])
                if t + 1 < NT and s + 1 < SPT:
                    ln1(t + 1, s + 1)

        dma("gpsimd", ag_in[0:H * DK * DK].rearrange("(hh k c) -> k hh c",
                                                      k=DK, c=DK),
            S_loc.rearrange("p (hh vv) -> p hh vv", vv=DK), ["S_loc"], ["ag_in_d"])
        dma("gpsimd", ag_in[H * DK * DK:].rearrange("(k hh) -> k hh", hh=H),
            cumD, ["cumD"], ["ag_in_d"])
        tk.add("gpsimd", lambda e: e.collective_compute(
            "AllGather", ALU.bypass, replica_groups=[list(range(NCORE))],
            ins=[ag_in], outs=[ag_out]),
            reads=["ag_in_d"], writes=["ag_out_d"], cc=True)
        ar.mark()
        gS = A(H * DK)
        gD = A(H)
        op("vector", lambda e: e.memset(cumc, 1.0), [], ["cumc"])
        AGW = H * DK * DK + H * DK
        for j in range(NCORE - 1, -1, -1):
            base = j * AGW
            dma("sync", gS.rearrange("p (hh vv) -> p hh vv", vv=DK),
                ag_out[base:base + H * DK * DK]
                .rearrange("(hh k c) -> k hh c", k=DK, c=DK),
                ["ag_out_d"], ["gS"])
            dma("sync", gD,
                ag_out[base + H * DK * DK:base + AGW]
                .rearrange("(k hh) -> k hh", hh=H), ["ag_out_d"], ["gD"])
            pmj = pmb[:, j:j + 1]
            op("vector", lambda e, pmj=pmj: e.tensor_scalar(
                stv_s[:, 0:H], cumc[:, 0:H], pmj, None, op0=ALU.mult),
               ["pmb", "cumc"], ["stv_s"])
            op("vector", lambda e: e.tensor_tensor(
                ppt_s.rearrange("p (hh vv) -> p hh vv", vv=DK),
                gS.rearrange("p (hh vv) -> p hh vv", vv=DK),
                stv_s[:, 0:H].to_broadcast([128, H, DK]), ALU.mult),
               ["gS", "stv_s"], ["ppt_s"])
            op("vector", lambda e: e.tensor_add(S_in, S_in, ppt_s),
               ["S_in", "ppt_s"], ["S_in"])
            op("vector", lambda e: e.tensor_scalar_add(
                stv_s[:, 16:16 + H], gD[:, 0:H], -1.0), ["gD"], ["stv_s"])
            op("vector", lambda e, pmj=pmj: e.tensor_scalar(
                stv_s[:, 16:16 + H], stv_s[:, 16:16 + H], pmj, 1.0,
                op0=ALU.mult, op1=ALU.add), ["stv_s", "pmb"], ["stv_s"])
            op("vector", lambda e: e.tensor_mul(
                cumc[:, 0:H], cumc[:, 0:H], stv_s[:, 16:16 + H]),
               ["cumc", "stv_s"], ["cumc"])
        ar.release()

        tk.barrier()
        ar.release()

        ar.mark()
        o2 = A(H * 512)
        qc2 = A(H * 512)
        xtile = A(8 * 512)
        x1t = A(8 * 512)
        h2t = A(8 * 512)
        httok = [A(DW) for _ in range(2)]
        wo_s = A(8 * D)
        osq = [A(512) for _ in range(2)]
        gsc = A(8)
        mx8g = A(8)
        selw = A(16)
        ilv = A(16)
        dma("sync", wo_s.rearrange("p (c f) -> p c f", f=D),
            wo_d.rearrange("(c p) f -> p c f", p=128), [], ["wo_s"])

        for t in range(NT):
            tok0 = t * TILE
            dma("sync", o2.rearrange("p (hh tt) -> p hh tt", tt=512),
                o_sp.rearrange("hh v tt -> v hh tt")[:, :, tok0:tok0 + TILE],
                ["o_sp_d"], ["o2"])
            dma("sync", qc2.rearrange("p (hh tt) -> p hh tt", tt=512),
                qc_sp.rearrange("hh k tt -> k hh tt")[:, :, tok0:tok0 + TILE],
                ["qc_sp_d"], ["qc2"])
            dma("sync", xtile.rearrange("p (c tt) -> p c tt", tt=512),
                xin[:, tok0:tok0 + TILE].rearrange("(c p) tt -> p c tt", p=128),
                [], ["xtile"])
            for hh in range(H):
                par = hh % 2
                Ssl = S_in.rearrange("p (hh vv) -> p hh vv", vv=DK)[:, hh]
                qcs = qc2.rearrange("p (hh tt) -> p hh tt", tt=512)[:, hh]
                osl = o2.rearrange("p (hh tt) -> p hh tt", tt=512)[:, hh]
                op("tensor", lambda e, Ssl=Ssl, qcs=qcs, par=par: e.matmul(
                    pb[par][:, 0:512], Ssl, qcs, start=True, stop=True),
                   ["S_in", "qc2"], [KPB[par]])
                op("vector", lambda e, osl=osl, par=par: e.tensor_add(
                    osl, osl, pb[par][:, 0:512]), ["o2", KPB[par]], ["o2"])
            for hh in range(H):
                par = hh % 2
                osl = o2.rearrange("p (hh tt) -> p hh tt", tt=512)[:, hh]
                op("scalar", lambda e, osl=osl, par=par: e.activation(
                    osq[par], osl, AF.Square), ["o2"], [f"osq{par}"])
                op("tensor", lambda e, par=par, f=(hh == 0), l=(hh == 7): e.matmul(
                    pb[2][0:1, 0:512], colsel[:, 0:1], osq[par],
                    start=f, stop=l), [f"osq{par}", "colsel"], [KPB[2]])
            rr2 = prow[0:1, 0:512]
            op("vector", lambda e: e.tensor_scalar(
                rr2, pb[2][0:1, 0:512], 1.0 / D, EPS, op0=ALU.mult, op1=ALU.add),
               [KPB[2]], ["prow"])
            op("scalar", lambda e: e.activation(rr2, rr2, AF.Ln), ["prow"], ["prow"])
            op("scalar", lambda e: e.activation(rr2, rr2, AF.Exp, scale=-0.5),
               ["prow"], ["prow"])
            op("tensor", lambda e: e.matmul(
                pb[3][:, 0:512], onesr[0:1, 0:128], rr2, start=True, stop=True),
               ["prow", "onesr"], [KPB[3]])
            for hh in range(H):
                osl = o2.rearrange("p (hh tt) -> p hh tt", tt=512)[:, hh]
                op("vector", lambda e, osl=osl, hh=hh: e.scalar_tensor_tensor(
                    osl, osl, gnwc[:, hh:hh + 1], pb[3][:, 0:512],
                    op0=ALU.mult, op1=ALU.mult), ["o2", "gnwc", KPB[3]], ["o2"])
            for c in range(8):
                par = c % 2
                for hv in range(8):
                    op("tensor", lambda e, c=c, hv=hv, par=par, f=(hv == 0), l=(hv == 7):
                       e.matmul(pb[4 + par][:, 0:512],
                                wo_s.rearrange("p (cc f) -> p cc f", f=D)[
                                    :, hv, c * 128:(c + 1) * 128],
                                o2.rearrange("p (hh tt) -> p hh tt", tt=512)[:, hv],
                                start=f, stop=l), ["wo_s", "o2"], [KPB[4 + par]])
                x1c = x1t.rearrange("p (c tt) -> p c tt", tt=512)[:, c]
                xc = xtile.rearrange("p (c tt) -> p c tt", tt=512)[:, c]
                op("vector", lambda e, x1c=x1c, xc=xc, c=c, par=par:
                   e.scalar_tensor_tensor(x1c, pb[4 + par][:, 0:512], modcol(2, c),
                                          xc, op0=ALU.mult, op1=ALU.add),
                   [KPB[4 + par], "modc", "xtile"], ["x1t"])
            dma("scalar", x1_sp.rearrange("c p tt -> p c tt")[:, :, tok0:tok0 + TILE],
                x1t.rearrange("p (c tt) -> p c tt", tt=512), ["x1t"], ["x1_sp_d"])
            x1v = x1t.rearrange("p (c tt) -> p c tt", tt=512)
            for c in range(8):
                op("tensor", lambda e, c=c, x1v=x1v, f=(c == 0), l=(c == 7):
                   e.matmul(pb[2][0:1, 0:512], colsel[:, 0:1], x1v[:, c],
                            start=f, stop=l), ["x1t", "colsel"], [KPB[2]])
            for c in range(8):
                par = c % 2
                op("scalar", lambda e, c=c, x1v=x1v, par=par: e.activation(
                    osq[par], x1v[:, c], AF.Square), ["x1t"], [f"osq{par}"])
                op("tensor", lambda e, c=c, par=par, f=(c == 0), l=(c == 7):
                   e.matmul(pb[3][0:1, 0:512], colsel[:, 0:1], osq[par],
                            start=f, stop=l), [f"osq{par}", "colsel"], [KPB[3]])
            m2 = prow[0:1, 0:512]; r2 = prow[0:1, 512:1024]; mr2 = prow[0:1, 1024:1536]
            op("vector", lambda e: e.tensor_scalar_mul(m2, pb[2][0:1, 0:512], 1.0 / D),
               [KPB[2]], ["prow"])
            op("vector", lambda e: e.tensor_mul(r2, m2, m2), ["prow"], ["prow"])
            op("vector", lambda e: e.scalar_tensor_tensor(
                r2, pb[3][0:1, 0:512], 1.0 / D, r2, op0=ALU.mult, op1=ALU.subtract),
               [KPB[3], "prow"], ["prow"])
            op("scalar", lambda e: e.activation(r2, r2, AF.Ln, bias=EPS),
               ["prow"], ["prow"])
            op("scalar", lambda e: e.activation(r2, r2, AF.Exp, scale=-0.5),
               ["prow"], ["prow"])
            op("vector", lambda e: e.tensor_mul(mr2, m2, r2), ["prow"], ["prow"])
            for c in range(8):
                op("tensor", lambda e, c=c: e.matmul(
                    pb[2][:, 0:512], modrow_sl(4, c), r2, start=True, stop=True),
                   ["modrow", "prow"], [KPB[2]])
                op("tensor", lambda e, c=c: e.matmul(
                    pb[3][:, 0:512], modrow_sl(4, c), mr2, start=True, stop=False),
                   ["modrow", "prow"], [KPB[3]])
                op("tensor", lambda e, c=c: e.matmul(
                    pb[3][:, 0:512], modrow_sl(3, c), negr[0:1, 0:512],
                    start=False, stop=True), ["modrow", "negr"], [KPB[3]])
                h2c = h2t.rearrange("p (c tt) -> p c tt", tt=512)[:, c]
                op("vector", lambda e, h2c=h2c, c=c, x1v=x1v: e.tensor_mul(
                    h2c, x1v[:, c], pb[2][:, 0:512]), ["x1t", KPB[2]], ["h2t"])
                op("vector", lambda e, h2c=h2c: e.tensor_sub(
                    h2c, h2c, pb[3][:, 0:512]), ["h2t", KPB[3]], ["h2t"])
            dma("scalar", h2_sp.rearrange("c p tt -> p c tt")[:, :, tok0:tok0 + TILE],
                h2t.rearrange("p (c tt) -> p c tt", tt=512), ["h2t"], ["h2_sp_d"])
            h2vv = h2t.rearrange("p (c tt) -> p c tt", tt=512)
            for s in range(SPT):
                hpar = s % 2
                hb = httok[hpar]
                hbk = f"httok{hpar}"
                for c in range(8):
                    tp = c % 2
                    op("tensor", lambda e, c=c, s=s, tp=tp, h2vv=h2vv: e.matmul(
                        pb[tp][:, 0:128], h2vv[:, c, s * CT:(s + 1) * CT], ident,
                        is_transpose=True, start=True, stop=True),
                       ["h2t", "ident"], [KPB[tp]])
                    op("scalar", lambda e, c=c, tp=tp, hb=hb: e.copy(
                        hb[:, c * 128:(c + 1) * 128], pb[tp][:, 0:128]),
                       [KPB[tp]], [hbk])
                sub0 = tok0 + s * CT
                op("vector", lambda e, hb=hb, sub0=sub0: e.tensor_scalar_add(
                    hb[:, D:D + 1], iotac, float(sub0)), ["iotac"], [hbk])
                dma("scalar", h2tok_d[sub0:sub0 + CT, :], hb, [hbk], ["h2tok_dd"])
            h2v = h2t.rearrange("p (c tt) -> p c tt", tt=512)
            for s in range(SPT):
                for c in range(8):
                    op("tensor", lambda e, c=c, s=s, h2v=h2v, f=(c == 0):
                       e.matmul(pb[6][:, 0:H], h2v[:, c, s * CT:(s + 1) * CT],
                                gw_s.rearrange("p (c hh) -> p c hh", hh=H)[:, c],
                                start=f, stop=False), ["h2t", "gw_s"], [KPB[6]])
                op("tensor", lambda e: e.matmul(
                    pb[6][:, 0:H], onesr[0:1, 0:128], rowsc[0:1, 16:24],
                    start=False, stop=True), ["onesr", "rowsc"], [KPB[6]])
                op("scalar", lambda e: e.activation(gsc, pb[6][:, 0:H], AF.Sigmoid),
                   [KPB[6]], ["gsc"])
                op("vector", lambda e: e.max(mx8g, gsc), ["gsc"], ["mx8g"])
                op("vector", lambda e: e.tensor_scalar(
                    selw[:, 0:8], gsc, mx8g[:, 1:2], None, op0=ALU.is_ge),
                   ["gsc", "mx8g"], ["selw"])
                op("vector", lambda e: e.tensor_add(
                    selw[:, 8:9], mx8g[:, 0:1], mx8g[:, 1:2]), ["mx8g"], ["selw"])
                op("vector", lambda e: e.tensor_scalar_add(
                    selw[:, 8:9], selw[:, 8:9], 1e-8), ["selw"], ["selw"])
                op("vector", lambda e: e.reciprocal(selw[:, 9:10], selw[:, 8:9]),
                   ["selw"], ["selw"])
                ilv2 = ilv.rearrange("p (ee two) -> p ee two", two=2)
                op("vector", lambda e, ilv2=ilv2: e.tensor_mul(
                    ilv2[:, :, 0], selw[:, 0:8], gsc), ["selw", "gsc"], ["ilv"])
                op("vector", lambda e, ilv2=ilv2: e.tensor_scalar(
                    ilv2[:, :, 0], ilv2[:, :, 0], selw[:, 9:10], None, op0=ALU.mult),
                   ["ilv", "selw"], ["ilv"])
                op("vector", lambda e, ilv2=ilv2: e.tensor_copy(
                    ilv2[:, :, 1], selw[:, 0:8]), ["selw"], ["ilv"])
                fs = t * SPT + s
                op("vector", lambda e, fs=fs: e.tensor_copy(
                    selcols.rearrange("p (ee f) -> p ee f", f=NSUB)[:, :, fs],
                    selw[:, 0:8]), ["selw"], ["selcols"])
                first = (t == 0 and s == 0)
                last = (t == NT - 1 and s == SPT - 1)
                op("tensor", lambda e, f=first, l=last: e.matmul(
                    pb[7][0:16, 0:1], ilv, colsel[:, 0:1],
                    start=f, stop=l), ["ilv", "colsel"], [KPB[7]])

        op("vector", lambda e: e.tensor_copy(g16[0:16, 0:1], pb[7][0:16, 0:1]),
           [KPB[7]], ["g16"])
        dma("gpsimd", ar_in[:, None], g16[0:16, 0:1], ["g16"], ["ar_in_d"])
        tk.add("gpsimd", lambda e: e.collective_compute(
            "AllReduce", ALU.add, replica_groups=[list(range(NCORE))],
            ins=[ar_in], outs=[ar_out]), reads=["ar_in_d"], writes=["ar_out_d"],
            cc=True)
        tk.barrier()
        ar.release()

        ar.mark()
        sgw_s = A(8 * D); suw_s = A(8 * D); sdw_s = A(8 * D)
        h2l = A(8 * 512)
        zt = A(8 * 512)
        zg = [A(512) for _ in range(2)]
        ysh = [A(D) for _ in range(2)]
        csr2 = prow[:, 0:8 * NSUB]
        cexr = prow[:, 512:512 + 8 * NSUB]
        ccolT = A(NSUB)
        ccolF = A(8)
        cexf = A(8)
        postot = A(8 * NSUB)
        nfbc = A(8)
        nff_f = A(8)
        tailf = A(32)
        ztile = A(DW)
        rstage = [A(DW) for _ in range(2)]

        NW = 8 * NSUB
        BNDREG = {}

        def _gp_pro(eng):
            r = eng.alloc_register(name="bndcape")
            eng.reg_mov(r, CAPE - 1)
            BNDREG["r"] = r
        tk.prologue["gpsimd"] = _gp_pro
        op("tensor", lambda e: e.matmul(pb[1][0:1, 0:NW], colsel[:, 0:1],
                                        selcols, start=True, stop=True),
           ["selcols", "colsel"], [KPB[1]])
        op("vector", lambda e: e.tensor_copy(csr2[0:1, :], pb[1][0:1, 0:NW]),
           [KPB[1]], ["csr2", "prow"])
        op("vector", lambda e: e.reduce_sum(
            nff_f[0:1, 0:8],
            csr2[0:1, :].rearrange("o (ee f) -> o ee f", f=NSUB),
            axis=AX.X), ["csr2"], ["nff"])
        gdep_op = tk.add("vector", lambda e: e.tensor_copy(
            nfi8[0:1, 0:8], nff_f[0:1, 0:8]), reads=["nff"], writes=["nfi8"])
        tk.guard_src = lambda e_i: nfi8[0:1, e_i:e_i + 1]
        tk.guard_dep = gdep_op
        dma("scalar", rowx_d[None, :], csr2[0:1, :], ["csr2"], ["rowx"])
        dma("scalar", ccolT[0:8, :], rowx_d.rearrange("(p f) -> p f", p=8),
            ["rowx"], ["ccolT"])
        op("tensor", lambda e: e.matmul(
            pb[1][0:NSUB, 256:264], ccolT[0:8, :], ident[0:8, 0:8],
            is_transpose=True, start=True, stop=True), ["ccolT", "ident"], [KPB[1]])
        op("vector", lambda e: e.tensor_copy(ccolF[0:NSUB, 0:8],
                                             pb[1][0:NSUB, 256:264]),
           [KPB[1]], ["ccolF"])
        op("tensor", lambda e: e.matmul(
            pb[1][0:NSUB, 264:272], t32[0:NSUB, 0:NSUB], ccolF[0:NSUB, 0:8],
            start=True, stop=True), ["t32", "ccolF"], [KPB[1]])
        op("vector", lambda e: e.tensor_copy(cexf[0:NSUB, 0:8],
                                             pb[1][0:NSUB, 264:272]),
           [KPB[1]], ["cexf"])
        dma("scalar", rowx2_d.rearrange("(ee f) -> f ee", f=NSUB),
            cexf[0:NSUB, 0:8], ["cexf"], ["rowx2"], slow=True)
        dma("scalar", cexr[0:1, :], rowx2_d[None, :], ["rowx2"], ["cexr"])
        op("tensor", lambda e: e.matmul(pb[0][:, 0:NW], tri, selcols,
                                        start=True, stop=False),
           ["tri", "selcols"], [KPB[0]])
        op("tensor", lambda e: e.matmul(pb[0][:, 0:NW], onesr[0:1, 0:128],
                                        cexr[0:1, :], start=False, stop=True),
           ["onesr", "cexr"], [KPB[0]])
        op("vector", lambda e: e.tensor_copy(postot, pb[0][:, 0:NW]),
           [KPB[0]], ["postot"])
        op("vector", lambda e: e.tensor_sub(postot, postot, selcols),
           ["postot", "selcols"], ["postot"])
        op("vector", lambda e: e.scalar_tensor_tensor(
            postot, selcols, -BIGP, postot, op0=ALU.mult, op1=ALU.add),
           ["selcols", "postot"], ["postot"])
        op("vector", lambda e: e.tensor_scalar_add(postot, postot, BIGP),
           ["postot"], ["postot"])
        op("vector", lambda e: e.tensor_copy(idxi[:, :], postot),
           ["postot"], ["idxi"])
        op("tensor", lambda e: e.matmul(pb[1][:, 272:280], onesr[0:1, 0:128],
                                        nff_f[0:1, 0:8], start=True, stop=True),
           ["onesr", "nff"], [KPB[1]])
        op("vector", lambda e: e.tensor_copy(nfbc, pb[1][:, 272:280]),
           [KPB[1]], ["nfbc"])
        for e_i in range(8):
            for j in range(4):
                op("vector", lambda e, e_i=e_i, j=j: e.scalar_tensor_tensor(
                    tailf[:, e_i * 4 + j:e_i * 4 + j + 1], iotac, float(j * 128),
                    nfbc[:, e_i:e_i + 1], op0=ALU.add, op1=ALU.add),
                   ["iotac", "nfbc"], ["tailf"])
        op("vector", lambda e: e.tensor_copy(idxt[:, :], tailf),
           ["tailf"], ["idxt"])
        op("vector", lambda e: e.memset(ztile, 0.0), [], ["ztile"])
        op("vector", lambda e: e.memset(ztile[:, D:D + 1], BIGP), [], ["ztile"])
        op("vector", lambda e: e.memset(postot[:, 0:4], BIGP), ["postot"],
           ["postot"])
        op("vector", lambda e: e.tensor_copy(bigi[:, :], postot[:, 0:4]),
           ["postot"], ["bigi"])
        for fs in range(NSUB):
            rb = rstage[fs % 2]
            rbk = f"rstage{fs % 2}"
            dma("gpsimd", rb, h2tok_d[fs * CT:(fs + 1) * CT, :],
                ["h2tok_dd"], [rbk])
            for e_i in range(8):
                idxcol = idxi[:, e_i * NSUB + fs:e_i * NSUB + fs + 1]
                tk.add("gpsimd", lambda e, e_i=e_i, rb=rb, idxcol=idxcol:
                       e.indirect_dma_start(
                           out=compact_d[e_i],
                           out_offset=IndirectOffsetOnAxis(idxcol, 0),
                           in_=rb, in_offset=None,
                           bounds_check=BNDREG["r"], oob_is_err=False),
                       reads=[rbk, "idxi"], writes=[f"cp{e_i}"], dma=True,
                       tag=f"cp{e_i}")
        for e_i in range(8):
            for j in range(4):
                idc = idxt[:, e_i * 4 + j:e_i * 4 + j + 1]
                tk.add("gpsimd", lambda e, e_i=e_i, idc=idc:
                       e.indirect_dma_start(
                           out=compact_d[e_i],
                           out_offset=IndirectOffsetOnAxis(idc, 0),
                           in_=ztile, in_offset=None,
                           bounds_check=BNDREG["r"], oob_is_err=False),
                       reads=["ztile", "idxt"], writes=[f"cp{e_i}"], dma=True,
                       tag=f"cp{e_i}")
        dma("sync", sgw_s.rearrange("p (c f) -> p c f", f=D),
            sgwt.rearrange("(c p) f -> p c f", p=128), [], ["sgw_s"])
        dma("sync", suw_s.rearrange("p (c f) -> p c f", f=D),
            suwt.rearrange("(c p) f -> p c f", p=128), [], ["suw_s"])
        dma("sync", sdw_s.rearrange("p (c f) -> p c f", f=D),
            sdwt.rearrange("(c p) f -> p c f", p=128), [], ["sdw_s"])
        for t in range(NT):
            tok0 = t * TILE
            dma("sync", h2l.rearrange("p (c tt) -> p c tt", tt=512),
                h2_sp.rearrange("c p tt -> p c tt")[:, :, tok0:tok0 + TILE],
                ["h2_sp_d"], ["h2l"])
            h2v = h2l.rearrange("p (c tt) -> p c tt", tt=512)
            for ic in range(8):
                gp_ = ic % 2
                for c in range(8):
                    op("tensor", lambda e, ic=ic, c=c, h2v=h2v, gp_=gp_,
                       f=(c == 0), l=(c == 7):
                       e.matmul(pb[gp_][:, 0:512],
                                sgw_s.rearrange("p (cc f) -> p cc f", f=D)[
                                    :, c, ic * 128:(ic + 1) * 128],
                                h2v[:, c], start=f, stop=l),
                       ["sgw_s", "h2l"], [KPB[gp_]])
                op("scalar", lambda e, gp_=gp_: e.activation(
                    zg[gp_], pb[gp_][:, 0:512], AF.Silu), [KPB[gp_]], [f"zg{gp_}"])
                for c in range(8):
                    op("tensor", lambda e, ic=ic, c=c, h2v=h2v, gp_=gp_,
                       f=(c == 0), l=(c == 7):
                       e.matmul(pb[2 + gp_][:, 0:512],
                                suw_s.rearrange("p (cc f) -> p cc f", f=D)[
                                    :, c, ic * 128:(ic + 1) * 128],
                                h2v[:, c], start=f, stop=l),
                       ["suw_s", "h2l"], [KPB[2 + gp_]])
                zsl = zt.rearrange("p (ic tt) -> p ic tt", tt=512)[:, ic]
                op("vector", lambda e, zsl=zsl, gp_=gp_: e.tensor_mul(
                    zsl, zg[gp_], pb[2 + gp_][:, 0:512]),
                   [f"zg{gp_}", KPB[2 + gp_]], ["zt"])
            for s4 in range(SPT):
                ypar = s4 % 2
                yb = ysh[ypar]
                ybk = f"ysh{ypar}"
                for half in range(2):
                    par2 = (s4 * 2 + half) % 2
                    for ic in range(8):
                        op("tensor", lambda e, s4=s4, half=half, ic=ic, par2=par2,
                           f=(ic == 0), l=(ic == 7):
                           e.matmul(pb[4 + par2][:, 0:512],
                                    zt.rearrange("p (ic tt) -> p ic tt", tt=512)[
                                        :, ic, s4 * CT:(s4 + 1) * CT],
                                    sdw_s.rearrange("p (cc f) -> p cc f", f=D)[
                                        :, ic, half * 512:(half + 1) * 512],
                                    start=f, stop=l), ["zt", "sdw_s"],
                           [KPB[4 + par2]])
                    op("vector", lambda e, half=half, yb=yb, par2=par2:
                       e.tensor_copy(yb[:, half * 512:(half + 1) * 512],
                                     pb[4 + par2][:, 0:512]),
                       [KPB[4 + par2]], [ybk])
                sub0 = tok0 + s4 * CT
                dma("scalar", acc_tok_d[sub0:sub0 + CT, :], yb, [ybk],
                    ["acc_tok"])
        tk.barrier()
        ar.release()

        dma("gpsimd", wm2[0:8, :], ar_out.rearrange("(ee two) -> ee two", two=2),
            ["ar_out_d"], ["wm2"])
        op("vector", lambda e: e.tensor_scalar_max(wm2[0:8, 1:2], wm2[0:8, 1:2], 1.0),
           ["wm2"], ["wm2"])
        op("vector", lambda e: e.reciprocal(wm2[0:8, 1:2], wm2[0:8, 1:2]),
           ["wm2"], ["wm2"])
        op("vector", lambda e: e.tensor_mul(wm2[0:8, 0:1], wm2[0:8, 0:1],
                                            wm2[0:8, 1:2]), ["wm2"], ["wm2"])
        dma("gpsimd", wm_d[:, None], wm2[0:8, 0:1], ["wm2"], ["wm_dd"])
        for e_i in range(8):
            dma("gpsimd", wmcol[:, e_i:e_i + 1],
                wm_d[e_i:e_i + 1].partition_broadcast(128), ["wm_dd"], ["wmcol"])
        ar.mark()
        wgd = [A(8 * 512) for _ in range(2)]
        wud = [A(8 * 512) for _ in range(2)]
        wdd = [A(4 * D) for _ in range(2)]
        cblk = [A(4 * DW)]
        h2cm = [A(8 * 512)]
        zcm = [A(4 * 512)]
        zge = [A(512) for _ in range(2)]
        ysb = [A(D) for _ in range(4)]
        def ld_wg(e_i):
            p_ = e_i % 2
            dma("scalar", wgd[p_].rearrange("p (c f) -> p c f", f=512),
                egwt[e_i].rearrange("(c p) f -> p c f", p=128), [],
                [f"wg_s{p_}"])

        def ld_wu(e_i):
            p_ = e_i % 2
            dma("scalar", wud[p_].rearrange("p (c f) -> p c f", f=512),
                euwt[e_i].rearrange("(c p) f -> p c f", p=128), [],
                [f"wu_s{p_}"])

        def ld_wd(e_i):
            p_ = e_i % 2
            dma("scalar", wdd[p_].rearrange("p (c f) -> p c f", f=D),
                edwt[e_i].rearrange("(c p) f -> p c f", p=128), [],
                [f"wdn_s{p_}"])

        ld_wg(0); ld_wu(0); ld_wd(0)
        for e_i in range(8):
            epar = e_i % 2
            wg_s = wgd[epar]; wu_s = wud[epar]; wdn_s = wdd[epar]
            wgk = f"wg_s{epar}"; wuk = f"wu_s{epar}"; wdk = f"wdn_s{epar}"
            tk.add("gpsimd", lambda e: e.nop(),
                   reads=["acc_tok", "acca0", "acca1"],
                   writes=[f"afence{e_i}"])
            for b in range(NBLK):
                if e_i + 1 < 8:
                    if b == min(2, NBLK - 1):
                        ld_wg(e_i + 1)
                    if b == min(4, NBLK - 1):
                        ld_wu(e_i + 1)
                    if b == min(6, NBLK - 1):
                        ld_wd(e_i + 1)
                g = (e_i, b * 512)
                bpar = b % 2
                cb = cblk[0]; cbk = "cblk0"
                hc = h2cm[0]; hck = "h2cm0"
                zb = zcm[0]; zbk = "zcm0"
                tk.add("sync", lambda e, cb=cb, e_i=e_i, b=b: e.dma_start(
                    out=cb.rearrange("p (s f) -> p s f", f=DW),
                    in_=compact_d[e_i][b * 512:(b + 1) * 512, :]
                    .rearrange("(s p) f -> p s f", p=128)),
                    reads=[f"cp{e_i}"], writes=[cbk], dma=True, tag=cbk, guard=g)
                cbv = cb.rearrange("p (s f) -> p s f", f=DW)
                op("vector", lambda e, bpar=bpar: e.tensor_copy(
                    invi[:, bpar * 4:(bpar + 1) * 4], bigi[:, :]),
                   ["bigi"], ["invi"])
                for s4 in range(SPT):
                    for c in range(8):
                        tp = 6 + (c % 2)
                        op2 = tk.add("tensor", lambda e, s4=s4, c=c, tp=tp, cbv=cbv:
                                     e.matmul(pb[tp][:, 0:128],
                                              cbv[:, s4, c * 128:(c + 1) * 128],
                                              ident, is_transpose=True,
                                              start=True, stop=True),
                                     reads=[cbk, "ident"], writes=[KPB[tp]],
                                     guard=g)
                        tk.add("scalar", lambda e, s4=s4, c=c, tp=tp, hc=hc:
                               e.copy(hc.rearrange("p (cc tt) -> p cc tt", tt=512)[
                                   :, c, s4 * CT:(s4 + 1) * CT], pb[tp][:, 0:128]),
                               reads=[KPB[tp]], writes=[hck], guard=g)
                    tk.add("vector", lambda e, s4=s4, bpar=bpar, cbv=cbv:
                           e.tensor_copy(invi[:, bpar * 4 + s4:bpar * 4 + s4 + 1],
                                         cbv[:, s4, D:D + 1]),
                           reads=[cbk], writes=["invi"], guard=g)
                hcv = hc.rearrange("p (cc tt) -> p cc tt", tt=512)
                for ic in range(4):
                    gp_ = ic % 2
                    for c in range(8):
                        tk.add("tensor", lambda e, ic=ic, c=c, hcv=hcv, gp_=gp_,
                               wg_s=wg_s, f=(c == 0), l=(c == 7):
                               e.matmul(pb[gp_][:, 0:512],
                                        wg_s.rearrange("p (cc f) -> p cc f",
                                                       f=512)[
                                            :, c, ic * 128:(ic + 1) * 128],
                                        hcv[:, c], start=f, stop=l),
                               reads=[wgk, hck], writes=[KPB[gp_]], guard=g)
                    tk.add("scalar", lambda e, gp_=gp_: e.activation(
                        zge[gp_], pb[gp_][:, 0:512], AF.Silu),
                        reads=[KPB[gp_]], writes=[f"zge{gp_}"], guard=g)
                    for c in range(8):
                        tk.add("tensor", lambda e, ic=ic, c=c, hcv=hcv, gp_=gp_,
                               wu_s=wu_s, f=(c == 0), l=(c == 7):
                               e.matmul(pb[2 + gp_][:, 0:512],
                                        wu_s.rearrange("p (cc f) -> p cc f",
                                                       f=512)[
                                            :, c, ic * 128:(ic + 1) * 128],
                                        hcv[:, c], start=f, stop=l),
                               reads=[wuk, hck], writes=[KPB[2 + gp_]], guard=g)
                    zsl = zb.rearrange("p (ic tt) -> p ic tt", tt=512)[:, ic]
                    tk.add("vector", lambda e, zsl=zsl, gp_=gp_: e.tensor_mul(
                        zsl, zge[gp_], pb[2 + gp_][:, 0:512]),
                        reads=[f"zge{gp_}", KPB[2 + gp_]], writes=[zbk], guard=g)
                for s4 in range(SPT):
                    yb = ysb[s4]; ybk2 = f"ysb{s4}"
                    for half in range(2):
                        par2 = (s4 * 2 + half) % 2
                        for ic in range(4):
                            tk.add("tensor", lambda e, s4=s4, half=half, ic=ic,
                                   par2=par2, zb=zb, wdn_s=wdn_s,
                                   f=(ic == 0), l=(ic == 3):
                                   e.matmul(pb[4 + par2][:, 0:512],
                                            zb.rearrange("p (ic tt) -> p ic tt",
                                                         tt=512)[
                                                :, ic, s4 * CT:(s4 + 1) * CT],
                                            wdn_s.rearrange("p (cc f) -> p cc f",
                                                            f=D)[
                                                :, ic,
                                                half * 512:(half + 1) * 512],
                                            start=f, stop=l),
                                   reads=[zbk, wdk], writes=[KPB[4 + par2]],
                                   guard=g)
                        tk.add("scalar", lambda e, half=half, yb=yb, par2=par2,
                               e_i=e_i: e.activation(
                                   yb[:, half * 512:(half + 1) * 512],
                                   pb[4 + par2][:, 0:512], AF.Copy,
                                   scale=wmcol[:, e_i:e_i + 1]),
                               reads=[KPB[4 + par2], "wmcol"], writes=[ybk2],
                               guard=g)
                    iv = invi[:, bpar * 4 + s4:bpar * 4 + s4 + 1]
                    ack = f"acca{s4 % 2}"
                    tk.add("gpsimd", lambda e, yb=yb, iv=iv:
                           e.indirect_dma_start(
                               out=acc_tok_d,
                               out_offset=IndirectOffsetOnAxis(iv, 0),
                               in_=yb, in_offset=None,
                               bounds_check=BNDREG["r"], oob_is_err=False,
                               compute_op=ALU.add),
                           reads=[ybk2, "invi", f"afence{e_i}"], writes=[ack],
                           dma=True, tag=ack)
        tk.barrier()
        ar.release()

        ar.mark()
        x1l2 = [A(8 * 512) for _ in range(2)]
        accl2 = [A(4 * D) for _ in range(2)]
        gbbc = A(D)
        tmp4 = [A(D) for _ in range(2)]
        outt = [A(D) for _ in range(2)]
        dma("sync", gbbc, modb_d[5 * D:6 * D].partition_broadcast(128),
            ["modb"], ["gbbc"])
        for t in range(NT):
            tok0 = t * TILE
            tp4 = t % 2
            x1l = x1l2[tp4]
            accl = accl2[tp4]
            x1lk = f"x1l{tp4}"
            acclk = f"accl{tp4}"
            dma("sync", x1l.rearrange("p (c tt) -> p c tt", tt=512),
                x1_sp.rearrange("c p tt -> p c tt")[:, :, tok0:tok0 + TILE],
                ["x1_sp_d"], [x1lk])
            dma("sync", accl.rearrange("p (s f) -> p s f", f=D),
                acc_tok_d[tok0:tok0 + TILE, :].rearrange("(s p) f -> p s f",
                                                         p=128),
                ["acc_tok"], [acclk])
            accv = accl.rearrange("p (s f) -> p s f", f=D)
            x1v4 = x1l.rearrange("p (c tt) -> p c tt", tt=512)
            for s in range(SPT):
                par = s % 2
                ob = outt[par]
                obk = f"outt{par}"
                for c in range(8):
                    tp = 4 + (c % 2)
                    op("tensor", lambda e, c=c, s=s, tp=tp, x1v4=x1v4: e.matmul(
                        pb[tp][:, 0:128], x1v4[:, c, s * CT:(s + 1) * CT],
                        ident, is_transpose=True, start=True, stop=True),
                       [x1lk, "ident"], [KPB[tp]])
                    op("vector", lambda e, c=c, tp=tp, ob=ob: e.tensor_copy(
                        ob[:, c * 128:(c + 1) * 128], pb[tp][:, 0:128]),
                       [KPB[tp]], [obk])
                tb = tmp4[par]
                op("vector", lambda e, s=s, tb=tb, accv=accv: e.tensor_mul(
                    tb, accv[:, s], gbbc), [acclk, "gbbc"], [f"tmp4{par}"])
                op("vector", lambda e, ob=ob, tb=tb: e.tensor_add(ob, ob, tb),
                   [obk, f"tmp4{par}"], [obk])
                sub0 = tok0 + s * CT
                dma("gpsimd", yout[sub0:sub0 + CT, :], ob, [obk], ["yout_d"])
        tk.barrier()
        ar.release()

        globals()["_last_tk"] = tk
        semnames = sorted({o.sem for o in tk.ops})
        sems = {nm: ctx.enter_context(nc.semaphore(nm)) for nm in semnames}
        block = ctx.enter_context(nc.Block())
        tk.emit(block, sems)
    return nc


def _host_prep(inputs, TPC):
    x = np.ascontiguousarray(np.asarray(inputs["L0_attn_hidden_states"],
                                        dtype=np.float32))
    B, T, _ = x.shape
    ident = np.eye(128, dtype=np.float32)
    tri = np.triu(np.ones((128, 128), np.float32))
    tri4 = np.ascontiguousarray(np.tile(tri, (1, 4)))
    colsel = np.zeros((128, 2), np.float32)
    colsel[:, 0] = 1.0
    colsel[:CT // 2, 1] = 1.0
    onesr = np.ones((1, 512), np.float32)
    negr = -np.ones((1, 512), np.float32)
    k32 = np.arange(32)
    t32 = (k32[:, None] < k32[None, :]).astype(np.float32)
    iotac = np.arange(128, dtype=np.float32)[:, None]
    f32c = lambda a: np.ascontiguousarray(np.asarray(a, dtype=np.float32))
    common = {
        "wq": f32c(inputs["L0_attn_Wq"]), "wk": f32c(inputs["L0_attn_Wk"]),
        "wv": f32c(inputs["L0_attn_Wv"]), "wr": f32c(inputs["L0_attn_Wr"]),
        "wa": f32c(inputs["L0_attn_Wa"]), "wo": f32c(inputs["L0_attn_Wo"]),
        "a_log": f32c(inputs["L0_attn_A_log"]),
        "dt_b": f32c(inputs["L0_attn_dt_bias"]),
        "qnw": f32c(inputs["L0_attn_q_norm_w"]),
        "knw": f32c(inputs["L0_attn_k_norm_w"]),
        "gnw": f32c(inputs["L0_attn_g_norm_w"]),
        "gate_wt": f32c(np.transpose(inputs["L0_ffn_gate_w"])),
        "lb_b": f32c(inputs["L0_ffn_lb_bias"]),
        "egwt": f32c(np.transpose(inputs["L0_ffn_expert_gate_w"], (0, 2, 1))),
        "euwt": f32c(np.transpose(inputs["L0_ffn_expert_up_w"], (0, 2, 1))),
        "edwt": f32c(np.transpose(inputs["L0_ffn_expert_down_w"], (0, 2, 1))),
        "sgwt": f32c(np.transpose(inputs["L0_ffn_shared_gate_w"])),
        "suwt": f32c(np.transpose(inputs["L0_ffn_shared_up_w"])),
        "sdwt": f32c(np.transpose(inputs["L0_ffn_shared_down_w"])),
        "wmod": f32c(inputs["L0_wmod"]),
        "ident": ident, "tri": tri, "tri4": tri4, "colsel": colsel,
        "onesr": onesr, "negr": negr, "t32": t32, "iotac": iotac,
    }
    cfull = f32c(inputs["L0_c"])
    in_maps = []
    for core in range(NCORE):
        b = core // 4
        p = core % 4
        xs = np.ascontiguousarray(x[b, p * TPC:(p + 1) * TPC, :].T)
        pmask = np.zeros(NCORE, np.float32)
        for j in range(NCORE):
            if j // 4 == b and (j % 4) < p:
                pmask[j] = 1.0
        m = dict(common)
        m["xin"] = xs
        m["c_in"] = np.ascontiguousarray(cfull[b:b + 1, :])
        m["pmask"] = pmask
        in_maps.append(m)
    return in_maps, (B, T)


def kernel(**inputs):
    T = int(np.asarray(inputs["L0_attn_hidden_states"]).shape[1])
    TPC = 2 * T // NCORE
    nc = build_program(TPC)
    in_maps, (B, T) = _host_prep(inputs, TPC)
    res = run_bass_kernel_spmd(nc, in_maps, core_ids=list(range(NCORE)))
    out = np.zeros((B, T, D), np.float32)
    for core in range(NCORE):
        b = core // 4
        p = core % 4
        out[b, p * TPC:(p + 1) * TPC, :] = res.results[core]["yout"]
    return out
